# version 21
# baseline (speedup 1.0000x reference)
import contextlib
import math

import numpy as np
import ml_dtypes

import concourse.bass as bass
import concourse.mybir as mybir
from concourse.bass_utils import run_bass_kernel_spmd

BF = ml_dtypes.bfloat16
F32 = mybir.dt.float32
BF16 = mybir.dt.bfloat16
FP16 = mybir.dt.float16
U16 = mybir.dt.uint16
AF = mybir.ActivationFunctionType
ALU = mybir.AluOpType

NCORES = 8
BC = 2
L = 2048
D = 2048
DCH = 16
RQ = 1536
RQC = 12
RKV = 512
RKC = 4
NH = 16
HD = 128
I_FF = 5632
GP_N = 44
TW = 1024
HALF = 1024
SCALE = 1.0 / math.sqrt(HD)
BASE = 2.0 ** (-0.5)
SLOPES = [BASE ** (i + 1) for i in range(NH)]
QW = [128, 128, 256, 256] + [512] * 12
NCB = 19
EPS_LN = 1e-5
EPS_FF = 1e-6
S14 = 16384.0

ENGINES = ("tensor", "vector", "scalar", "gpsimd", "sync")

_CACHE = {}


class Ctx:
    def __init__(self):
        self.q = {e: [] for e in ENGINES}
        self.meta = {e: [] for e in ENGINES}
        self.cnt = {}
        self.pg = {i: ("dve", 0) for i in range(8)}

    def newsem(self, name):
        self.cnt[name] = 0

    def at(self, name):
        return self.cnt[name]

    def wait(self, eng, sname, val):
        if val is None or val <= 0:
            return
        if sname in ("gld", "sld", "sst", "ga", "gb", "sa", "sb", "so0", "so1"):
            val = self.cnt[sname]
        if val <= 0:
            return
        def f(e, S, sname=sname, val=val):
            e.wait_ge(S[sname], val)
        self.q[eng].append(f)
        self.meta[eng].append(("w", sname, val))

    def op(self, eng, fn, inc=None):
        tgt = None
        if inc is not None:
            self.cnt[inc] += 1
            tgt = self.cnt[inc]
        def f(e, S, fn=fn, inc=inc):
            ins = fn(e, S)
            if inc is not None:
                ins.then_inc(S[inc], 1)
        self.q[eng].append(f)
        self.meta[eng].append(("i", inc, 1))
        return tgt

    def dma(self, queue, out, in_, sname):
        self.cnt[sname] += 16
        tgt = self.cnt[sname]
        def f(e, S, out=out, in_=in_, sname=sname):
            e.dma_start(out=out, in_=in_).then_inc(S[sname], 16)
        self.q[queue].append(f)
        self.meta[queue].append(("i", sname, 16))
        return tgt

    def bank_wait(self, i, eng="tensor"):
        sem, cnt = self.pg[i]
        self.wait(eng, sem, cnt)

    def bank_free(self, i, sem, cnt):
        self.pg[i] = (sem, cnt)


def _split_rows(n_rows, max_rows=512):
    out = []
    r = 0
    while r < n_rows:
        c = min(max_rows, n_rows - r)
        out.append((r, c))
        r += c
    return out


def build_nc(dbg_x2=False):
    nc = bass.Bass("TRN2", target_bir_lowering=False)

    t = {}
    t["x"] = nc.dram_tensor("x", [BC, D, L], F32, kind="ExternalInput").ap()
    t["wqdT"] = nc.dram_tensor("wqdT", [D, RQ], F32, kind="ExternalInput").ap()
    t["wquT"] = nc.dram_tensor("wquT", [RQ, D], F32, kind="ExternalInput").ap()
    t["wkvT"] = nc.dram_tensor("wkvT", [D, RKV], F32, kind="ExternalInput").ap()
    t["kup"] = nc.dram_tensor("kup", [RKV, D], F32, kind="ExternalInput").ap()
    t["vup"] = nc.dram_tensor("vup", [RKV, D], F32, kind="ExternalInput").ap()
    t["woT"] = nc.dram_tensor("woT", [D, D], F32, kind="ExternalInput").ap()
    t["gu_pk"] = nc.dram_tensor("gu_pk", [D, 2 * I_FF], U16, kind="ExternalInput").ap()
    t["gu_sc"] = nc.dram_tensor("gu_sc", [D, 2 * I_FF], FP16, kind="ExternalInput").ap()
    t["dn_pk"] = nc.dram_tensor("dn_pk", [I_FF, D], U16, kind="ExternalInput").ap()
    t["dn_sc"] = nc.dram_tensor("dn_sc", [I_FF, D], FP16, kind="ExternalInput").ap()
    t["c_ones"] = nc.dram_tensor("c_ones", [128, 128], BF16, kind="ExternalInput").ap()
    t["c_shifts"] = nc.dram_tensor("c_shifts", [128, 512], U16, kind="ExternalInput").ap()
    t["c_bias"] = nc.dram_tensor("c_bias", [128, NH * NCB], F32, kind="ExternalInput").ap()
    t["out"] = nc.dram_tensor("out", [BC, D, L], F32, kind="ExternalOutput").ap()
    t["cache_gu"] = nc.dram_tensor("cache_gu", [D, 2 * I_FF], BF16,
                                   kind="ExternalOutput" if dbg_x2 else "Internal").ap()
    t["cache_dn"] = nc.dram_tensor("cache_dn", [I_FF, D], BF16,
                                   kind="ExternalOutput" if dbg_x2 else "Internal").ap()
    t["x2c"] = nc.dram_tensor("x2c", [BC, D, L], BF16,
                          kind="ExternalOutput" if dbg_x2 else "Internal").ap()
    if dbg_x2:
        t["dbg_o"] = nc.dram_tensor("dbg_o", [128, 16384], BF16, kind="ExternalOutput").ap()
        t["dbg_attn"] = nc.dram_tensor("dbg_attn", [128, 512], BF16, kind="ExternalOutput").ap()
        t["dbg_den"] = nc.dram_tensor("dbg_den", [128, 512], F32, kind="ExternalOutput").ap()
        t["dbg_ax"] = nc.dram_tensor("dbg_ax", [128, 16384], BF16, kind="ExternalOutput").ap()
        t["dbg_ao3"] = nc.dram_tensor("dbg_ao3", [128, 16384], BF16, kind="ExternalOutput").ap()
        t["dbg_h2"] = nc.dram_tensor("dbg_h2", [128, 16384], BF16, kind="ExternalOutput").ap()
        t["dbg_hint"] = nc.dram_tensor("dbg_hint", [128, 12288], BF16, kind="ExternalOutput").ap()
        t["dbg_y"] = nc.dram_tensor("dbg_y", [128, 16384], BF16, kind="ExternalOutput").ap()
        t["dbg_ao2"] = nc.dram_tensor("dbg_ao2", [128, 16384], BF16, kind="ExternalOutput").ap()
    t["_dbg"] = dbg_x2

    ctx = Ctx()
    sem_names = ["pe", "act", "dve", "gp", "gld", "sld", "sst",
                 "ga", "gb", "sa", "sb", "so0", "so1", "dqv"]
    for s in sem_names:
        ctx.newsem(s)

    es = contextlib.ExitStack()
    with es:
        sb = {}
        sb["ar_x"] = es.enter_context(nc.sbuf_tensor("ar_x", [128, 16384], BF16))
        sb["ar_ct"] = es.enter_context(nc.sbuf_tensor("ar_ct", [128, 8192], BF16))
        sb["ar_q"] = es.enter_context(nc.sbuf_tensor("ar_q", [128, 12288], BF16))
        sb["ar_o"] = es.enter_context(nc.sbuf_tensor("ar_o", [128, 16384], BF16))
        sb["ar_ao"] = es.enter_context(nc.sbuf_tensor("ar_ao", [128, 16384], BF16))
        sb["ar_h"] = es.enter_context(nc.sbuf_tensor("ar_h", [128, 12288], BF16))
        sb["ar_w"] = es.enter_context(nc.sbuf_tensor("ar_w", [128, 6144], BF16))
        sb["ar_dq"] = es.enter_context(nc.sbuf_tensor("ar_dq", [128, 4096], BF16))
        sb["ar_s"] = es.enter_context(nc.sbuf_tensor("ar_s", [128, 4096], F32))
        sb["ar_st"] = es.enter_context(nc.sbuf_tensor("ar_st", [128, 1024], F32))
        sb["s_wsum"] = es.enter_context(nc.sbuf_tensor("s_wsum", [128, 16], F32))
        sb["s_ones"] = es.enter_context(nc.sbuf_tensor("s_ones", [128, 128], BF16))
        sb["s_shift"] = es.enter_context(nc.sbuf_tensor("s_shift", [128, 512], U16))
        sb["s_bias"] = es.enter_context(nc.sbuf_tensor("s_bias", [128, NH * NCB], F32))
        pb = [es.enter_context(nc.psum_tensor(f"pb{i}", [128, 512], F32))
              for i in range(8)]
        S = {s: es.enter_context(nc.semaphore(s)) for s in sem_names}

        emit_program(ctx, t, sb, pb)
        _CACHE["ctx"] = ctx

        with nc.Block() as block:
            @block.tensor
            def _(e):
                for f in ctx.q["tensor"]:
                    f(e, S)

            @block.vector
            def _(e):
                for f in ctx.q["vector"]:
                    f(e, S)

            @block.scalar
            def _(e):
                for f in ctx.q["scalar"]:
                    f(e, S)

            @block.gpsimd
            def _(e):
                for f in ctx.q["gpsimd"]:
                    f(e, S)

            @block.sync
            def _(e):
                for f in ctx.q["sync"]:
                    f(e, S)

    return nc


def emit_program(ctx, t, sb, pb):
    ar_x, ar_ct, ar_q, ar_o, ar_ao, ar_h, ar_w = (
        sb["ar_x"], sb["ar_ct"], sb["ar_q"], sb["ar_o"], sb["ar_ao"],
        sb["ar_h"], sb["ar_w"])
    ar_s, ar_st, ar_dq = sb["ar_s"], sb["ar_st"], sb["ar_dq"]
    s_wsum, s_ones, s_shift, s_bias = (
        sb["s_wsum"], sb["s_ones"], sb["s_shift"], sb["s_bias"])

    m_b = ar_s[:, 0:1024]
    r_b = ar_s[:, 1024:2048]
    m_bf = sb["ar_h"][:, 5632:6656]
    r_bf = sb["ar_h"][:, 6656:7680]
    m_bf_f = sb["ar_ct"][:, 3072:4096]
    r_bf_f = sb["ar_ct"][:, 4096:5120]
    tmp_b = ar_s[:, 2048:3072]
    aux_b = ar_s[:, 3072:4096]
    aux_bf = ar_s[:, 2048:4096].bitcast(BF16)

    dq_pk = [ar_dq[:, 0:512].bitcast(U16), ar_dq[:, 512:1024].bitcast(U16)]
    dq_sc = [ar_dq[:, 1024:1536].bitcast(FP16), ar_dq[:, 1536:2048].bitcast(FP16)]
    dq_t1 = ar_dq[:, 2048:2560].bitcast(U16)
    dq_lo = ar_dq[:, 2560:3072].bitcast(U16)
    dq_w = [ar_dq[:, 3072:3584], ar_dq[:, 3584:4096]]

    def mm(out_ap, lhsT, rhs, start, stop, inc=None):
        return ctx.op(
            "tensor",
            lambda e, S, o=out_ap, a=lhsT, b=rhs, st=start, sp=stop:
                e.matmul(o, a, b, start=st, stop=sp),
            inc=inc)

    def vop(fn, inc=None):
        return ctx.op("vector", fn, inc=inc)

    def acopy(out_ap, in_ap):
        return ctx.op("scalar",
                      lambda e, S, o=out_ap, a=in_ap:
                          e.activation(out=o, in_=a, func=AF.Copy),
                      inc="act")

    def aop(fn, inc=None):
        return ctx.op("scalar", fn, inc=inc)

    ctx.dma("sync", s_ones[:], t["c_ones"], "sa")
    ctx.dma("sync", s_shift[:], t["c_shifts"], "sa")
    ctx.dma("sync", s_bias[:], t["c_bias"], "sa")
    ctx.wait("vector", "sa", 1)
    ctx.wait("scalar", "sa", 1)
    ctx.wait("tensor", "sa", 1)
    dq_range = {"vector": None, "sync": None}
    dq_s = {qn: len(ctx.q[qn]) for qn in ("vector", "sync")}

    dq_specs = []
    for (pk_name, sc_name, c_name, n_dc, width) in [
        ("gu_pk", "gu_sc", "cache_gu", DCH, 2 * I_FF),
        ("dn_pk", "dn_sc", "cache_dn", GP_N, D),
    ]:
        for dc in range(n_dc):
            for ob in range(width // 512):
                dq_specs.append((pk_name, sc_name, c_name, dc, ob))
    n_dq = len(dq_specs)
    dq_ld = [0] * n_dq
    dq_cons = [0] * n_dq
    dq_st = [0] * n_dq

    def dq_load(i):
        pk_name, sc_name, c_name, dc, ob = dq_specs[i]
        bi = i % 2
        sem = ("sa", "sb")[bi]
        if i >= 2:
            ctx.wait("sync", "dqv", dq_cons[i - 2])
        ctx.dma("sync", dq_pk[bi][:],
                t[pk_name][dc * 128:(dc + 1) * 128, ob * 512:(ob + 1) * 512], sem)
        ctx.dma("sync", dq_sc[bi][:],
                t[sc_name][dc * 128:(dc + 1) * 128, ob * 512:(ob + 1) * 512], sem)
        dq_ld[i] = sem

    dq_load(0)
    dq_load(1)
    for i in range(n_dq):
        pk_name, sc_name, c_name, dc, ob = dq_specs[i]
        bi = i % 2
        ctx.wait("vector", dq_ld[i], 1)
        if i >= 2:
            ctx.wait("vector", dq_st[i - 2], 1)
        vop(lambda e, S, a=dq_pk[bi]:
            e.tensor_tensor(dq_t1[:], a[:], s_shift[:], op=ALU.logical_shift_right))
        vop(lambda e, S: e.tensor_scalar(dq_lo[:], dq_t1[:], 7, 9,
                                         op0=ALU.bitwise_and, op1=ALU.logical_shift_left))
        vop(lambda e, S, o=dq_pk[bi]: e.tensor_scalar(o[:], dq_t1[:], 8, 12,
                                                      op0=ALU.bitwise_and, op1=ALU.logical_shift_left))
        vop(lambda e, S, b2=dq_pk[bi]:
            e.tensor_tensor(dq_t1[:], dq_lo[:], b2[:], op=ALU.bitwise_or))
        dq_cons[i] = vop(
            lambda e, S, o=dq_w[bi], b2=dq_sc[bi]:
                e.tensor_tensor(o[:], dq_t1[:].bitcast(FP16), b2[:], op=ALU.mult),
            inc="dqv")
        ctx.wait("sync", "dqv", dq_cons[i])
        ctx.dma(
            "sync", t[c_name][dc * 128:(dc + 1) * 128, ob * 512:(ob + 1) * 512],
            dq_w[bi][:], ("so0", "so1")[bi])
        dq_st[i] = ("so0", "so1")[bi]
        if i + 2 < n_dq:
            dq_load(i + 2)
    dq_range["vector"] = (dq_s["vector"], len(ctx.q["vector"]))
    dq_range["sync"] = (dq_s["sync"], len(ctx.q["sync"]))
    b0_mark = {}
    ffn0_mark = {}

    class Rot:
        def __init__(self, banks):
            self.banks = banks
            self.i = 0

        def acquire(self):
            j = self.banks[self.i % len(self.banks)]
            self.i += 1
            ctx.bank_wait(j)
            return j

    rot = Rot([4, 5, 6, 7])

    def ln_stats(src_fn, n_ch, width, s1b, s2b, eps_const, extra_coef, negate,
                 ready_fn=None, load_fn=None):
        nts = width // 512
        for bk in s1b + s2b:
            ctx.bank_wait(bk)
        sq = [aux_bf[:, 0:width], aux_bf[:, width:2 * width]]
        c_sq = [0] * n_ch
        c_done = [0] * n_ch
        for c in range(n_ch):
            if ready_fn is not None:
                sem, cnt = ready_fn(c)
                ctx.wait("scalar", sem, cnt)
                ctx.wait("tensor", sem, cnt)
            if c >= 2:
                ctx.wait("scalar", "pe", c_done[c - 2])
            c_sq[c] = aop(lambda e, S, o=sq[c % 2], a=src_fn(c):
                          e.activation(out=o, in_=a, func=AF.Square), inc="act")
            for ts_ in range(nts):
                mm(pb[s1b[ts_]][:, 0:512], s_ones[:],
                   src_fn(c)[:, ts_ * 512:(ts_ + 1) * 512],
                   start=(c == 0), stop=(c == n_ch - 1))
            ctx.wait("tensor", "act", c_sq[c])
            for ts_ in range(nts):
                mm(pb[s2b[ts_]][:, 0:512], s_ones[:],
                   sq[c % 2][:, ts_ * 512:(ts_ + 1) * 512],
                   start=(c == 0), stop=(c == n_ch - 1),
                   inc="pe" if ts_ == nts - 1 else None)
            c_done[c] = ctx.at("pe")
            if load_fn is not None and c + 2 < n_ch:
                load_fn(c + 2, c_done[c], c_sq[c])
        done = ctx.at("pe")
        ctx.wait("vector", "pe", done)
        inv = 1.0 / D
        for ts_ in range(nts):
            sl = slice(ts_ * 512, (ts_ + 1) * 512)
            vop(lambda e, S, b2=pb[s1b[ts_]], sl=sl:
                e.tensor_scalar(m_b[:, sl], b2[:, 0:512], inv, None, op0=ALU.mult))
            vop(lambda e, S, b2=pb[s2b[ts_]], sl=sl:
                e.tensor_scalar(aux_b[:, sl], b2[:, 0:512], inv, None, op0=ALU.mult))
        w = width
        vop(lambda e, S: e.tensor_tensor(tmp_b[:, 0:w], m_b[:, 0:w], m_b[:, 0:w], op=ALU.mult))
        vop(lambda e, S: e.tensor_tensor(tmp_b[:, 0:w], aux_b[:, 0:w], tmp_b[:, 0:w], op=ALU.subtract))
        if extra_coef:
            vop(lambda e, S, k=extra_coef:
                e.scalar_tensor_tensor(tmp_b[:, 0:w], aux_b[:, 0:w], k, tmp_b[:, 0:w],
                                       op0=ALU.mult, op1=ALU.add))
        vop(lambda e, S, ec=eps_const:
            e.tensor_scalar(tmp_b[:, 0:w], tmp_b[:, 0:w], ec, None, op0=ALU.add))
        rc = vop(lambda e, S: e.reciprocal(tmp_b[:, 0:w], tmp_b[:, 0:w]), inc="dve")
        for bk in s1b + s2b:
            ctx.bank_free(bk, "dve", rc)
        ctx.wait("scalar", "dve", rc)
        sd = aop(lambda e, S: e.activation(out=r_b[:, 0:w], in_=tmp_b[:, 0:w], func=AF.Sqrt),
                 inc="act")
        ctx.wait("vector", "act", sd)
        if negate:
            vop(lambda e, S: e.tensor_scalar(r_b[:, 0:w], r_b[:, 0:w], -1.0, None, op0=ALU.mult))
        return rc

    def cast_cols(src2d, nrows, col0, ncols, dst, slot=0, split=512):
        sem = ("ga", "gb")[slot % 2]
        for (r, c) in _split_rows(nrows, split):
            src = src2d[r:r + c, col0:col0 + ncols]
            src = src.rearrange("(c p) j -> p c j", p=128)
            d0 = (r // 128) * ncols
            ctx.dma("gpsimd", dst[:, d0:d0 + (c // 128) * ncols], src, sem)
        return sem

    x2_wr_last = 0
    ffn_dve_last = 0

    prev_xst = [0] * DCH
    for b in range(BC):
        if b == 0:
            b0_mark.update({qn: len(ctx.q[qn]) for qn in ("vector", "sync")})
        for Hf in range(2):
            q0 = Hf * HALF
            KR = (Hf + 1) * HALF
            KT = KR // 128

            ctx.wait("gpsimd", "dve", ffn_dve_last)
            x_tgt = []
            for dc in range(DCH):
                if Hf == 1:
                    ctx.wait("gpsimd", "dve", prev_xst[dc])
                else:
                    ctx.wait("gpsimd", "sst", 1)
                src = t["x"][b, dc * 128:(dc + 1) * 128, q0:q0 + HALF]
                ctx.dma("gpsimd", ar_x[:, dc * 1024:(dc + 1) * 1024], src,
                        ("ga", "gb")[dc % 2])
                x_tgt.append(("ga", "gb")[dc % 2])

            ln_stats(lambda c: ar_x[:, c * 1024:(c + 1) * 1024], DCH, HALF,
                     [4, 5], [6, 7], EPS_LN, 0.0, True,
                     ready_fn=lambda c: (x_tgt[c], 1))

            wqd_buf = [ar_w[:, 0:2048], ar_w[:, 2048:4096]]
            wqd_free = [0, 0]

            def proj(n_out_ch, wsrc, wsum_off, out_fn):
                wtgt = [0] * n_out_ch
                ctx.wait("gpsimd", "dve", wqd_free[0])
                ctx.wait("gpsimd", "sst", 1)
                wtgt[0] = cast_cols(t[wsrc], D, 0, 128, wqd_buf[0], slot=0)
                last_ev = 0
                for oc in range(n_out_ch):
                    if oc + 1 < n_out_ch:
                        ctx.wait("gpsimd", "dve", wqd_free[(oc + 1) % 2])
                        wtgt[oc + 1] = cast_cols(t[wsrc], D, (oc + 1) * 128, 128,
                                                 wqd_buf[(oc + 1) % 2], slot=oc + 1)
                    wb = wqd_buf[oc % 2]
                    ctx.wait("tensor", wtgt[oc], 1)
                    bk = rot.acquire()
                    for dc in range(DCH):
                        if oc == 0:
                            ctx.wait("tensor", x_tgt[dc], 1)
                        mm(pb[bk][:, 0:1], wb[:, dc * 128:(dc + 1) * 128], s_ones[:, 0:1],
                           start=(dc == 0), stop=(dc == DCH - 1),
                           inc="pe" if dc == DCH - 1 else None)
                    ctx.wait("vector", "pe", ctx.at("pe"))
                    wev = vop(lambda e, S, b2=pb[bk], oc=oc, off=wsum_off:
                              e.tensor_copy(s_wsum[:, off + oc:off + oc + 1], b2[:, 0:1]),
                              inc="dve")
                    ctx.bank_free(bk, "dve", wev)
                    for tsub in range(2):
                        bk = rot.acquire()
                        for dc in range(DCH):
                            mm(pb[bk][:, 0:512], wb[:, dc * 128:(dc + 1) * 128],
                               ar_x[:, dc * 1024 + tsub * 512:dc * 1024 + tsub * 512 + 512],
                               start=(dc == 0), stop=(dc == DCH - 1),
                               inc="pe" if dc == DCH - 1 else None)
                        ctx.wait("vector", "pe", ctx.at("pe"))
                        msl = slice(tsub * 512, tsub * 512 + 512)
                        vop(lambda e, S, b2=pb[bk], msl=msl, oc=oc, off=wsum_off:
                            e.scalar_tensor_tensor(tmp_b[:, 0:512], m_b[:, msl],
                                                   s_wsum[:, off + oc:off + oc + 1],
                                                   b2[:, 0:512],
                                                   op0=ALU.mult, op1=ALU.subtract))
                        ev = vop(lambda e, S, o=out_fn(oc, tsub), msl=msl:
                                 e.tensor_tensor(o, tmp_b[:, 0:512], r_b[:, msl],
                                                 op=ALU.mult),
                                 inc="dve")
                        ctx.bank_free(bk, "dve", ev)
                        last_ev = ev
                    wqd_free[oc % 2] = last_ev
                return last_ev

            qlow = ar_q
            proj(RQC, "wqdT", 0,
                 lambda oc, tsub: qlow[:, oc * 1024 + tsub * 512:oc * 1024 + tsub * 512 + 512])
            lastc = proj(RKC, "wkvT", 12,
                         lambda oc, tsub: ar_ct[:, oc * 2048 + q0 + tsub * 512:
                                                oc * 2048 + q0 + tsub * 512 + 512])

            vtok = ar_h[:, 0:4096]
            kT = ar_h[:, 4096:6144]
            qT = ar_h[:, 6144:7168]
            attn = [ar_h[:, 7168:7680], ar_h[:, 7680:8192]]
            wqu_buf = [ar_h[:, 8192:9728], ar_h[:, 9728:11264]]
            kup_buf = [ar_w[:, 4096:4608], ar_w[:, 4608:5120]]
            vup_buf = [ar_w[:, 0:1024], ar_w[:, 1024:2048]]
            kup_tgt = [0] * NH
            wqu_tgt = [0] * NH
            vup_tgt = [0] * (NH // 2)
            kup_free = [0, 0]
            wqu_free = [0, 0]
            vup_free = [0, 0]

            ctx.wait("gpsimd", "dve", lastc)
            ctx.wait("gpsimd", "pe", ctx.at("pe"))
            ctx.wait("gpsimd", "sst", 1)
            ctx.wait("tensor", "dve", lastc)
            kup_tgt[0] = cast_cols(t["kup"], RKV, 0, 128, kup_buf[0], slot=0)
            vup_tgt[0] = cast_cols(t["vup"], RKV, 0, 256, vup_buf[0], slot=0)
            wqu_tgt[0] = cast_cols(t["wquT"], RQ, 0, 128, wqu_buf[0], slot=0)

            for h in range(NH):
                pr = h // 2
                if h + 1 < NH:
                    ctx.wait("gpsimd", *kup_free[(h + 1) % 2]) if isinstance(kup_free[(h + 1) % 2], tuple) else ctx.wait("gpsimd", "dve", kup_free[(h + 1) % 2])
                    kup_tgt[h + 1] = cast_cols(t["kup"], RKV, (h + 1) * 128, 128,
                                               kup_buf[(h + 1) % 2], slot=h + 1)
                    ctx.wait("gpsimd", *wqu_free[(h + 1) % 2]) if isinstance(wqu_free[(h + 1) % 2], tuple) else ctx.wait("gpsimd", "dve", wqu_free[(h + 1) % 2])
                    wqu_tgt[h + 1] = cast_cols(t["wquT"], RQ, (h + 1) * 128, 128,
                                               wqu_buf[(h + 1) % 2], slot=h + 1)
                if h % 2 == 0 and pr + 1 < NH // 2:
                    ctx.wait("gpsimd", *vup_free[(pr + 1) % 2]) if isinstance(vup_free[(pr + 1) % 2], tuple) else ctx.wait("gpsimd", "dve", vup_free[(pr + 1) % 2])
                    vup_tgt[pr + 1] = cast_cols(t["vup"], RKV, (pr + 1) * 256, 256,
                                                vup_buf[(pr + 1) % 2], slot=pr + 1)

                if h % 2 == 0:
                    ctx.wait("tensor", vup_tgt[pr], 1)
                    for tt in range(KT):
                        bk = rot.acquire()
                        for rc in range(RKC):
                            mm(pb[bk][:, 0:256],
                               ar_ct[:, rc * 2048 + tt * 128:rc * 2048 + tt * 128 + 128],
                               vup_buf[pr % 2][:, rc * 256:(rc + 1) * 256],
                               start=(rc == 0), stop=(rc == RKC - 1),
                               inc="pe" if rc == RKC - 1 else None)
                        ctx.wait("scalar", "pe", ctx.at("pe"))
                        vev = acopy(vtok[:, tt * 256:(tt + 1) * 256], pb[bk][:, 0:256])
                        ctx.bank_free(bk, "act", vev)
                    vup_free[pr % 2] = ("act", vev)

                ctx.wait("tensor", kup_tgt[h], 1)
                for ks in range(KR // 512):
                    bk = rot.acquire()
                    for rc in range(RKC):
                        mm(pb[bk][:, 0:512], kup_buf[h % 2][:, rc * 128:(rc + 1) * 128],
                           ar_ct[:, rc * 2048 + ks * 512:rc * 2048 + ks * 512 + 512],
                           start=(rc == 0), stop=(rc == RKC - 1),
                           inc="pe" if rc == RKC - 1 else None)
                    ctx.wait("scalar", "pe", ctx.at("pe"))
                    kev = acopy(kT[:, ks * 512:(ks + 1) * 512], pb[bk][:, 0:512])
                    ctx.bank_free(bk, "act", kev)
                kup_free[h % 2] = ("act", kev)

                ctx.wait("tensor", wqu_tgt[h], 1)
                for tsub in range(2):
                    bk = rot.acquire()
                    for r1c in range(RQC):
                        mm(pb[bk][:, 0:512], wqu_buf[h % 2][:, r1c * 128:(r1c + 1) * 128],
                           qlow[:, r1c * 1024 + tsub * 512:r1c * 1024 + tsub * 512 + 512],
                           start=(r1c == 0), stop=(r1c == RQC - 1),
                           inc="pe" if r1c == RQC - 1 else None)
                    ctx.wait("scalar", "pe", ctx.at("pe"))
                    qev = acopy(qT[:, tsub * 512:(tsub + 1) * 512], pb[bk][:, 0:512])
                    ctx.bank_free(bk, "act", qev)
                wqu_free[h % 2] = ("act", qev)
                ctx.wait("tensor", "act", qev)

                qw = QW[h]
                nsub = 512 // qw
                for qc in range(2):
                    q_off = q0 + qc * 512
                    kt_hi = q_off // 128 + 4
                    c_sc = [0] * kt_hi
                    c_ex = [0] * kt_hi
                    c_dn = [0] * kt_hi
                    c_mk = [0] * kt_hi
                    ctx.bank_wait(2)
                    ctx.bank_wait(3)
                    for kt in range(kt_hi + 1):
                        if kt < kt_hi:
                            ctx.bank_wait(kt % 2)
                            c_sc[kt] = mm(pb[kt % 2][:, 0:512], kT[:, kt * 128:(kt + 1) * 128],
                                          qT[:, qc * 512:qc * 512 + 512], True, True, inc="pe")
                            ctx.wait("scalar", "pe", c_sc[kt])
                            for sub in range(nsub):
                                qso = q_off + sub * qw
                                cidx = (kt * 128 - qso + 2048 - qw) // 128
                                bias_ap = s_bias[:, h * NCB + cidx:h * NCB + cidx + 1]
                                c_ex[kt] = aop(
                                    lambda e, S, kt=kt, sub=sub, qw=qw, ba=bias_ap:
                                        e.activation(out=attn[kt % 2][:, sub * qw:(sub + 1) * qw],
                                                     in_=pb[kt % 2][:, sub * qw:(sub + 1) * qw],
                                                     func=AF.Exp, bias=ba, scale=SCALE),
                                    inc="act")
                            ctx.bank_free(kt % 2, "act", c_ex[kt])
                            if kt * 128 + 128 > q_off:
                                ctx.wait("gpsimd", "act", c_ex[kt])
                                c_mk[kt] = ctx.op(
                                    "gpsimd",
                                    lambda e, S, kt=kt, q_off=q_off:
                                        e.affine_select(out=attn[kt % 2][:],
                                                        in_=attn[kt % 2][:],
                                                        pattern=[[1, 512]],
                                                        compare_op=ALU.is_ge, fill=0.0,
                                                        base=q_off - kt * 128,
                                                        channel_multiplier=-1),
                                    inc="gp")
                        if kt >= 1:
                            j = kt - 1
                            if c_mk[j]:
                                ctx.wait("tensor", "gp", c_mk[j])
                            else:
                                ctx.wait("tensor", "act", c_ex[j])
                            mm(pb[2][:, 0:512],
                               vtok[:, j * 256 + (h % 2) * 128:j * 256 + (h % 2) * 128 + 128],
                               attn[j % 2][:], start=(j == 0), stop=(j == kt_hi - 1))
                            c_dn[j] = mm(pb[3][:, 0:512], s_ones[:], attn[j % 2][:],
                                         start=(j == 0), stop=(j == kt_hi - 1), inc="pe")
                            if t.get("_dbg") and b == 0 and Hf == 0 and h == 0 and qc == 0 and j == 0:
                                ctx.wait("sync", "pe", c_dn[j])
                                dmp = ctx.dma("sync", t["dbg_attn"], attn[0][:], "sst")
                                ctx.wait("tensor", "sst", dmp)
                    ctx.wait("vector", "pe", c_dn[kt_hi - 1])
                    if t.get("_dbg") and b == 0 and Hf == 0 and h == 0 and qc == 0:
                        dv = vop(lambda e, S: e.tensor_copy(ar_st[:, 0:512], pb[3][:, 0:512]), inc="dve")
                        ctx.wait("sync", "dve", dv)
                        dmp2 = ctx.dma("sync", t["dbg_den"], ar_st[:, 0:512], "sst")
                        ctx.wait("vector", "sst", dmp2)
                    vop(lambda e, S: e.reciprocal(tmp_b[:, 0:512], pb[3][:, 0:512]))
                    ocol = h * 1024 + qc * 512
                    oev = vop(lambda e, S, ocol=ocol:
                              e.tensor_tensor(ar_o[:, ocol:ocol + 512], pb[2][:, 0:512],
                                              tmp_b[:, 0:512], op=ALU.mult),
                              inc="dve")
                    ctx.bank_free(2, "dve", oev)
                    ctx.bank_free(3, "dve", oev)
            all_o = oev
            if t.get("_dbg") and b == 0 and Hf == 0:
                ctx.wait("sync", "dve", all_o)
                dmp3 = ctx.dma("sync", t["dbg_o"], ar_o[:], "sst")
                dmp3 = ctx.dma("sync", t["dbg_ax"], ar_x[:], "sst")
                ctx.wait("tensor", "sst", dmp3)

            wo_buf = [ar_w[:, 2048:4096], ar_w[:, 4096:6144]]
            wo_tgt = [0] * DCH
            kf = [x[1] if isinstance(x, tuple) else x for x in kup_free]
            wo_free = [0, ("act", max(kf))]
            ao_ev = [0] * DCH
            ctx.wait("gpsimd", "act", max(kf))
            wo_tgt[0] = cast_cols(t["woT"], D, 0, 128, wo_buf[0], slot=0)
            for fc in range(DCH):
                if fc + 1 < DCH:
                    ctx.wait("gpsimd", *wo_free[(fc + 1) % 2]) if isinstance(wo_free[(fc + 1) % 2], tuple) else ctx.wait("gpsimd", "dve", wo_free[(fc + 1) % 2])
                    wo_tgt[fc + 1] = cast_cols(t["woT"], D, (fc + 1) * 128, 128,
                                               wo_buf[(fc + 1) % 2], slot=fc + 1)
                ctx.wait("tensor", wo_tgt[fc], 1)
                if fc == 0:
                    ctx.wait("tensor", "dve", all_o)
                for tsub in range(2):
                    bk = rot.acquire()
                    for ec in range(DCH):
                        mm(pb[bk][:, 0:512], wo_buf[fc % 2][:, ec * 128:(ec + 1) * 128],
                           ar_o[:, ec * 1024 + tsub * 512:ec * 1024 + tsub * 512 + 512],
                           start=(ec == 0), stop=(ec == DCH - 1),
                           inc="pe" if ec == DCH - 1 else None)
                    ctx.wait("scalar", "pe", ctx.at("pe"))
                    aev = acopy(ar_ao[:, fc * 1024 + tsub * 512:fc * 1024 + tsub * 512 + 512],
                                pb[bk][:, 0:512])
                    ctx.bank_free(bk, "act", aev)
                ao_ev[fc] = aev
                wo_free[fc % 2] = ("act", aev)

            ln_stats(lambda c: ar_ao[:, c * 1024:(c + 1) * 1024], DCH, HALF,
                     [6, 7], [2, 3], EPS_LN, 0.0, False,
                     ready_fn=lambda c: ("act", ao_ev[c]))
            if t.get("_dbg") and b == 0 and Hf == 0:
                ctx.wait("sync", "dve", ao_ev[DCH - 1])
                dmp4 = ctx.dma("sync", t["dbg_ao2"], ar_ao[:], "sst")
                dmp4 = ctx.dma("sync", t["dbg_ao3"], ar_ao[:], "sst")
                ctx.wait("vector", "sst", dmp4)
            for dc in range(DCH):
                vop(lambda e, S, dc=dc:
                    e.tensor_tensor(tmp_b[:, 0:1024], ar_ao[:, dc * 1024:(dc + 1) * 1024],
                                    m_b[:, 0:1024], op=ALU.subtract))
                vop(lambda e, S:
                    e.tensor_tensor(tmp_b[:, 0:1024], tmp_b[:, 0:1024], r_b[:, 0:1024],
                                    op=ALU.mult))
                xu = vop(lambda e, S, dc=dc:
                         e.tensor_tensor(ar_x[:, dc * 1024:(dc + 1) * 1024],
                                         ar_x[:, dc * 1024:(dc + 1) * 1024],
                                         tmp_b[:, 0:1024], op=ALU.add),
                         inc="dve")
                ctx.wait("gpsimd", "dve", xu)
                x2_wr_last = ctx.dma(
                    "gpsimd", t["x2c"][b, dc * 128:(dc + 1) * 128, q0:q0 + HALF],
                    ar_x[:, dc * 1024:(dc + 1) * 1024], "sst")

        if b == 0:
            ffn0_mark.update({qn: len(ctx.q[qn]) for qn in ("vector", "sync")})
        for tc in (1, 0):
            tr0 = tc * TW
            x2r = [ar_ct[:, 0:1024], ar_ct[:, 1024:2048]]
            silu_sb = [ar_ct[:, 2048:2560], ar_ct[:, 2560:3072]]

            pe_epoch = ctx.at("pe")
            ctx.wait("sync", "pe", pe_epoch)
            ctx.wait("sync", "sst", 1)

            x2_tgt = [0] * DCH

            def x2_load(dc):
                ctx.dma(
                    "sync", x2r[dc % 2][:],
                    t["x2c"][b, dc * 128:(dc + 1) * 128, tr0:tr0 + TW],
                    ("sa", "sb")[dc % 2])
                x2_tgt[dc] = ("sa", "sb")[dc % 2]

            if tc == 1:
                dve_mark = ctx.at("dve")
                ln_stats(lambda c: ar_x[:, c * 1024:(c + 1) * 1024], DCH, TW,
                         [4, 5], [6, 7], EPS_LN, 0.0, False,
                         ready_fn=lambda c: ("dve", dve_mark))
            else:
                x2_load(0)
                x2_load(1)

                def ln2_loadfn(c, gate_pe, gate_sq):
                    ctx.wait("sync", "pe", gate_pe)
                    ctx.wait("sync", "act", gate_sq)
                    x2_load(c)

                ln_stats(lambda c: x2r[c % 2], DCH, TW, [4, 5], [6, 7],
                         EPS_LN, 0.0, False,
                         ready_fn=lambda c: (x2_tgt[c], 1),
                         load_fn=ln2_loadfn)

            h2 = ar_x
            h2_ev = [0] * DCH
            tmp_bf = tmp_b.bitcast(BF16)[:, 0:1024]
            if tc == 1:
                vop(lambda e, S: e.tensor_copy(m_bf_f[:], m_b[:, 0:1024]))
                vop(lambda e, S: e.tensor_copy(r_bf_f[:], r_b[:, 0:1024]))
                ctx.wait("vector", "sst", 1)
                for dc in range(DCH):
                    vop(lambda e, S, dc=dc:
                        e.tensor_tensor(tmp_bf,
                                        ar_x[:, dc * 1024:(dc + 1) * 1024],
                                        m_bf_f[:], op=ALU.subtract))
                    h2_ev[dc] = vop(lambda e, S, dc=dc:
                                    e.tensor_tensor(h2[:, dc * 1024:(dc + 1) * 1024],
                                                    tmp_bf, r_bf_f[:],
                                                    op=ALU.mult),
                                    inc="dve")
            else:
                ctx.wait("sync", "pe", ctx.at("pe"))
                ctx.wait("sync", "act", ctx.at("act"))
                x2_load(0)
                x2_load(1)
                vop(lambda e, S: e.tensor_copy(m_bf_f[:], m_b[:, 0:1024]))
                vop(lambda e, S: e.tensor_copy(r_bf_f[:], r_b[:, 0:1024]))
                for dc in range(DCH):
                    ctx.wait("vector", x2_tgt[dc], 1)
                    vop(lambda e, S, dc=dc:
                        e.tensor_tensor(tmp_bf, x2r[dc % 2][:], m_bf_f[:],
                                        op=ALU.subtract))
                    h2_ev[dc] = vop(lambda e, S, dc=dc:
                                    e.tensor_tensor(h2[:, dc * 1024:(dc + 1) * 1024],
                                                    tmp_bf, r_bf_f[:],
                                                    op=ALU.mult),
                                    inc="dve")
                    if dc + 2 < DCH:
                        ctx.wait("sync", "dve", h2_ev[dc])
                        x2_load(dc + 2)

            if t.get("_dbg") and b == 0 and tc == 0:
                ctx.wait("sync", "dve", h2_ev[DCH - 1])
                dmph = ctx.dma("sync", t["dbg_h2"], h2[:], "sst")
                ctx.wait("tensor", "sst", dmph)
            wgu_buf = [ar_h[:, 0:4096], ar_h[:, 4096:8192]]
            wgu_tgt = [0] * GP_N
            wgu_free = [0, 0]

            def wgu_load(gp, bi):
                tgt = 0
                for j in range(4):
                    src = t["cache_gu"][j * 512:(j + 1) * 512, gp * 256:(gp + 1) * 256]
                    src = src.rearrange("(c p) o -> p c o", p=128)
                    ctx.dma("sync", wgu_buf[bi][:, j * 1024:(j + 1) * 1024], src,
                            ("sa", "sb")[bi])
                wgu_tgt[gp] = ("sa", "sb")[bi]

            if b == 0:
                ctx.wait("sync", "so0", 1)
                ctx.wait("sync", "so1", 1)
            wgu_load(0, 0)
            wgu_load(1, 1)
            hint_ev = [0] * GP_N
            ctx.wait("tensor", "dve", h2_ev[DCH - 1])
            for gp in range(GP_N):
                gbank = [0 + 4 * (gp % 2), 1 + 4 * (gp % 2)]
                ubank = [2 + 4 * (gp % 2), 3 + 4 * (gp % 2)]
                wb = wgu_buf[gp % 2]
                ctx.wait("tensor", wgu_tgt[gp], 1)
                c_g = [0, 0]
                for tsub in range(2):
                    ctx.bank_wait(gbank[tsub])
                    for dc in range(DCH):
                        mm(pb[gbank[tsub]][:, 0:512], wb[:, dc * 256:dc * 256 + 128],
                           h2[:, dc * 1024 + tsub * 512:dc * 1024 + tsub * 512 + 512],
                           start=(dc == 0), stop=(dc == DCH - 1),
                           inc="pe" if dc == DCH - 1 else None)
                    c_g[tsub] = ctx.at("pe")
                    ctx.bank_wait(ubank[tsub])
                    for dc in range(DCH):
                        mm(pb[ubank[tsub]][:, 0:512], wb[:, dc * 256 + 128:dc * 256 + 256],
                           h2[:, dc * 1024 + tsub * 512:dc * 1024 + tsub * 512 + 512],
                           start=(dc == 0), stop=(dc == DCH - 1),
                           inc="pe" if dc == DCH - 1 else None)
                cu = ctx.at("pe")
                wgu_free[gp % 2] = cu
                if gp + 2 < GP_N:
                    ctx.wait("sync", "pe", cu)
                    wgu_load(gp + 2, gp % 2)
                if gp >= 1:
                    ctx.wait("scalar", "dve", hint_ev[gp - 1])
                for tsub in range(2):
                    ctx.wait("scalar", "pe", c_g[tsub])
                    sev = aop(lambda e, S, gb=pb[gbank[tsub]], o=silu_sb[tsub]:
                              e.activation(out=o[:], in_=gb[:, 0:512], func=AF.Silu,
                                           scale=S14),
                              inc="act")
                    ctx.bank_free(gbank[tsub], "act", sev)
                    ctx.wait("vector", "act", sev)
                    ctx.wait("vector", "pe", cu)
                    col = gp * 1024 + tsub * 512
                    if col < 12288:
                        hdst = ar_q[:, col:col + 512]
                    elif col < 12288 + 16384:
                        hdst = ar_o[:, col - 12288:col - 12288 + 512]
                    else:
                        hdst = ar_ao[:, col - 28672:col - 28672 + 512]
                    hint_ev[gp] = vop(
                        lambda e, S, ub=pb[ubank[tsub]], o=hdst, s2=silu_sb[tsub]:
                            e.scalar_tensor_tensor(o, ub[:, 0:512], S14, s2[:],
                                                   op0=ALU.mult, op1=ALU.mult),
                        inc="dve")
                    ctx.bank_free(ubank[tsub], "dve", hint_ev[gp])

            def hint_slice(ic, tsub):
                col = ic * 1024 + tsub * 512
                if col < 12288:
                    return ar_q[:, col:col + 512]
                elif col < 12288 + 16384:
                    return ar_o[:, col - 12288:col - 12288 + 512]
                return ar_ao[:, col - 28672:col - 28672 + 512]

            if t.get("_dbg") and b == 0 and tc == 0:
                ctx.wait("sync", "dve", hint_ev[GP_N - 1])
                dmpi = ctx.dma("sync", t["dbg_hint"], ar_q[:], "sst")
                ctx.wait("tensor", "sst", dmpi)
            wdn_buf = [ar_h[:, 0:5632], ar_h[:, 5632:11264]]
            wdn_tgt = [0] * DCH

            def wdn_load(hc, bi):
                tgt = 0
                for j in range(11):
                    src = t["cache_dn"][j * 512:(j + 1) * 512, hc * 128:(hc + 1) * 128]
                    src = src.rearrange("(c p) o -> p c o", p=128)
                    ctx.dma("sync", wdn_buf[bi][:, j * 512:(j + 1) * 512], src,
                            ("sa", "sb")[bi])
                wdn_tgt[hc] = ("sa", "sb")[bi]

            ctx.wait("sync", "pe", max(wgu_free))
            wdn_load(0, 0)
            wdn_load(1, 1)
            y_sb = ar_x
            y_ev = [0] * DCH
            wdn_free = [0, 0]
            ctx.wait("tensor", "dve", hint_ev[GP_N - 1])
            for hc in range(DCH):
                ctx.wait("tensor", wdn_tgt[hc], 1)
                for tsub in range(2):
                    bk = rot.acquire()
                    for ic in range(GP_N):
                        mm(pb[bk][:, 0:512], wdn_buf[hc % 2][:, ic * 128:(ic + 1) * 128],
                           hint_slice(ic, tsub),
                           start=(ic == 0), stop=(ic == GP_N - 1),
                           inc="pe" if ic == GP_N - 1 else None)
                    ctx.wait("scalar", "pe", ctx.at("pe"))
                    yev = acopy(y_sb[:, hc * 1024 + tsub * 512:hc * 1024 + tsub * 512 + 512],
                                pb[bk][:, 0:512])
                    ctx.bank_free(bk, "act", yev)
                y_ev[hc] = yev
                wdn_free[hc % 2] = ctx.at("pe")
                if hc + 2 < DCH:
                    ctx.wait("sync", "pe", wdn_free[hc % 2])
                    wdn_load(hc + 2, hc % 2)

            if t.get("_dbg") and b == 0 and tc == 0:
                ctx.wait("sync", "dve", y_ev[DCH - 1])
                dmpy = ctx.dma("sync", t["dbg_y"], y_sb[:], "sst")
                ctx.wait("tensor", "sst", dmpy)
            ln_stats(lambda c: y_sb[:, c * 1024:(c + 1) * 1024], DCH, TW,
                     [4, 5], [6, 7],
                     EPS_LN * EPS_FF * (2.0 ** -28), EPS_LN, False,
                     ready_fn=lambda c: ("act", y_ev[c]))

            ctx.wait("sync", "pe", ctx.at("pe"))
            ctx.wait("sync", "act", ctx.at("act"))
            x2_load(0)
            x2_load(1)
            stg = [ar_st[:, 0:512], ar_st[:, 512:1024]]
            st_free = [0, 0]
            vop(lambda e, S: e.tensor_copy(m_bf_f[:], m_b[:, 0:1024]))
            vop(lambda e, S: e.tensor_copy(r_bf_f[:], r_b[:, 0:1024]))
            for dc in range(DCH):
                ctx.wait("vector", x2_tgt[dc], 1)
                vop(lambda e, S, dc=dc:
                    e.tensor_tensor(tmp_bf, y_sb[:, dc * 1024:(dc + 1) * 1024],
                                    m_bf_f[:], op=ALU.subtract))
                vop(lambda e, S:
                    e.tensor_tensor(tmp_bf, tmp_bf, r_bf_f[:], op=ALU.mult))
                lastoev = 0
                for tsub in range(2):
                    bi = tsub
                    ctx.wait("vector", ("so0", "so1")[bi], 1)
                    oev2 = vop(lambda e, S, dc=dc, bi=bi, tsub=tsub:
                               e.tensor_tensor(stg[bi][:],
                                               x2r[dc % 2][:, tsub * 512:tsub * 512 + 512],
                                               tmp_bf[:, tsub * 512:tsub * 512 + 512],
                                               op=ALU.add),
                               inc="dve")
                    ctx.wait("sync", "dve", oev2)
                    st_free[bi] = ctx.dma(
                        "sync",
                        t["out"][b, dc * 128:(dc + 1) * 128,
                                 tr0 + tsub * 512:tr0 + tsub * 512 + 512],
                        stg[bi][:], ("so0", "so1")[bi])
                    lastoev = oev2
                ffn_dve_last = lastoev
                if dc + 2 < DCH:
                    ctx.wait("sync", "dve", lastoev)
                    x2_load(dc + 2)

    ctx.wait("sync", "sst", ctx.at("sst"))
    ctx.wait("sync", "so0", 1)
    ctx.wait("sync", "so1", 1)

    for qn in ("vector",):
        s, e = dq_range[qn]
        for attr in ("q", "meta"):
            q = getattr(ctx, attr)[qn]
            dq = q[s:e]
            rest = q[:s] + q[e:]
            a = b0_mark[qn] - (e - s)
            bnd = ffn0_mark[qn] - (e - s)
            mid = rest[a:bnd]
            n, m = len(mid), len(dq)
            out = []
            j = 0
            for i, op in enumerate(mid):
                while j < m and j * n <= i * m:
                    out.append(dq[j])
                    j += 1
                out.append(op)
            out.extend(dq[j:])
            getattr(ctx, attr)[qn] = rest[:a] + out + rest[bnd:]



def _pair_perm():
    perm = np.empty(2 * I_FF, dtype=np.int64)
    for g in range(GP_N):
        perm[g * 256:g * 256 + 128] = np.arange(g * 128, (g + 1) * 128)
        perm[g * 256 + 128:(g + 1) * 256] = np.arange(I_FF + g * 128, I_FF + (g + 1) * 128)
    return perm


def _host_prep(inputs):
    f32 = np.float32
    x = np.ascontiguousarray(
        np.asarray(inputs["L0_attn_hidden_states"], dtype=f32).transpose(0, 2, 1))
    wqdT = np.ascontiguousarray(np.asarray(inputs["L0_attn_Wq_down"], f32).T)
    wquT = np.ascontiguousarray(np.asarray(inputs["L0_attn_Wq_up"], f32).T)
    wkvT = np.ascontiguousarray(np.asarray(inputs["L0_attn_Wkv_down"], f32).T)
    kup = np.ascontiguousarray(
        np.asarray(inputs["L0_attn_k_up"], f32).transpose(1, 0, 2).reshape(RKV, NH * HD))
    vup = np.ascontiguousarray(
        np.asarray(inputs["L0_attn_v_up"], f32).transpose(1, 0, 2).reshape(RKV, NH * HD))
    woT = np.ascontiguousarray(np.asarray(inputs["L0_attn_Wo"], f32).T)

    perm = _pair_perm()
    gpk = np.asarray(inputs["L0_ffn_gate_up_packed"], np.int32).view(np.uint16)
    gpk = np.ascontiguousarray(np.repeat(gpk[perm].T, 4, axis=0))
    gsc = np.asarray(inputs["L0_ffn_gate_up_scales"], np.float16)
    gsc = np.ascontiguousarray(np.repeat(gsc[:, perm], 128, axis=0))
    dpk = np.asarray(inputs["L0_ffn_down_packed"], np.int32).view(np.uint16)
    dpk = np.ascontiguousarray(np.repeat(dpk.T, 4, axis=0))
    dsc = np.asarray(inputs["L0_ffn_down_scales"], np.float16)
    dsc = np.ascontiguousarray(np.repeat(dsc, 128, axis=0))

    ones = np.ones((128, 128), dtype=BF)
    shifts = np.broadcast_to(
        (4 * (np.arange(128) % 4)).astype(np.uint16).reshape(128, 1), (128, 512)).copy()

    bias = np.zeros((128, NH * NCB), dtype=f32)
    p = np.arange(128, dtype=np.float64)
    for h in range(NH):
        qw = QW[h]
        slope = SLOPES[h]
        center = (qw - 1) / 2.0
        for c in range(NCB):
            delta = c * 128 - (2048 - qw)
            if delta > qw - 128:
                bias[:, h * NCB + c] = -30000.0
            else:
                bias[:, h * NCB + c] = (slope * (delta + p - center)).astype(f32)

    return x, dict(wqdT=wqdT, wquT=wquT, wkvT=wkvT, kup=kup, vup=vup, woT=woT,
                   gu_pk=gpk, gu_sc=gsc, dn_pk=dpk, dn_sc=dsc,
                   c_ones=ones, c_shifts=shifts, c_bias=bias)


def kernel(**inputs):
    x, shared = _host_prep(inputs)
    if "nc" not in _CACHE:
        _CACHE["nc"] = build_nc()
    nc = _CACHE["nc"]
    in_maps = []
    for c in range(NCORES):
        m = dict(shared)
        m["x"] = np.ascontiguousarray(x[c * BC:(c + 1) * BC])
        in_maps.append(m)
    res = run_bass_kernel_spmd(nc, in_maps, core_ids=list(range(NCORES)))
    out = np.concatenate([res.results[c]["out"] for c in range(NCORES)], axis=0)
    return np.ascontiguousarray(out.transpose(0, 2, 1))


def simulate_deadlock(ctx):
    pos = {e: 0 for e in ENGINES}
    cnt = {}
    qs = ctx.meta
    progress = True
    while progress:
        progress = False
        for e in ENGINES:
            while pos[e] < len(qs[e]):
                m = qs[e][pos[e]]
                if m[0] == "w":
                    if cnt.get(m[1], 0) >= m[2]:
                        pos[e] += 1
                        progress = True
                    else:
                        break
                else:
                    if m[1] is not None:
                        cnt[m[1]] = cnt.get(m[1], 0) + m[2]
                    pos[e] += 1
                    progress = True
    stuck = {e: pos[e] for e in ENGINES if pos[e] < len(qs[e])}
    if not stuck:
        print("NO DEADLOCK: all queues drain")
        return
    print("DEADLOCK! queue positions:")
    for e, p in stuck.items():
        m = qs[e][p]
        print(f"  {e}: pos {p}/{len(qs[e])} waiting {m} (cnt={cnt.get(m[1],0)})")


# version 26
# speedup vs baseline: 1.0683x; 1.0683x over previous
import contextlib
import math

import numpy as np
import ml_dtypes

import concourse.bass as bass
import concourse.mybir as mybir
from concourse.bass_utils import run_bass_kernel_spmd

BF = ml_dtypes.bfloat16
F32 = mybir.dt.float32
BF16 = mybir.dt.bfloat16
FP16 = mybir.dt.float16
U16 = mybir.dt.uint16
AF = mybir.ActivationFunctionType
ALU = mybir.AluOpType

NCORES = 8
BC = 2
L = 2048
D = 2048
DCH = 16
RQ = 1536
RQC = 12
RKV = 512
RKC = 4
NH = 16
HD = 128
I_FF = 5632
GP_N = 44
TW = 1024
HALF = 1024
SCALE = 1.0 / math.sqrt(HD)
BASE = 2.0 ** (-0.5)
SLOPES = [BASE ** (i + 1) for i in range(NH)]
QW = [128, 128, 256, 256] + [512] * 12
NCB = 19
EPS_LN = 1e-5
EPS_FF = 1e-6
S14 = 16384.0

ENGINES = ("tensor", "vector", "scalar", "gpsimd", "sync")

_CACHE = {}


class Ctx:
    def __init__(self):
        self.q = {e: [] for e in ENGINES}
        self.meta = {e: [] for e in ENGINES}
        self.cnt = {}
        self.pg = {i: ("dve", 0) for i in range(8)}

    def newsem(self, name):
        self.cnt[name] = 0

    def at(self, name):
        return self.cnt[name]

    def wait(self, eng, sname, val):
        if val is None or val <= 0:
            return
        if sname in ("gld", "sld", "sst", "ga", "gb", "sa", "sb", "so0", "so1"):
            val = self.cnt[sname]
        if val <= 0:
            return
        def f(e, S, sname=sname, val=val):
            e.wait_ge(S[sname], val)
        self.q[eng].append(f)
        self.meta[eng].append(("w", sname, val))

    def op(self, eng, fn, inc=None):
        tgt = None
        if inc is not None:
            self.cnt[inc] += 1
            tgt = self.cnt[inc]
        def f(e, S, fn=fn, inc=inc):
            ins = fn(e, S)
            if inc is not None:
                ins.then_inc(S[inc], 1)
        self.q[eng].append(f)
        self.meta[eng].append(("i", inc, 1))
        return tgt

    def dma(self, queue, out, in_, sname):
        self.cnt[sname] += 16
        tgt = self.cnt[sname]
        def f(e, S, out=out, in_=in_, sname=sname):
            e.dma_start(out=out, in_=in_).then_inc(S[sname], 16)
        self.q[queue].append(f)
        self.meta[queue].append(("i", sname, 16))
        return tgt

    def bank_wait(self, i, eng="tensor"):
        sem, cnt = self.pg[i]
        self.wait(eng, sem, cnt)

    def bank_free(self, i, sem, cnt):
        self.pg[i] = (sem, cnt)


def _split_rows(n_rows, max_rows=512):
    out = []
    r = 0
    while r < n_rows:
        c = min(max_rows, n_rows - r)
        out.append((r, c))
        r += c
    return out


def build_nc(dbg_x2=False):
    nc = bass.Bass("TRN2", target_bir_lowering=False)

    t = {}
    t["x"] = nc.dram_tensor("x", [BC, D, L], F32, kind="ExternalInput").ap()
    t["wqdT"] = nc.dram_tensor("wqdT", [D, RQ], F32, kind="ExternalInput").ap()
    t["wquT"] = nc.dram_tensor("wquT", [RQ, D], F32, kind="ExternalInput").ap()
    t["wkvT"] = nc.dram_tensor("wkvT", [D, RKV], F32, kind="ExternalInput").ap()
    t["kup"] = nc.dram_tensor("kup", [RKV, D], F32, kind="ExternalInput").ap()
    t["vup"] = nc.dram_tensor("vup", [RKV, D], F32, kind="ExternalInput").ap()
    t["woT"] = nc.dram_tensor("woT", [D, D], F32, kind="ExternalInput").ap()
    t["gu_pk"] = nc.dram_tensor("gu_pk", [D, 2 * I_FF], U16, kind="ExternalInput").ap()
    t["gu_sc"] = nc.dram_tensor("gu_sc", [D, 2 * I_FF], FP16, kind="ExternalInput").ap()
    t["dn_pk"] = nc.dram_tensor("dn_pk", [I_FF, D], U16, kind="ExternalInput").ap()
    t["dn_sc"] = nc.dram_tensor("dn_sc", [I_FF, D], FP16, kind="ExternalInput").ap()
    t["c_ones"] = nc.dram_tensor("c_ones", [128, 128], BF16, kind="ExternalInput").ap()
    t["c_shifts"] = nc.dram_tensor("c_shifts", [128, 512], U16, kind="ExternalInput").ap()
    t["c_bias"] = nc.dram_tensor("c_bias", [128, NH * NCB], F32, kind="ExternalInput").ap()
    t["out"] = nc.dram_tensor("out", [BC, D, L], F32, kind="ExternalOutput").ap()
    t["cache_gu"] = nc.dram_tensor("cache_gu", [D, 2 * I_FF], BF16,
                                   kind="ExternalOutput" if dbg_x2 else "Internal").ap()
    t["cache_dn"] = nc.dram_tensor("cache_dn", [I_FF, D], BF16,
                                   kind="ExternalOutput" if dbg_x2 else "Internal").ap()
    t["x2c"] = nc.dram_tensor("x2c", [BC, D, L], BF16,
                          kind="ExternalOutput" if dbg_x2 else "Internal").ap()
    if dbg_x2:
        t["dbg_o"] = nc.dram_tensor("dbg_o", [128, 16384], BF16, kind="ExternalOutput").ap()
        t["dbg_attn"] = nc.dram_tensor("dbg_attn", [128, 512], BF16, kind="ExternalOutput").ap()
        t["dbg_den"] = nc.dram_tensor("dbg_den", [128, 512], F32, kind="ExternalOutput").ap()
        t["dbg_ax"] = nc.dram_tensor("dbg_ax", [128, 16384], BF16, kind="ExternalOutput").ap()
        t["dbg_ao3"] = nc.dram_tensor("dbg_ao3", [128, 16384], BF16, kind="ExternalOutput").ap()
        t["dbg_h2"] = nc.dram_tensor("dbg_h2", [128, 16384], BF16, kind="ExternalOutput").ap()
        t["dbg_hint"] = nc.dram_tensor("dbg_hint", [128, 12288], BF16, kind="ExternalOutput").ap()
        t["dbg_y"] = nc.dram_tensor("dbg_y", [128, 16384], BF16, kind="ExternalOutput").ap()
        t["dbg_ao2"] = nc.dram_tensor("dbg_ao2", [128, 16384], BF16, kind="ExternalOutput").ap()
    t["_dbg"] = dbg_x2

    ctx = Ctx()
    sem_names = ["pe", "act", "dve", "gp", "gld", "sld", "sst",
                 "ga", "gb", "sa", "sb", "so0", "so1", "dqv"]
    for s in sem_names:
        ctx.newsem(s)

    es = contextlib.ExitStack()
    with es:
        sb = {}
        sb["ar_x"] = es.enter_context(nc.sbuf_tensor("ar_x", [128, 16384], BF16))
        sb["ar_ct"] = es.enter_context(nc.sbuf_tensor("ar_ct", [128, 8192], BF16))
        sb["ar_q"] = es.enter_context(nc.sbuf_tensor("ar_q", [128, 12288], BF16))
        sb["ar_o"] = es.enter_context(nc.sbuf_tensor("ar_o", [128, 16384], BF16))
        sb["ar_ao"] = es.enter_context(nc.sbuf_tensor("ar_ao", [128, 16384], BF16))
        sb["ar_h"] = es.enter_context(nc.sbuf_tensor("ar_h", [128, 12288], BF16))
        sb["ar_w"] = es.enter_context(nc.sbuf_tensor("ar_w", [128, 6144], BF16))
        sb["ar_dq"] = es.enter_context(nc.sbuf_tensor("ar_dq", [128, 4096], BF16))
        sb["ar_s"] = es.enter_context(nc.sbuf_tensor("ar_s", [128, 4096], F32))
        sb["ar_st"] = es.enter_context(nc.sbuf_tensor("ar_st", [128, 1024], F32))
        sb["s_wsum"] = es.enter_context(nc.sbuf_tensor("s_wsum", [128, 16], F32))
        sb["s_ones"] = es.enter_context(nc.sbuf_tensor("s_ones", [128, 128], BF16))
        sb["s_shift"] = es.enter_context(nc.sbuf_tensor("s_shift", [128, 512], U16))
        sb["s_bias"] = es.enter_context(nc.sbuf_tensor("s_bias", [128, NH * NCB], F32))
        pb = [es.enter_context(nc.psum_tensor(f"pb{i}", [128, 512], F32))
              for i in range(8)]
        S = {s: es.enter_context(nc.semaphore(s)) for s in sem_names}

        emit_program(ctx, t, sb, pb)
        _CACHE["ctx"] = ctx

        with nc.Block() as block:
            @block.tensor
            def _(e):
                for f in ctx.q["tensor"]:
                    f(e, S)

            @block.vector
            def _(e):
                for f in ctx.q["vector"]:
                    f(e, S)

            @block.scalar
            def _(e):
                for f in ctx.q["scalar"]:
                    f(e, S)

            @block.gpsimd
            def _(e):
                for f in ctx.q["gpsimd"]:
                    f(e, S)

            @block.sync
            def _(e):
                for f in ctx.q["sync"]:
                    f(e, S)

    return nc


def emit_program(ctx, t, sb, pb):
    ar_x, ar_ct, ar_q, ar_o, ar_ao, ar_h, ar_w = (
        sb["ar_x"], sb["ar_ct"], sb["ar_q"], sb["ar_o"], sb["ar_ao"],
        sb["ar_h"], sb["ar_w"])
    ar_s, ar_st, ar_dq = sb["ar_s"], sb["ar_st"], sb["ar_dq"]
    s_wsum, s_ones, s_shift, s_bias = (
        sb["s_wsum"], sb["s_ones"], sb["s_shift"], sb["s_bias"])

    m_b = ar_s[:, 0:1024]
    r_b = ar_s[:, 1024:2048]
    m_bf = sb["ar_h"][:, 5632:6656]
    r_bf = sb["ar_h"][:, 6656:7680]
    m_bf_f = sb["ar_ct"][:, 3072:4096]
    r_bf_f = sb["ar_ct"][:, 4096:5120]
    tmp_b = ar_s[:, 2048:3072]
    aux_b = ar_s[:, 3072:4096]
    aux_bf = ar_s[:, 2048:4096].bitcast(BF16)

    dq_pk = [ar_dq[:, 0:512].bitcast(U16), ar_dq[:, 512:1024].bitcast(U16)]
    dq_sc = [ar_dq[:, 1024:1536].bitcast(FP16), ar_dq[:, 1536:2048].bitcast(FP16)]
    dq_t1 = ar_dq[:, 2048:2560].bitcast(U16)
    dq_lo = ar_dq[:, 2560:3072].bitcast(U16)
    dq_w = [ar_dq[:, 3072:3584], ar_dq[:, 3584:4096]]

    def mm(out_ap, lhsT, rhs, start, stop, inc=None):
        return ctx.op(
            "tensor",
            lambda e, S, o=out_ap, a=lhsT, b=rhs, st=start, sp=stop:
                e.matmul(o, a, b, start=st, stop=sp),
            inc=inc)

    def vop(fn, inc=None):
        return ctx.op("vector", fn, inc=inc)

    def aop(fn, inc=None):
        return ctx.op("scalar", fn, inc=inc)

    ctx.dma("sync", s_ones[:], t["c_ones"], "sa")
    ctx.dma("sync", s_shift[:], t["c_shifts"], "sa")
    ctx.dma("sync", s_bias[:], t["c_bias"], "sa")
    ctx.wait("vector", "sa", 1)
    ctx.wait("scalar", "sa", 1)
    ctx.wait("tensor", "sa", 1)
    dq_range = {"vector": None, "sync": None}
    dq_s = {qn: len(ctx.q[qn]) for qn in ("vector", "sync")}

    dq_specs = []
    for (pk_name, sc_name, c_name, n_dc, width) in [
        ("gu_pk", "gu_sc", "cache_gu", DCH, 2 * I_FF),
        ("dn_pk", "dn_sc", "cache_dn", GP_N, D),
    ]:
        for dc in range(n_dc):
            for ob in range(width // 512):
                dq_specs.append((pk_name, sc_name, c_name, dc, ob))
    n_dq = len(dq_specs)
    dq_ld = [0] * n_dq
    dq_cons = [0] * n_dq
    dq_st = [0] * n_dq

    def dq_load(i):
        pk_name, sc_name, c_name, dc, ob = dq_specs[i]
        bi = i % 2
        sem = ("sa", "sb")[bi]
        if i >= 2:
            ctx.wait("sync", "dqv", dq_cons[i - 2])
        ctx.dma("sync", dq_pk[bi][:],
                t[pk_name][dc * 128:(dc + 1) * 128, ob * 512:(ob + 1) * 512], sem)
        ctx.dma("sync", dq_sc[bi][:],
                t[sc_name][dc * 128:(dc + 1) * 128, ob * 512:(ob + 1) * 512], sem)
        dq_ld[i] = sem

    dq_load(0)
    dq_load(1)
    for i in range(n_dq):
        pk_name, sc_name, c_name, dc, ob = dq_specs[i]
        bi = i % 2
        ctx.wait("vector", dq_ld[i], 1)
        if i >= 2:
            ctx.wait("vector", dq_st[i - 2], 1)
        vop(lambda e, S, a=dq_pk[bi]:
            e.tensor_tensor(dq_t1[:], a[:], s_shift[:], op=ALU.logical_shift_right))
        vop(lambda e, S: e.tensor_scalar(dq_lo[:], dq_t1[:], 7, 9,
                                         op0=ALU.bitwise_and, op1=ALU.logical_shift_left))
        vop(lambda e, S, o=dq_pk[bi]: e.tensor_scalar(o[:], dq_t1[:], 8, 12,
                                                      op0=ALU.bitwise_and, op1=ALU.logical_shift_left))
        vop(lambda e, S, b2=dq_pk[bi]:
            e.tensor_tensor(dq_t1[:], dq_lo[:], b2[:], op=ALU.bitwise_or))
        dq_cons[i] = vop(
            lambda e, S, o=dq_w[bi], b2=dq_sc[bi]:
                e.tensor_tensor(o[:], dq_t1[:].bitcast(FP16), b2[:], op=ALU.mult),
            inc="dqv")
        ctx.wait("sync", "dqv", dq_cons[i])
        ctx.dma(
            "sync", t[c_name][dc * 128:(dc + 1) * 128, ob * 512:(ob + 1) * 512],
            dq_w[bi][:], ("so0", "so1")[bi])
        dq_st[i] = ("so0", "so1")[bi]
        if i + 2 < n_dq:
            dq_load(i + 2)
    dq_range["vector"] = (dq_s["vector"], len(ctx.q["vector"]))
    dq_range["sync"] = (dq_s["sync"], len(ctx.q["sync"]))
    b0_mark = {}
    ffn0_mark = {}

    class Rot:
        def __init__(self, banks):
            self.banks = banks
            self.i = 0

        def acquire(self):
            j = self.banks[self.i % len(self.banks)]
            self.i += 1
            ctx.bank_wait(j)
            return j

    rot = Rot([4, 5, 6, 7])

    def ln_stats(src_fn, n_ch, width, s1b, s2b, eps_const, extra_coef, negate,
                 ready_fn=None, load_fn=None):
        nts = width // 512
        for bk in s1b + s2b:
            ctx.bank_wait(bk)
        sq = [aux_bf[:, 0:width], aux_bf[:, width:2 * width]]
        c_sq = [0] * n_ch
        c_done = [0] * n_ch
        for c in range(n_ch):
            if ready_fn is not None:
                sem, cnt = ready_fn(c)
                ctx.wait("scalar", sem, cnt)
                ctx.wait("tensor", sem, cnt)
            if c >= 2:
                ctx.wait("scalar", "pe", c_done[c - 2])
            c_sq[c] = aop(lambda e, S, o=sq[c % 2], a=src_fn(c):
                          e.activation(out=o, in_=a, func=AF.Square), inc="act")
            for ts_ in range(nts):
                mm(pb[s1b[ts_]][:, 0:512], s_ones[:],
                   src_fn(c)[:, ts_ * 512:(ts_ + 1) * 512],
                   start=(c == 0), stop=(c == n_ch - 1))
            ctx.wait("tensor", "act", c_sq[c])
            for ts_ in range(nts):
                mm(pb[s2b[ts_]][:, 0:512], s_ones[:],
                   sq[c % 2][:, ts_ * 512:(ts_ + 1) * 512],
                   start=(c == 0), stop=(c == n_ch - 1),
                   inc="pe" if ts_ == nts - 1 else None)
            c_done[c] = ctx.at("pe")
            if load_fn is not None and c + 2 < n_ch:
                load_fn(c + 2, c_done[c], c_sq[c])
        done = ctx.at("pe")
        ctx.wait("vector", "pe", done)
        inv = 1.0 / D
        for ts_ in range(nts):
            sl = slice(ts_ * 512, (ts_ + 1) * 512)
            vop(lambda e, S, b2=pb[s1b[ts_]], sl=sl:
                e.tensor_scalar(m_b[:, sl], b2[:, 0:512], inv, None, op0=ALU.mult))
            vop(lambda e, S, b2=pb[s2b[ts_]], sl=sl:
                e.tensor_scalar(aux_b[:, sl], b2[:, 0:512], inv, None, op0=ALU.mult))
        w = width
        vop(lambda e, S: e.tensor_tensor(tmp_b[:, 0:w], m_b[:, 0:w], m_b[:, 0:w], op=ALU.mult))
        vop(lambda e, S: e.tensor_tensor(tmp_b[:, 0:w], aux_b[:, 0:w], tmp_b[:, 0:w], op=ALU.subtract))
        if extra_coef:
            vop(lambda e, S, k=extra_coef:
                e.scalar_tensor_tensor(tmp_b[:, 0:w], aux_b[:, 0:w], k, tmp_b[:, 0:w],
                                       op0=ALU.mult, op1=ALU.add))
        vop(lambda e, S, ec=eps_const:
            e.tensor_scalar(tmp_b[:, 0:w], tmp_b[:, 0:w], ec, None, op0=ALU.add))
        rc = vop(lambda e, S: e.reciprocal(tmp_b[:, 0:w], tmp_b[:, 0:w]), inc="dve")
        for bk in s1b + s2b:
            ctx.bank_free(bk, "dve", rc)
        ctx.wait("scalar", "dve", rc)
        sd = aop(lambda e, S: e.activation(out=r_b[:, 0:w], in_=tmp_b[:, 0:w], func=AF.Sqrt),
                 inc="act")
        ctx.wait("vector", "act", sd)
        if negate:
            vop(lambda e, S: e.tensor_scalar(r_b[:, 0:w], r_b[:, 0:w], -1.0, None, op0=ALU.mult))
        return rc

    def cast_cols(src2d, nrows, col0, ncols, dst, slot=0, split=512):
        sem = ("ga", "gb")[slot % 2]
        for (r, c) in _split_rows(nrows, split):
            src = src2d[r:r + c, col0:col0 + ncols]
            src = src.rearrange("(c p) j -> p c j", p=128)
            d0 = (r // 128) * ncols
            ctx.dma("gpsimd", dst[:, d0:d0 + (c // 128) * ncols], src, sem)
        return sem

    x2_wr_last = 0
    ffn_dve_last = 0

    prev_xst = [0] * DCH
    for b in range(BC):
        if b == 0:
            b0_mark.update({qn: len(ctx.q[qn]) for qn in ("vector", "sync")})
        for Hf in range(2):
            q0 = Hf * HALF
            KR = (Hf + 1) * HALF
            KT = KR // 128

            ctx.wait("gpsimd", "dve", ffn_dve_last)
            x_tgt = []
            for dc in range(DCH):
                if Hf == 1:
                    ctx.wait("gpsimd", "dve", prev_xst[dc])
                else:
                    ctx.wait("gpsimd", "sst", 1)
                src = t["x"][b, dc * 128:(dc + 1) * 128, q0:q0 + HALF]
                ctx.dma("gpsimd", ar_x[:, dc * 1024:(dc + 1) * 1024], src,
                        ("ga", "gb")[dc % 2])
                x_tgt.append(("ga", "gb")[dc % 2])

            ln_stats(lambda c: ar_x[:, c * 1024:(c + 1) * 1024], DCH, HALF,
                     [4, 5], [6, 7], EPS_LN, 0.0, True,
                     ready_fn=lambda c: (x_tgt[c], 1))

            wqd_buf = [ar_w[:, 0:2048], ar_w[:, 2048:4096]]
            wqd_free = [0, 0]

            def proj(n_out_ch, wsrc, wsum_off, out_fn):
                wtgt = [0] * n_out_ch
                ctx.wait("gpsimd", "dve", wqd_free[0])
                ctx.wait("gpsimd", "sst", 1)
                wtgt[0] = cast_cols(t[wsrc], D, 0, 128, wqd_buf[0], slot=0)
                last_ev = 0
                for oc in range(n_out_ch):
                    if oc + 1 < n_out_ch:
                        ctx.wait("gpsimd", "dve", wqd_free[(oc + 1) % 2])
                        wtgt[oc + 1] = cast_cols(t[wsrc], D, (oc + 1) * 128, 128,
                                                 wqd_buf[(oc + 1) % 2], slot=oc + 1)
                    wb = wqd_buf[oc % 2]
                    ctx.wait("tensor", wtgt[oc], 1)
                    bk = rot.acquire()
                    for dc in range(DCH):
                        if oc == 0:
                            ctx.wait("tensor", x_tgt[dc], 1)
                        mm(pb[bk][:, 0:1], wb[:, dc * 128:(dc + 1) * 128], s_ones[:, 0:1],
                           start=(dc == 0), stop=(dc == DCH - 1),
                           inc="pe" if dc == DCH - 1 else None)
                    ctx.wait("vector", "pe", ctx.at("pe"))
                    wev = vop(lambda e, S, b2=pb[bk], oc=oc, off=wsum_off:
                              e.tensor_copy(s_wsum[:, off + oc:off + oc + 1], b2[:, 0:1]),
                              inc="dve")
                    ctx.bank_free(bk, "dve", wev)
                    for tsub in range(2):
                        bk = rot.acquire()
                        for dc in range(DCH):
                            mm(pb[bk][:, 0:512], wb[:, dc * 128:(dc + 1) * 128],
                               ar_x[:, dc * 1024 + tsub * 512:dc * 1024 + tsub * 512 + 512],
                               start=(dc == 0), stop=(dc == DCH - 1),
                               inc="pe" if dc == DCH - 1 else None)
                        ctx.wait("vector", "pe", ctx.at("pe"))
                        msl = slice(tsub * 512, tsub * 512 + 512)
                        vop(lambda e, S, b2=pb[bk], msl=msl, oc=oc, off=wsum_off:
                            e.scalar_tensor_tensor(tmp_b[:, 0:512], m_b[:, msl],
                                                   s_wsum[:, off + oc:off + oc + 1],
                                                   b2[:, 0:512],
                                                   op0=ALU.mult, op1=ALU.subtract))
                        ev = vop(lambda e, S, o=out_fn(oc, tsub), msl=msl:
                                 e.tensor_tensor(o, tmp_b[:, 0:512], r_b[:, msl],
                                                 op=ALU.mult),
                                 inc="dve")
                        ctx.bank_free(bk, "dve", ev)
                        last_ev = ev
                    wqd_free[oc % 2] = last_ev
                return last_ev

            qlow = ar_q
            proj(RQC, "wqdT", 0,
                 lambda oc, tsub: qlow[:, oc * 1024 + tsub * 512:oc * 1024 + tsub * 512 + 512])
            lastc = proj(RKC, "wkvT", 12,
                         lambda oc, tsub: ar_ct[:, oc * 2048 + q0 + tsub * 512:
                                                oc * 2048 + q0 + tsub * 512 + 512])

            vtokS = [ar_h[:, 0:4096], ar_ao[:, 0:4096]]
            kTS = [ar_h[:, 4096:6144], ar_ao[:, 4096:6144]]
            qTS = [ar_h[:, 6144:7168], ar_ao[:, 6144:7168]]
            attn = [ar_h[:, 7168:7680], ar_h[:, 7680:8192]]
            wqu_buf = [ar_h[:, 8192:9728], ar_h[:, 9728:11264]]
            kup_buf = [ar_w[:, 4096:4608], ar_w[:, 4608:5120]]
            vup_buf = [ar_w[:, 0:1024], ar_w[:, 1024:2048]]
            kup_tgt = [0] * NH
            wqu_tgt = [0] * NH
            vup_tgt = [0] * (NH // 2)
            kup_free = [0, 0]
            wqu_free = [0, 0]
            vup_free = [0, 0]
            chain_done = [0] * NH

            ctx.wait("gpsimd", "dve", lastc)
            ctx.wait("gpsimd", "pe", ctx.at("pe"))
            ctx.wait("gpsimd", "sst", 1)
            ctx.wait("tensor", "dve", lastc)

            def prefetch_w(h):
                pr = h // 2
                ctx.wait("gpsimd", "dve", kup_free[h % 2])
                kup_tgt[h] = cast_cols(t["kup"], RKV, h * 128, 128,
                                       kup_buf[h % 2], slot=h)
                ctx.wait("gpsimd", "dve", wqu_free[h % 2])
                wqu_tgt[h] = cast_cols(t["wquT"], RQ, h * 128, 128,
                                       wqu_buf[h % 2], slot=h)
                if h % 2 == 0:
                    ctx.wait("gpsimd", "dve", vup_free[pr % 2])
                    vup_tgt[pr] = cast_cols(t["vup"], RKV, pr * 256, 256,
                                            vup_buf[pr % 2], slot=pr)

            def chain_gen(h):
                pr = h // 2
                if h % 2 == 0:
                    ctx.wait("tensor", vup_tgt[pr], 1)
                    vt = vtokS[pr % 2]
                    for tt in range(KT):
                        bk = rot.acquire()
                        for rc in range(RKC):
                            mm(pb[bk][:, 0:256],
                               ar_ct[:, rc * 2048 + tt * 128:rc * 2048 + tt * 128 + 128],
                               vup_buf[pr % 2][:, rc * 256:(rc + 1) * 256],
                               start=(rc == 0), stop=(rc == RKC - 1),
                               inc="pe" if rc == RKC - 1 else None)
                        ctx.wait("vector", "pe", ctx.at("pe"))
                        vev = vop(lambda e, S, b2=pb[bk], tt=tt, vt=vt:
                                  e.tensor_copy(vt[:, tt * 256:(tt + 1) * 256], b2[:, 0:256]),
                                  inc="dve")
                        ctx.bank_free(bk, "dve", vev)
                        yield
                    vup_free[pr % 2] = vev
                ctx.wait("tensor", kup_tgt[h], 1)
                kt_ = kTS[h % 2]
                for ks in range(KR // 512):
                    bk = rot.acquire()
                    for rc in range(RKC):
                        mm(pb[bk][:, 0:512], kup_buf[h % 2][:, rc * 128:(rc + 1) * 128],
                           ar_ct[:, rc * 2048 + ks * 512:rc * 2048 + ks * 512 + 512],
                           start=(rc == 0), stop=(rc == RKC - 1),
                           inc="pe" if rc == RKC - 1 else None)
                    ctx.wait("vector", "pe", ctx.at("pe"))
                    kev = vop(lambda e, S, b2=pb[bk], ks=ks, kt_=kt_:
                              e.tensor_copy(kt_[:, ks * 512:(ks + 1) * 512], b2[:, 0:512]),
                              inc="dve")
                    ctx.bank_free(bk, "dve", kev)
                    yield
                kup_free[h % 2] = kev
                ctx.wait("tensor", wqu_tgt[h], 1)
                qt_ = qTS[h % 2]
                for tsub in range(2):
                    bk = rot.acquire()
                    for r1c in range(RQC):
                        mm(pb[bk][:, 0:512], wqu_buf[h % 2][:, r1c * 128:(r1c + 1) * 128],
                           qlow[:, r1c * 1024 + tsub * 512:r1c * 1024 + tsub * 512 + 512],
                           start=(r1c == 0), stop=(r1c == RQC - 1),
                           inc="pe" if r1c == RQC - 1 else None)
                    ctx.wait("vector", "pe", ctx.at("pe"))
                    qev = vop(lambda e, S, b2=pb[bk], tsub=tsub, qt_=qt_:
                              e.tensor_copy(qt_[:, tsub * 512:(tsub + 1) * 512], b2[:, 0:512]),
                              inc="dve")
                    ctx.bank_free(bk, "dve", qev)
                    yield
                wqu_free[h % 2] = qev
                chain_done[h] = qev

            prefetch_w(0)
            gen = chain_gen(0)
            for _ in gen:
                pass

            for h in range(NH):
                pr = h // 2
                nxt = None
                if h + 1 < NH:
                    prefetch_w(h + 1)
                    nxt = chain_gen(h + 1)
                vtok = vtokS[pr % 2]
                kT = kTS[h % 2]
                qT = qTS[h % 2]
                ctx.wait("tensor", "dve", chain_done[h])

                qw = QW[h]
                nsub = 512 // qw
                for qc in range(2):
                    q_off = q0 + qc * 512
                    kt_hi = q_off // 128 + 4
                    c_sc = [0] * kt_hi
                    c_ex = [0] * kt_hi
                    c_dn = [0] * kt_hi
                    c_mk = [0] * kt_hi
                    ctx.bank_wait(2)
                    ctx.bank_wait(3)
                    for kt in range(kt_hi + 1):
                        if kt < kt_hi:
                            ctx.bank_wait(kt % 2)
                            c_sc[kt] = mm(pb[kt % 2][:, 0:512], kT[:, kt * 128:(kt + 1) * 128],
                                          qT[:, qc * 512:qc * 512 + 512], True, True, inc="pe")
                            ctx.wait("scalar", "pe", c_sc[kt])
                            for sub in range(nsub):
                                qso = q_off + sub * qw
                                cidx = (kt * 128 - qso + 2048 - qw) // 128
                                bias_ap = s_bias[:, h * NCB + cidx:h * NCB + cidx + 1]
                                c_ex[kt] = aop(
                                    lambda e, S, kt=kt, sub=sub, qw=qw, ba=bias_ap:
                                        e.activation(out=attn[kt % 2][:, sub * qw:(sub + 1) * qw],
                                                     in_=pb[kt % 2][:, sub * qw:(sub + 1) * qw],
                                                     func=AF.Exp, bias=ba, scale=SCALE),
                                    inc="act")
                            ctx.bank_free(kt % 2, "act", c_ex[kt])
                            if kt * 128 + 128 > q_off:
                                ctx.wait("gpsimd", "act", c_ex[kt])
                                c_mk[kt] = ctx.op(
                                    "gpsimd",
                                    lambda e, S, kt=kt, q_off=q_off:
                                        e.affine_select(out=attn[kt % 2][:],
                                                        in_=attn[kt % 2][:],
                                                        pattern=[[1, 512]],
                                                        compare_op=ALU.is_ge, fill=0.0,
                                                        base=q_off - kt * 128,
                                                        channel_multiplier=-1),
                                    inc="gp")
                        if nxt is not None:
                            next(nxt, None)
                        if kt >= 1:
                            j = kt - 1
                            if c_mk[j]:
                                ctx.wait("tensor", "gp", c_mk[j])
                            else:
                                ctx.wait("tensor", "act", c_ex[j])
                            mm(pb[2][:, 0:512],
                               vtok[:, j * 256 + (h % 2) * 128:j * 256 + (h % 2) * 128 + 128],
                               attn[j % 2][:], start=(j == 0), stop=(j == kt_hi - 1))
                            c_dn[j] = mm(pb[3][:, 0:512], s_ones[:], attn[j % 2][:],
                                         start=(j == 0), stop=(j == kt_hi - 1), inc="pe")
                    ctx.wait("vector", "pe", c_dn[kt_hi - 1])
                    vop(lambda e, S: e.reciprocal(tmp_b[:, 0:512], pb[3][:, 0:512]))
                    ocol = h * 1024 + qc * 512
                    oev = vop(lambda e, S, ocol=ocol:
                              e.tensor_tensor(ar_o[:, ocol:ocol + 512], pb[2][:, 0:512],
                                              tmp_b[:, 0:512], op=ALU.mult),
                              inc="dve")
                    ctx.bank_free(2, "dve", oev)
                    ctx.bank_free(3, "dve", oev)
                if nxt is not None:
                    for _ in nxt:
                        pass
            all_o = oev
            if t.get("_dbg") and b == 0 and Hf == 0:
                ctx.wait("sync", "dve", all_o)
                dmp3 = ctx.dma("sync", t["dbg_o"], ar_o[:], "sst")
                dmp3 = ctx.dma("sync", t["dbg_ax"], ar_x[:], "sst")
                ctx.wait("tensor", "sst", dmp3)

            wo_buf = [ar_w[:, 2048:4096], ar_w[:, 4096:6144]]
            wo_tgt = [0] * DCH
            wo_free = [0, max(kup_free)]
            ao_ev = [0] * DCH
            ctx.wait("gpsimd", "dve", max(kup_free))
            wo_tgt[0] = cast_cols(t["woT"], D, 0, 128, wo_buf[0], slot=0)
            for fc in range(DCH):
                if fc + 1 < DCH:
                    ctx.wait("gpsimd", "dve", wo_free[(fc + 1) % 2])
                    wo_tgt[fc + 1] = cast_cols(t["woT"], D, (fc + 1) * 128, 128,
                                               wo_buf[(fc + 1) % 2], slot=fc + 1)
                ctx.wait("tensor", wo_tgt[fc], 1)
                if fc == 0:
                    ctx.wait("tensor", "dve", all_o)
                for tsub in range(2):
                    bk = rot.acquire()
                    for ec in range(DCH):
                        mm(pb[bk][:, 0:512], wo_buf[fc % 2][:, ec * 128:(ec + 1) * 128],
                           ar_o[:, ec * 1024 + tsub * 512:ec * 1024 + tsub * 512 + 512],
                           start=(ec == 0), stop=(ec == DCH - 1),
                           inc="pe" if ec == DCH - 1 else None)
                    ctx.wait("vector", "pe", ctx.at("pe"))
                    aev = vop(lambda e, S, b2=pb[bk], fc=fc, tsub=tsub:
                              e.tensor_copy(
                                  ar_ao[:, fc * 1024 + tsub * 512:fc * 1024 + tsub * 512 + 512],
                                  b2[:, 0:512]),
                              inc="dve")
                    ctx.bank_free(bk, "dve", aev)
                ao_ev[fc] = aev
                wo_free[fc % 2] = aev

            ln_stats(lambda c: ar_ao[:, c * 1024:(c + 1) * 1024], DCH, HALF,
                     [6, 7], [2, 3], EPS_LN, 0.0, False,
                     ready_fn=lambda c: ("dve", ao_ev[c]))
            if t.get("_dbg") and b == 0 and Hf == 0:
                ctx.wait("sync", "dve", ao_ev[DCH - 1])
                dmp4 = ctx.dma("sync", t["dbg_ao2"], ar_ao[:], "sst")
                dmp4 = ctx.dma("sync", t["dbg_ao3"], ar_ao[:], "sst")
                ctx.wait("vector", "sst", dmp4)
            for dc in range(DCH):
                vop(lambda e, S, dc=dc:
                    e.tensor_tensor(tmp_b[:, 0:1024], ar_ao[:, dc * 1024:(dc + 1) * 1024],
                                    m_b[:, 0:1024], op=ALU.subtract))
                vop(lambda e, S:
                    e.tensor_tensor(tmp_b[:, 0:1024], tmp_b[:, 0:1024], r_b[:, 0:1024],
                                    op=ALU.mult))
                xu = vop(lambda e, S, dc=dc:
                         e.tensor_tensor(ar_x[:, dc * 1024:(dc + 1) * 1024],
                                         ar_x[:, dc * 1024:(dc + 1) * 1024],
                                         tmp_b[:, 0:1024], op=ALU.add),
                         inc="dve")
                ctx.wait("gpsimd", "dve", xu)
                x2_wr_last = ctx.dma(
                    "gpsimd", t["x2c"][b, dc * 128:(dc + 1) * 128, q0:q0 + HALF],
                    ar_x[:, dc * 1024:(dc + 1) * 1024], "sst")

        if b == 0:
            ffn0_mark.update({qn: len(ctx.q[qn]) for qn in ("vector", "sync")})
        for tc in (1, 0):
            tr0 = tc * TW
            x2r = [ar_ct[:, 0:1024], ar_ct[:, 1024:2048]]
            silu_sb = [ar_ct[:, 2048:2560], ar_ct[:, 2560:3072]]

            pe_epoch = ctx.at("pe")
            ctx.wait("sync", "pe", pe_epoch)
            ctx.wait("sync", "sst", 1)

            x2_tgt = [0] * DCH

            def x2_load(dc):
                ctx.dma(
                    "sync", x2r[dc % 2][:],
                    t["x2c"][b, dc * 128:(dc + 1) * 128, tr0:tr0 + TW],
                    ("sa", "sb")[dc % 2])
                x2_tgt[dc] = ("sa", "sb")[dc % 2]

            if tc == 1:
                dve_mark = ctx.at("dve")
                ln_stats(lambda c: ar_x[:, c * 1024:(c + 1) * 1024], DCH, TW,
                         [4, 5], [6, 7], EPS_LN, 0.0, False,
                         ready_fn=lambda c: ("dve", dve_mark))
            else:
                x2_load(0)
                x2_load(1)

                def ln2_loadfn(c, gate_pe, gate_sq):
                    ctx.wait("sync", "pe", gate_pe)
                    ctx.wait("sync", "act", gate_sq)
                    x2_load(c)

                ln_stats(lambda c: x2r[c % 2], DCH, TW, [4, 5], [6, 7],
                         EPS_LN, 0.0, False,
                         ready_fn=lambda c: (x2_tgt[c], 1),
                         load_fn=ln2_loadfn)

            h2 = ar_x
            h2_ev = [0] * DCH
            tmp_bf = tmp_b.bitcast(BF16)[:, 0:1024]
            if tc == 1:
                vop(lambda e, S: e.tensor_copy(m_bf_f[:], m_b[:, 0:1024]))
                vop(lambda e, S: e.tensor_copy(r_bf_f[:], r_b[:, 0:1024]))
                ctx.wait("vector", "sst", 1)
                for dc in range(DCH):
                    vop(lambda e, S, dc=dc:
                        e.tensor_tensor(tmp_bf,
                                        ar_x[:, dc * 1024:(dc + 1) * 1024],
                                        m_bf_f[:], op=ALU.subtract))
                    h2_ev[dc] = vop(lambda e, S, dc=dc:
                                    e.tensor_tensor(h2[:, dc * 1024:(dc + 1) * 1024],
                                                    tmp_bf, r_bf_f[:],
                                                    op=ALU.mult),
                                    inc="dve")
            else:
                ctx.wait("sync", "pe", ctx.at("pe"))
                ctx.wait("sync", "act", ctx.at("act"))
                x2_load(0)
                x2_load(1)
                vop(lambda e, S: e.tensor_copy(m_bf_f[:], m_b[:, 0:1024]))
                vop(lambda e, S: e.tensor_copy(r_bf_f[:], r_b[:, 0:1024]))
                for dc in range(DCH):
                    ctx.wait("vector", x2_tgt[dc], 1)
                    vop(lambda e, S, dc=dc:
                        e.tensor_tensor(tmp_bf, x2r[dc % 2][:], m_bf_f[:],
                                        op=ALU.subtract))
                    h2_ev[dc] = vop(lambda e, S, dc=dc:
                                    e.tensor_tensor(h2[:, dc * 1024:(dc + 1) * 1024],
                                                    tmp_bf, r_bf_f[:],
                                                    op=ALU.mult),
                                    inc="dve")
                    if dc + 2 < DCH:
                        ctx.wait("sync", "dve", h2_ev[dc])
                        x2_load(dc + 2)

            if t.get("_dbg") and b == 0 and tc == 0:
                ctx.wait("sync", "dve", h2_ev[DCH - 1])
                dmph = ctx.dma("sync", t["dbg_h2"], h2[:], "sst")
                ctx.wait("tensor", "sst", dmph)
            wgu_buf = [ar_h[:, 0:4096], ar_h[:, 4096:8192]]
            wgu_tgt = [0] * GP_N
            wgu_free = [0, 0]

            def wgu_load(gp, bi):
                tgt = 0
                for j in range(4):
                    src = t["cache_gu"][j * 512:(j + 1) * 512, gp * 256:(gp + 1) * 256]
                    src = src.rearrange("(c p) o -> p c o", p=128)
                    ctx.dma("sync", wgu_buf[bi][:, j * 1024:(j + 1) * 1024], src,
                            ("sa", "sb")[bi])
                wgu_tgt[gp] = ("sa", "sb")[bi]

            if b == 0:
                ctx.wait("sync", "so0", 1)
                ctx.wait("sync", "so1", 1)
            wgu_load(0, 0)
            wgu_load(1, 1)
            hint_ev = [0] * GP_N
            ctx.wait("tensor", "dve", h2_ev[DCH - 1])
            for gp in range(GP_N):
                gbank = [0 + 4 * (gp % 2), 1 + 4 * (gp % 2)]
                ubank = [2 + 4 * (gp % 2), 3 + 4 * (gp % 2)]
                wb = wgu_buf[gp % 2]
                ctx.wait("tensor", wgu_tgt[gp], 1)
                c_g = [0, 0]
                for tsub in range(2):
                    ctx.bank_wait(gbank[tsub])
                    for dc in range(DCH):
                        mm(pb[gbank[tsub]][:, 0:512], wb[:, dc * 256:dc * 256 + 128],
                           h2[:, dc * 1024 + tsub * 512:dc * 1024 + tsub * 512 + 512],
                           start=(dc == 0), stop=(dc == DCH - 1),
                           inc="pe" if dc == DCH - 1 else None)
                    c_g[tsub] = ctx.at("pe")
                    ctx.bank_wait(ubank[tsub])
                    for dc in range(DCH):
                        mm(pb[ubank[tsub]][:, 0:512], wb[:, dc * 256 + 128:dc * 256 + 256],
                           h2[:, dc * 1024 + tsub * 512:dc * 1024 + tsub * 512 + 512],
                           start=(dc == 0), stop=(dc == DCH - 1),
                           inc="pe" if dc == DCH - 1 else None)
                cu = ctx.at("pe")
                wgu_free[gp % 2] = cu
                if gp + 2 < GP_N:
                    ctx.wait("sync", "pe", cu)
                    wgu_load(gp + 2, gp % 2)
                if gp >= 1:
                    ctx.wait("scalar", "dve", hint_ev[gp - 1])
                for tsub in range(2):
                    ctx.wait("scalar", "pe", c_g[tsub])
                    sev = aop(lambda e, S, gb=pb[gbank[tsub]], o=silu_sb[tsub]:
                              e.activation(out=o[:], in_=gb[:, 0:512], func=AF.Silu,
                                           scale=S14),
                              inc="act")
                    ctx.bank_free(gbank[tsub], "act", sev)
                    ctx.wait("vector", "act", sev)
                    ctx.wait("vector", "pe", cu)
                    col = gp * 1024 + tsub * 512
                    if col < 12288:
                        hdst = ar_q[:, col:col + 512]
                    elif col < 12288 + 16384:
                        hdst = ar_o[:, col - 12288:col - 12288 + 512]
                    else:
                        hdst = ar_ao[:, col - 28672:col - 28672 + 512]
                    hint_ev[gp] = vop(
                        lambda e, S, ub=pb[ubank[tsub]], o=hdst, s2=silu_sb[tsub]:
                            e.scalar_tensor_tensor(o, ub[:, 0:512], S14, s2[:],
                                                   op0=ALU.mult, op1=ALU.mult),
                        inc="dve")
                    ctx.bank_free(ubank[tsub], "dve", hint_ev[gp])

            def hint_slice(ic, tsub):
                col = ic * 1024 + tsub * 512
                if col < 12288:
                    return ar_q[:, col:col + 512]
                elif col < 12288 + 16384:
                    return ar_o[:, col - 12288:col - 12288 + 512]
                return ar_ao[:, col - 28672:col - 28672 + 512]

            if t.get("_dbg") and b == 0 and tc == 0:
                ctx.wait("sync", "dve", hint_ev[GP_N - 1])
                dmpi = ctx.dma("sync", t["dbg_hint"], ar_q[:], "sst")
                ctx.wait("tensor", "sst", dmpi)
            wdn_buf = [ar_h[:, 0:5632], ar_h[:, 5632:11264]]
            wdn_tgt = [0] * DCH

            def wdn_load(hc, bi):
                tgt = 0
                for j in range(11):
                    src = t["cache_dn"][j * 512:(j + 1) * 512, hc * 128:(hc + 1) * 128]
                    src = src.rearrange("(c p) o -> p c o", p=128)
                    ctx.dma("sync", wdn_buf[bi][:, j * 512:(j + 1) * 512], src,
                            ("sa", "sb")[bi])
                wdn_tgt[hc] = ("sa", "sb")[bi]

            ctx.wait("sync", "pe", max(wgu_free))
            wdn_load(0, 0)
            wdn_load(1, 1)
            y_sb = ar_x
            y_ev = [0] * DCH
            wdn_free = [0, 0]
            ctx.wait("tensor", "dve", hint_ev[GP_N - 1])
            for hc in range(DCH):
                ctx.wait("tensor", wdn_tgt[hc], 1)
                for tsub in range(2):
                    bk = rot.acquire()
                    for ic in range(GP_N):
                        mm(pb[bk][:, 0:512], wdn_buf[hc % 2][:, ic * 128:(ic + 1) * 128],
                           hint_slice(ic, tsub),
                           start=(ic == 0), stop=(ic == GP_N - 1),
                           inc="pe" if ic == GP_N - 1 else None)
                    ctx.wait("vector", "pe", ctx.at("pe"))
                    yev = vop(lambda e, S, b2=pb[bk], hc=hc, tsub=tsub:
                              e.tensor_copy(
                                  y_sb[:, hc * 1024 + tsub * 512:hc * 1024 + tsub * 512 + 512],
                                  b2[:, 0:512]),
                              inc="dve")
                    ctx.bank_free(bk, "dve", yev)
                y_ev[hc] = yev
                wdn_free[hc % 2] = ctx.at("pe")
                if hc + 2 < DCH:
                    ctx.wait("sync", "pe", wdn_free[hc % 2])
                    wdn_load(hc + 2, hc % 2)

            if t.get("_dbg") and b == 0 and tc == 0:
                ctx.wait("sync", "dve", y_ev[DCH - 1])
                dmpy = ctx.dma("sync", t["dbg_y"], y_sb[:], "sst")
                ctx.wait("tensor", "sst", dmpy)
            ln_stats(lambda c: y_sb[:, c * 1024:(c + 1) * 1024], DCH, TW,
                     [4, 5], [6, 7],
                     EPS_LN * EPS_FF * (2.0 ** -28), EPS_LN, False,
                     ready_fn=lambda c: ("dve", y_ev[c]))

            ctx.wait("sync", "pe", ctx.at("pe"))
            ctx.wait("sync", "act", ctx.at("act"))
            x2_load(0)
            x2_load(1)
            stg = [ar_st[:, 0:512], ar_st[:, 512:1024]]
            st_free = [0, 0]
            vop(lambda e, S: e.tensor_copy(m_bf_f[:], m_b[:, 0:1024]))
            vop(lambda e, S: e.tensor_copy(r_bf_f[:], r_b[:, 0:1024]))
            for dc in range(DCH):
                ctx.wait("vector", x2_tgt[dc], 1)
                vop(lambda e, S, dc=dc:
                    e.tensor_tensor(tmp_bf, y_sb[:, dc * 1024:(dc + 1) * 1024],
                                    m_bf_f[:], op=ALU.subtract))
                vop(lambda e, S:
                    e.tensor_tensor(tmp_bf, tmp_bf, r_bf_f[:], op=ALU.mult))
                lastoev = 0
                for tsub in range(2):
                    bi = tsub
                    ctx.wait("vector", ("so0", "so1")[bi], 1)
                    oev2 = vop(lambda e, S, dc=dc, bi=bi, tsub=tsub:
                               e.tensor_tensor(stg[bi][:],
                                               x2r[dc % 2][:, tsub * 512:tsub * 512 + 512],
                                               tmp_bf[:, tsub * 512:tsub * 512 + 512],
                                               op=ALU.add),
                               inc="dve")
                    ctx.wait("sync", "dve", oev2)
                    st_free[bi] = ctx.dma(
                        "sync",
                        t["out"][b, dc * 128:(dc + 1) * 128,
                                 tr0 + tsub * 512:tr0 + tsub * 512 + 512],
                        stg[bi][:], ("so0", "so1")[bi])
                    lastoev = oev2
                ffn_dve_last = lastoev
                if dc + 2 < DCH:
                    ctx.wait("sync", "dve", lastoev)
                    x2_load(dc + 2)

    ctx.wait("sync", "sst", ctx.at("sst"))
    ctx.wait("sync", "so0", 1)
    ctx.wait("sync", "so1", 1)

    for qn in ("vector",):
        s, e = dq_range[qn]
        for attr in ("q", "meta"):
            q = getattr(ctx, attr)[qn]
            dq = q[s:e]
            rest = q[:s] + q[e:]
            a = b0_mark[qn] - (e - s)
            bnd = ffn0_mark[qn] - (e - s)
            mid = rest[a:bnd]
            n, m = len(mid), len(dq)
            out = []
            j = 0
            for i, op in enumerate(mid):
                while j < m and j * n <= i * m:
                    out.append(dq[j])
                    j += 1
                out.append(op)
            out.extend(dq[j:])
            getattr(ctx, attr)[qn] = rest[:a] + out + rest[bnd:]



def _pair_perm():
    perm = np.empty(2 * I_FF, dtype=np.int64)
    for g in range(GP_N):
        perm[g * 256:g * 256 + 128] = np.arange(g * 128, (g + 1) * 128)
        perm[g * 256 + 128:(g + 1) * 256] = np.arange(I_FF + g * 128, I_FF + (g + 1) * 128)
    return perm


def _host_prep(inputs):
    f32 = np.float32
    x = np.ascontiguousarray(
        np.asarray(inputs["L0_attn_hidden_states"], dtype=f32).transpose(0, 2, 1))
    wqdT = np.ascontiguousarray(np.asarray(inputs["L0_attn_Wq_down"], f32).T)
    wquT = np.ascontiguousarray(np.asarray(inputs["L0_attn_Wq_up"], f32).T)
    wkvT = np.ascontiguousarray(np.asarray(inputs["L0_attn_Wkv_down"], f32).T)
    kup = np.ascontiguousarray(
        np.asarray(inputs["L0_attn_k_up"], f32).transpose(1, 0, 2).reshape(RKV, NH * HD))
    vup = np.ascontiguousarray(
        np.asarray(inputs["L0_attn_v_up"], f32).transpose(1, 0, 2).reshape(RKV, NH * HD))
    woT = np.ascontiguousarray(np.asarray(inputs["L0_attn_Wo"], f32).T)

    perm = _pair_perm()
    gpk = np.asarray(inputs["L0_ffn_gate_up_packed"], np.int32).view(np.uint16)
    gpk = np.ascontiguousarray(np.repeat(gpk[perm].T, 4, axis=0))
    gsc = np.asarray(inputs["L0_ffn_gate_up_scales"], np.float16)
    gsc = np.ascontiguousarray(np.repeat(gsc[:, perm], 128, axis=0))
    dpk = np.asarray(inputs["L0_ffn_down_packed"], np.int32).view(np.uint16)
    dpk = np.ascontiguousarray(np.repeat(dpk.T, 4, axis=0))
    dsc = np.asarray(inputs["L0_ffn_down_scales"], np.float16)
    dsc = np.ascontiguousarray(np.repeat(dsc, 128, axis=0))

    ones = np.ones((128, 128), dtype=BF)
    shifts = np.broadcast_to(
        (4 * (np.arange(128) % 4)).astype(np.uint16).reshape(128, 1), (128, 512)).copy()

    bias = np.zeros((128, NH * NCB), dtype=f32)
    p = np.arange(128, dtype=np.float64)
    for h in range(NH):
        qw = QW[h]
        slope = SLOPES[h]
        center = (qw - 1) / 2.0
        for c in range(NCB):
            delta = c * 128 - (2048 - qw)
            if delta > qw - 128:
                bias[:, h * NCB + c] = -30000.0
            else:
                bias[:, h * NCB + c] = (slope * (delta + p - center)).astype(f32)

    return x, dict(wqdT=wqdT, wquT=wquT, wkvT=wkvT, kup=kup, vup=vup, woT=woT,
                   gu_pk=gpk, gu_sc=gsc, dn_pk=dpk, dn_sc=dsc,
                   c_ones=ones, c_shifts=shifts, c_bias=bias)


def kernel(**inputs):
    x, shared = _host_prep(inputs)
    if "nc" not in _CACHE:
        _CACHE["nc"] = build_nc()
    nc = _CACHE["nc"]
    in_maps = []
    for c in range(NCORES):
        m = dict(shared)
        m["x"] = np.ascontiguousarray(x[c * BC:(c + 1) * BC])
        in_maps.append(m)
    res = run_bass_kernel_spmd(nc, in_maps, core_ids=list(range(NCORES)))
    out = np.concatenate([res.results[c]["out"] for c in range(NCORES)], axis=0)
    return np.ascontiguousarray(out.transpose(0, 2, 1))


def simulate_deadlock(ctx):
    pos = {e: 0 for e in ENGINES}
    cnt = {}
    qs = ctx.meta
    progress = True
    while progress:
        progress = False
        for e in ENGINES:
            while pos[e] < len(qs[e]):
                m = qs[e][pos[e]]
                if m[0] == "w":
                    if cnt.get(m[1], 0) >= m[2]:
                        pos[e] += 1
                        progress = True
                    else:
                        break
                else:
                    if m[1] is not None:
                        cnt[m[1]] = cnt.get(m[1], 0) + m[2]
                    pos[e] += 1
                    progress = True
    stuck = {e: pos[e] for e in ENGINES if pos[e] < len(qs[e])}
    if not stuck:
        print("NO DEADLOCK: all queues drain")
        return
    print("DEADLOCK! queue positions:")
    for e, p in stuck.items():
        m = qs[e][p]
        print(f"  {e}: pos {p}/{len(qs[e])} waiting {m} (cnt={cnt.get(m[1],0)})")


# version 27
# speedup vs baseline: 1.1312x; 1.0588x over previous
import contextlib
import math

import numpy as np
import ml_dtypes

import concourse.bass as bass
import concourse.mybir as mybir
from concourse.bass_utils import run_bass_kernel_spmd

BF = ml_dtypes.bfloat16
F32 = mybir.dt.float32
BF16 = mybir.dt.bfloat16
FP16 = mybir.dt.float16
U16 = mybir.dt.uint16
AF = mybir.ActivationFunctionType
ALU = mybir.AluOpType

NCORES = 8
BC = 2
L = 2048
D = 2048
DCH = 16
RQ = 1536
RQC = 12
RKV = 512
RKC = 4
NH = 16
HD = 128
I_FF = 5632
GP_N = 44
TW = 1024
HALF = 1024
SCALE = 1.0 / math.sqrt(HD)
BASE = 2.0 ** (-0.5)
SLOPES = [BASE ** (i + 1) for i in range(NH)]
QW = [128, 128, 256, 256] + [512] * 12
NCB = 19
EPS_LN = 1e-5
EPS_FF = 1e-6
S14 = 16384.0

ENGINES = ("tensor", "vector", "scalar", "gpsimd", "sync")

_CACHE = {}


class Ctx:
    def __init__(self):
        self.q = {e: [] for e in ENGINES}
        self.meta = {e: [] for e in ENGINES}
        self.cnt = {}
        self.pg = {i: ("dve", 0) for i in range(8)}

    def newsem(self, name):
        self.cnt[name] = 0

    def at(self, name):
        return self.cnt[name]

    def wait(self, eng, sname, val):
        if val is None or val <= 0:
            return
        if sname in ("gld", "sld", "sst", "ga", "gb", "sa", "sb", "so0", "so1"):
            val = self.cnt[sname]
        if val <= 0:
            return
        def f(e, S, sname=sname, val=val):
            e.wait_ge(S[sname], val)
        self.q[eng].append(f)
        self.meta[eng].append(("w", sname, val))

    def op(self, eng, fn, inc=None):
        tgt = None
        if inc is not None:
            self.cnt[inc] += 1
            tgt = self.cnt[inc]
        def f(e, S, fn=fn, inc=inc):
            ins = fn(e, S)
            if inc is not None:
                ins.then_inc(S[inc], 1)
        self.q[eng].append(f)
        self.meta[eng].append(("i", inc, 1))
        return tgt

    def dma(self, queue, out, in_, sname):
        self.cnt[sname] += 16
        tgt = self.cnt[sname]
        def f(e, S, out=out, in_=in_, sname=sname):
            e.dma_start(out=out, in_=in_).then_inc(S[sname], 16)
        self.q[queue].append(f)
        self.meta[queue].append(("i", sname, 16))
        return tgt

    def bank_wait(self, i, eng="tensor"):
        sem, cnt = self.pg[i]
        self.wait(eng, sem, cnt)

    def bank_free(self, i, sem, cnt):
        self.pg[i] = (sem, cnt)


def _split_rows(n_rows, max_rows=512):
    out = []
    r = 0
    while r < n_rows:
        c = min(max_rows, n_rows - r)
        out.append((r, c))
        r += c
    return out


def build_nc(dbg_x2=False):
    nc = bass.Bass("TRN2", target_bir_lowering=False)

    t = {}
    t["x"] = nc.dram_tensor("x", [BC, D, L], F32, kind="ExternalInput").ap()
    t["wqdT"] = nc.dram_tensor("wqdT", [D, RQ], F32, kind="ExternalInput").ap()
    t["wquT"] = nc.dram_tensor("wquT", [RQ, D], F32, kind="ExternalInput").ap()
    t["wkvT"] = nc.dram_tensor("wkvT", [D, RKV], F32, kind="ExternalInput").ap()
    t["kup"] = nc.dram_tensor("kup", [RKV, D], F32, kind="ExternalInput").ap()
    t["vup"] = nc.dram_tensor("vup", [RKV, D], F32, kind="ExternalInput").ap()
    t["woT"] = nc.dram_tensor("woT", [D, D], F32, kind="ExternalInput").ap()
    t["gu_pk"] = nc.dram_tensor("gu_pk", [D, 2 * I_FF], U16, kind="ExternalInput").ap()
    t["gu_sc"] = nc.dram_tensor("gu_sc", [D, 2 * I_FF], FP16, kind="ExternalInput").ap()
    t["dn_pk"] = nc.dram_tensor("dn_pk", [I_FF, D], U16, kind="ExternalInput").ap()
    t["dn_sc"] = nc.dram_tensor("dn_sc", [I_FF, D], FP16, kind="ExternalInput").ap()
    t["c_ones"] = nc.dram_tensor("c_ones", [128, 128], BF16, kind="ExternalInput").ap()
    t["c_shifts"] = nc.dram_tensor("c_shifts", [128, 512], U16, kind="ExternalInput").ap()
    t["c_bias"] = nc.dram_tensor("c_bias", [128, NH * NCB], F32, kind="ExternalInput").ap()
    t["out"] = nc.dram_tensor("out", [BC, D, L], F32, kind="ExternalOutput").ap()
    t["cache_gu"] = nc.dram_tensor("cache_gu", [D, 2 * I_FF], BF16,
                                   kind="ExternalOutput" if dbg_x2 else "Internal").ap()
    t["cache_dn"] = nc.dram_tensor("cache_dn", [I_FF, D], BF16,
                                   kind="ExternalOutput" if dbg_x2 else "Internal").ap()
    t["x2c"] = nc.dram_tensor("x2c", [BC, D, L], BF16,
                          kind="ExternalOutput" if dbg_x2 else "Internal").ap()
    if dbg_x2:
        t["dbg_o"] = nc.dram_tensor("dbg_o", [128, 16384], BF16, kind="ExternalOutput").ap()
        t["dbg_attn"] = nc.dram_tensor("dbg_attn", [128, 512], BF16, kind="ExternalOutput").ap()
        t["dbg_den"] = nc.dram_tensor("dbg_den", [128, 512], F32, kind="ExternalOutput").ap()
        t["dbg_ax"] = nc.dram_tensor("dbg_ax", [128, 16384], BF16, kind="ExternalOutput").ap()
        t["dbg_ao3"] = nc.dram_tensor("dbg_ao3", [128, 16384], BF16, kind="ExternalOutput").ap()
        t["dbg_h2"] = nc.dram_tensor("dbg_h2", [128, 16384], BF16, kind="ExternalOutput").ap()
        t["dbg_hint"] = nc.dram_tensor("dbg_hint", [128, 12288], BF16, kind="ExternalOutput").ap()
        t["dbg_y"] = nc.dram_tensor("dbg_y", [128, 16384], BF16, kind="ExternalOutput").ap()
        t["dbg_ao2"] = nc.dram_tensor("dbg_ao2", [128, 16384], BF16, kind="ExternalOutput").ap()
    t["_dbg"] = dbg_x2

    ctx = Ctx()
    sem_names = ["pe", "act", "dve", "gp", "gld", "sld", "sst",
                 "ga", "gb", "sa", "sb", "so0", "so1", "dqv"]
    for s in sem_names:
        ctx.newsem(s)

    es = contextlib.ExitStack()
    with es:
        sb = {}
        sb["ar_x"] = es.enter_context(nc.sbuf_tensor("ar_x", [128, 16384], BF16))
        sb["ar_ct"] = es.enter_context(nc.sbuf_tensor("ar_ct", [128, 8192], BF16))
        sb["ar_q"] = es.enter_context(nc.sbuf_tensor("ar_q", [128, 12288], BF16))
        sb["ar_o"] = es.enter_context(nc.sbuf_tensor("ar_o", [128, 16384], BF16))
        sb["ar_ao"] = es.enter_context(nc.sbuf_tensor("ar_ao", [128, 16384], BF16))
        sb["ar_h"] = es.enter_context(nc.sbuf_tensor("ar_h", [128, 12288], BF16))
        sb["ar_w"] = es.enter_context(nc.sbuf_tensor("ar_w", [128, 6144], BF16))
        sb["ar_dq"] = es.enter_context(nc.sbuf_tensor("ar_dq", [128, 4096], BF16))
        sb["ar_s"] = es.enter_context(nc.sbuf_tensor("ar_s", [128, 4096], F32))
        sb["ar_st"] = es.enter_context(nc.sbuf_tensor("ar_st", [128, 1024], F32))
        sb["s_wsum"] = es.enter_context(nc.sbuf_tensor("s_wsum", [128, 16], F32))
        sb["s_ones"] = es.enter_context(nc.sbuf_tensor("s_ones", [128, 128], BF16))
        sb["s_shift"] = es.enter_context(nc.sbuf_tensor("s_shift", [128, 512], U16))
        sb["s_bias"] = es.enter_context(nc.sbuf_tensor("s_bias", [128, NH * NCB], F32))
        pb = [es.enter_context(nc.psum_tensor(f"pb{i}", [128, 512], F32))
              for i in range(8)]
        S = {s: es.enter_context(nc.semaphore(s)) for s in sem_names}

        emit_program(ctx, t, sb, pb)
        _CACHE["ctx"] = ctx

        with nc.Block() as block:
            @block.tensor
            def _(e):
                for f in ctx.q["tensor"]:
                    f(e, S)

            @block.vector
            def _(e):
                for f in ctx.q["vector"]:
                    f(e, S)

            @block.scalar
            def _(e):
                for f in ctx.q["scalar"]:
                    f(e, S)

            @block.gpsimd
            def _(e):
                for f in ctx.q["gpsimd"]:
                    f(e, S)

            @block.sync
            def _(e):
                for f in ctx.q["sync"]:
                    f(e, S)

    return nc


def emit_program(ctx, t, sb, pb):
    ar_x, ar_ct, ar_q, ar_o, ar_ao, ar_h, ar_w = (
        sb["ar_x"], sb["ar_ct"], sb["ar_q"], sb["ar_o"], sb["ar_ao"],
        sb["ar_h"], sb["ar_w"])
    ar_s, ar_st, ar_dq = sb["ar_s"], sb["ar_st"], sb["ar_dq"]
    s_wsum, s_ones, s_shift, s_bias = (
        sb["s_wsum"], sb["s_ones"], sb["s_shift"], sb["s_bias"])

    m_b = ar_s[:, 0:1024]
    r_b = ar_s[:, 1024:2048]
    m_bf = sb["ar_h"][:, 5632:6656]
    r_bf = sb["ar_h"][:, 6656:7680]
    m_bf_f = sb["ar_ct"][:, 3072:4096]
    r_bf_f = sb["ar_ct"][:, 4096:5120]
    tmp_b = ar_s[:, 2048:3072]
    aux_b = ar_s[:, 3072:4096]
    aux_bf = ar_s[:, 2048:4096].bitcast(BF16)

    dq_pk = [ar_dq[:, 0:512].bitcast(U16), ar_dq[:, 512:1024].bitcast(U16)]
    dq_sc = [ar_dq[:, 1024:1536].bitcast(FP16), ar_dq[:, 1536:2048].bitcast(FP16)]
    dq_t1 = ar_dq[:, 2048:2560].bitcast(U16)
    dq_lo = ar_dq[:, 2560:3072].bitcast(U16)
    dq_w = [ar_dq[:, 3072:3584], ar_dq[:, 3584:4096]]

    def mm(out_ap, lhsT, rhs, start, stop, inc=None):
        return ctx.op(
            "tensor",
            lambda e, S, o=out_ap, a=lhsT, b=rhs, st=start, sp=stop:
                e.matmul(o, a, b, start=st, stop=sp),
            inc=inc)

    def vop(fn, inc=None):
        return ctx.op("vector", fn, inc=inc)

    def pvop(par, fn, inc=False):
        eng = "vector" if par == 0 else "gpsimd"
        sem = "dve" if par == 0 else "gp"
        c = ctx.op(eng, fn, inc=sem if inc else None)
        return (sem, c)

    def aop(fn, inc=None):
        return ctx.op("scalar", fn, inc=inc)

    ctx.dma("sync", s_ones[:], t["c_ones"], "sa")
    ctx.dma("sync", s_shift[:], t["c_shifts"], "sa")
    ctx.dma("sync", s_bias[:], t["c_bias"], "sa")
    ctx.wait("vector", "sa", 1)
    ctx.wait("scalar", "sa", 1)
    ctx.wait("tensor", "sa", 1)
    dq_range = {"vector": None, "sync": None}
    dq_s = {qn: len(ctx.q[qn]) for qn in ("vector", "sync")}

    dq_specs = []
    for (pk_name, sc_name, c_name, n_dc, width) in [
        ("gu_pk", "gu_sc", "cache_gu", DCH, 2 * I_FF),
        ("dn_pk", "dn_sc", "cache_dn", GP_N, D),
    ]:
        for dc in range(n_dc):
            for ob in range(width // 512):
                dq_specs.append((pk_name, sc_name, c_name, dc, ob))
    n_dq = len(dq_specs)
    dq_ld = [0] * n_dq
    dq_cons = [0] * n_dq
    dq_st = [0] * n_dq

    def dq_load(i):
        pk_name, sc_name, c_name, dc, ob = dq_specs[i]
        bi = i % 2
        sem = ("sa", "sb")[bi]
        if i >= 2:
            ctx.wait("sync", "dqv", dq_cons[i - 2])
        ctx.dma("sync", dq_pk[bi][:],
                t[pk_name][dc * 128:(dc + 1) * 128, ob * 512:(ob + 1) * 512], sem)
        ctx.dma("sync", dq_sc[bi][:],
                t[sc_name][dc * 128:(dc + 1) * 128, ob * 512:(ob + 1) * 512], sem)
        dq_ld[i] = sem

    dq_load(0)
    dq_load(1)
    for i in range(n_dq):
        pk_name, sc_name, c_name, dc, ob = dq_specs[i]
        bi = i % 2
        ctx.wait("vector", dq_ld[i], 1)
        if i >= 2:
            ctx.wait("vector", dq_st[i - 2], 1)
        vop(lambda e, S, a=dq_pk[bi]:
            e.tensor_tensor(dq_t1[:], a[:], s_shift[:], op=ALU.logical_shift_right))
        vop(lambda e, S: e.tensor_scalar(dq_lo[:], dq_t1[:], 7, 9,
                                         op0=ALU.bitwise_and, op1=ALU.logical_shift_left))
        vop(lambda e, S, o=dq_pk[bi]: e.tensor_scalar(o[:], dq_t1[:], 8, 12,
                                                      op0=ALU.bitwise_and, op1=ALU.logical_shift_left))
        vop(lambda e, S, b2=dq_pk[bi]:
            e.tensor_tensor(dq_t1[:], dq_lo[:], b2[:], op=ALU.bitwise_or))
        dq_cons[i] = vop(
            lambda e, S, o=dq_w[bi], b2=dq_sc[bi]:
                e.tensor_tensor(o[:], dq_t1[:].bitcast(FP16), b2[:], op=ALU.mult),
            inc="dqv")
        ctx.wait("sync", "dqv", dq_cons[i])
        ctx.dma(
            "sync", t[c_name][dc * 128:(dc + 1) * 128, ob * 512:(ob + 1) * 512],
            dq_w[bi][:], ("so0", "so1")[bi])
        dq_st[i] = ("so0", "so1")[bi]
        if i + 2 < n_dq:
            dq_load(i + 2)
    dq_range["vector"] = (dq_s["vector"], len(ctx.q["vector"]))
    dq_range["sync"] = (dq_s["sync"], len(ctx.q["sync"]))
    b0_mark = {}
    ffn0_mark = {}

    class Rot:
        def __init__(self, banks):
            self.banks = banks
            self.i = 0

        def acquire(self):
            j = self.banks[self.i % len(self.banks)]
            self.i += 1
            ctx.bank_wait(j)
            return j

    rot = Rot([4, 5, 6, 7])

    def ln_stats(src_fn, n_ch, width, s1b, s2b, eps_const, extra_coef, negate,
                 ready_fn=None, load_fn=None):
        nts = width // 512
        for bk in s1b + s2b:
            ctx.bank_wait(bk)
        sq = [aux_bf[:, 0:width], aux_bf[:, width:2 * width]]
        c_sq = [0] * n_ch
        c_done = [0] * n_ch
        for c in range(n_ch):
            if ready_fn is not None:
                sem, cnt = ready_fn(c)
                ctx.wait("scalar", sem, cnt)
                ctx.wait("tensor", sem, cnt)
            if c >= 2:
                ctx.wait("scalar", "pe", c_done[c - 2])
            c_sq[c] = aop(lambda e, S, o=sq[c % 2], a=src_fn(c):
                          e.activation(out=o, in_=a, func=AF.Square), inc="act")
            for ts_ in range(nts):
                mm(pb[s1b[ts_]][:, 0:512], s_ones[:],
                   src_fn(c)[:, ts_ * 512:(ts_ + 1) * 512],
                   start=(c == 0), stop=(c == n_ch - 1))
            ctx.wait("tensor", "act", c_sq[c])
            for ts_ in range(nts):
                mm(pb[s2b[ts_]][:, 0:512], s_ones[:],
                   sq[c % 2][:, ts_ * 512:(ts_ + 1) * 512],
                   start=(c == 0), stop=(c == n_ch - 1),
                   inc="pe" if ts_ == nts - 1 else None)
            c_done[c] = ctx.at("pe")
            if load_fn is not None and c + 2 < n_ch:
                load_fn(c + 2, c_done[c], c_sq[c])
        done = ctx.at("pe")
        ctx.wait("vector", "pe", done)
        inv = 1.0 / D
        for ts_ in range(nts):
            sl = slice(ts_ * 512, (ts_ + 1) * 512)
            vop(lambda e, S, b2=pb[s1b[ts_]], sl=sl:
                e.tensor_scalar(m_b[:, sl], b2[:, 0:512], inv, None, op0=ALU.mult))
            vop(lambda e, S, b2=pb[s2b[ts_]], sl=sl:
                e.tensor_scalar(aux_b[:, sl], b2[:, 0:512], inv, None, op0=ALU.mult))
        w = width
        vop(lambda e, S: e.tensor_tensor(tmp_b[:, 0:w], m_b[:, 0:w], m_b[:, 0:w], op=ALU.mult))
        vop(lambda e, S: e.tensor_tensor(tmp_b[:, 0:w], aux_b[:, 0:w], tmp_b[:, 0:w], op=ALU.subtract))
        if extra_coef:
            vop(lambda e, S, k=extra_coef:
                e.scalar_tensor_tensor(tmp_b[:, 0:w], aux_b[:, 0:w], k, tmp_b[:, 0:w],
                                       op0=ALU.mult, op1=ALU.add))
        vop(lambda e, S, ec=eps_const:
            e.tensor_scalar(tmp_b[:, 0:w], tmp_b[:, 0:w], ec, None, op0=ALU.add))
        rc = vop(lambda e, S: e.reciprocal(tmp_b[:, 0:w], tmp_b[:, 0:w]), inc="dve")
        for bk in s1b + s2b:
            ctx.bank_free(bk, "dve", rc)
        ctx.wait("scalar", "dve", rc)
        sd = aop(lambda e, S: e.activation(out=r_b[:, 0:w], in_=tmp_b[:, 0:w], func=AF.Sqrt),
                 inc="act")
        ctx.wait("vector", "act", sd)
        if negate:
            vop(lambda e, S: e.tensor_scalar(r_b[:, 0:w], r_b[:, 0:w], -1.0, None, op0=ALU.mult))
        return rc

    def cast_cols(src2d, nrows, col0, ncols, dst, slot=0, split=512):
        sem = ("ga", "gb")[slot % 2]
        for (r, c) in _split_rows(nrows, split):
            src = src2d[r:r + c, col0:col0 + ncols]
            src = src.rearrange("(c p) j -> p c j", p=128)
            d0 = (r // 128) * ncols
            ctx.dma("gpsimd", dst[:, d0:d0 + (c // 128) * ncols], src, sem)
        return sem

    x2_wr_last = 0
    ffn_dve_last = 0

    prev_xst = [0] * DCH
    for b in range(BC):
        if b == 0:
            b0_mark.update({qn: len(ctx.q[qn]) for qn in ("vector", "sync")})
        for Hf in range(2):
            q0 = Hf * HALF
            KR = (Hf + 1) * HALF
            KT = KR // 128

            ctx.wait("gpsimd", "dve", ffn_dve_last)
            x_tgt = []
            for dc in range(DCH):
                if Hf == 1:
                    if isinstance(prev_xst[dc], tuple):
                        ctx.wait("gpsimd", *prev_xst[dc])
                    else:
                        ctx.wait("gpsimd", "dve", prev_xst[dc])
                else:
                    ctx.wait("gpsimd", "sst", 1)
                src = t["x"][b, dc * 128:(dc + 1) * 128, q0:q0 + HALF]
                ctx.dma("gpsimd", ar_x[:, dc * 1024:(dc + 1) * 1024], src,
                        ("ga", "gb")[dc % 2])
                x_tgt.append(("ga", "gb")[dc % 2])

            ln_stats(lambda c: ar_x[:, c * 1024:(c + 1) * 1024], DCH, HALF,
                     [4, 5], [6, 7], EPS_LN, 0.0, True,
                     ready_fn=lambda c: (x_tgt[c], 1))

            wqd_buf = [ar_w[:, 0:2048], ar_w[:, 2048:4096]]
            wqd_free = [0, 0]

            def proj(n_out_ch, wsrc, wsum_off, out_fn):
                wtgt = [0] * n_out_ch
                ctx.wait("gpsimd", "dve", wqd_free[0])
                ctx.wait("gpsimd", "sst", 1)
                wtgt[0] = cast_cols(t[wsrc], D, 0, 128, wqd_buf[0], slot=0)
                last_ev = 0
                for oc in range(n_out_ch):
                    if oc + 1 < n_out_ch:
                        ctx.wait("gpsimd", "dve", wqd_free[(oc + 1) % 2])
                        wtgt[oc + 1] = cast_cols(t[wsrc], D, (oc + 1) * 128, 128,
                                                 wqd_buf[(oc + 1) % 2], slot=oc + 1)
                    wb = wqd_buf[oc % 2]
                    ctx.wait("tensor", wtgt[oc], 1)
                    bk = rot.acquire()
                    for dc in range(DCH):
                        if oc == 0:
                            ctx.wait("tensor", x_tgt[dc], 1)
                        mm(pb[bk][:, 0:1], wb[:, dc * 128:(dc + 1) * 128], s_ones[:, 0:1],
                           start=(dc == 0), stop=(dc == DCH - 1),
                           inc="pe" if dc == DCH - 1 else None)
                    ctx.wait("vector", "pe", ctx.at("pe"))
                    wev = vop(lambda e, S, b2=pb[bk], oc=oc, off=wsum_off:
                              e.tensor_copy(s_wsum[:, off + oc:off + oc + 1], b2[:, 0:1]),
                              inc="dve")
                    ctx.bank_free(bk, "dve", wev)
                    for tsub in range(2):
                        bk = rot.acquire()
                        for dc in range(DCH):
                            mm(pb[bk][:, 0:512], wb[:, dc * 128:(dc + 1) * 128],
                               ar_x[:, dc * 1024 + tsub * 512:dc * 1024 + tsub * 512 + 512],
                               start=(dc == 0), stop=(dc == DCH - 1),
                               inc="pe" if dc == DCH - 1 else None)
                        ctx.wait("vector", "pe", ctx.at("pe"))
                        msl = slice(tsub * 512, tsub * 512 + 512)
                        vop(lambda e, S, b2=pb[bk], msl=msl, oc=oc, off=wsum_off:
                            e.scalar_tensor_tensor(tmp_b[:, 0:512], m_b[:, msl],
                                                   s_wsum[:, off + oc:off + oc + 1],
                                                   b2[:, 0:512],
                                                   op0=ALU.mult, op1=ALU.subtract))
                        ev = vop(lambda e, S, o=out_fn(oc, tsub), msl=msl:
                                 e.tensor_tensor(o, tmp_b[:, 0:512], r_b[:, msl],
                                                 op=ALU.mult),
                                 inc="dve")
                        ctx.bank_free(bk, "dve", ev)
                        last_ev = ev
                    wqd_free[oc % 2] = last_ev
                return last_ev

            qlow = ar_q
            proj(RQC, "wqdT", 0,
                 lambda oc, tsub: qlow[:, oc * 1024 + tsub * 512:oc * 1024 + tsub * 512 + 512])
            lastc = proj(RKC, "wkvT", 12,
                         lambda oc, tsub: ar_ct[:, oc * 2048 + q0 + tsub * 512:
                                                oc * 2048 + q0 + tsub * 512 + 512])

            vtokS = [ar_h[:, 0:4096], ar_ao[:, 0:4096]]
            kTS = [ar_h[:, 4096:6144], ar_ao[:, 4096:6144]]
            qTS = [ar_h[:, 6144:7168], ar_ao[:, 6144:7168]]
            attn = [ar_h[:, 7168:7680], ar_h[:, 7680:8192]]
            wqu_buf = [ar_h[:, 8192:9728], ar_h[:, 9728:11264]]
            kup_buf = [ar_w[:, 4096:4608], ar_w[:, 4608:5120]]
            vup_buf = [ar_w[:, 0:1024], ar_w[:, 1024:2048]]
            kup_tgt = [0] * NH
            wqu_tgt = [0] * NH
            vup_tgt = [0] * (NH // 2)
            kup_free = [0, 0]
            wqu_free = [0, 0]
            vup_free = [0, 0]
            chain_done = [0] * NH

            ctx.wait("gpsimd", "dve", lastc)
            ctx.wait("gpsimd", "pe", ctx.at("pe"))
            ctx.wait("gpsimd", "sst", 1)
            ctx.wait("tensor", "dve", lastc)

            def prefetch_w(h):
                pr = h // 2
                ctx.wait("gpsimd", "dve", kup_free[h % 2])
                kup_tgt[h] = cast_cols(t["kup"], RKV, h * 128, 128,
                                       kup_buf[h % 2], slot=h)
                ctx.wait("gpsimd", "dve", wqu_free[h % 2])
                wqu_tgt[h] = cast_cols(t["wquT"], RQ, h * 128, 128,
                                       wqu_buf[h % 2], slot=h)
                if h % 2 == 0:
                    ctx.wait("gpsimd", "dve", vup_free[pr % 2])
                    vup_tgt[pr] = cast_cols(t["vup"], RKV, pr * 256, 256,
                                            vup_buf[pr % 2], slot=pr)

            def chain_gen(h):
                pr = h // 2
                if h % 2 == 0:
                    ctx.wait("tensor", vup_tgt[pr], 1)
                    vt = vtokS[pr % 2]
                    for tt in range(KT):
                        bk = rot.acquire()
                        for rc in range(RKC):
                            mm(pb[bk][:, 0:256],
                               ar_ct[:, rc * 2048 + tt * 128:rc * 2048 + tt * 128 + 128],
                               vup_buf[pr % 2][:, rc * 256:(rc + 1) * 256],
                               start=(rc == 0), stop=(rc == RKC - 1),
                               inc="pe" if rc == RKC - 1 else None)
                        ctx.wait("vector", "pe", ctx.at("pe"))
                        vev = vop(lambda e, S, b2=pb[bk], tt=tt, vt=vt:
                                  e.tensor_copy(vt[:, tt * 256:(tt + 1) * 256], b2[:, 0:256]),
                                  inc="dve")
                        ctx.bank_free(bk, "dve", vev)
                        yield
                    vup_free[pr % 2] = vev
                ctx.wait("tensor", kup_tgt[h], 1)
                kt_ = kTS[h % 2]
                for ks in range(KR // 512):
                    bk = rot.acquire()
                    for rc in range(RKC):
                        mm(pb[bk][:, 0:512], kup_buf[h % 2][:, rc * 128:(rc + 1) * 128],
                           ar_ct[:, rc * 2048 + ks * 512:rc * 2048 + ks * 512 + 512],
                           start=(rc == 0), stop=(rc == RKC - 1),
                           inc="pe" if rc == RKC - 1 else None)
                    ctx.wait("vector", "pe", ctx.at("pe"))
                    kev = vop(lambda e, S, b2=pb[bk], ks=ks, kt_=kt_:
                              e.tensor_copy(kt_[:, ks * 512:(ks + 1) * 512], b2[:, 0:512]),
                              inc="dve")
                    ctx.bank_free(bk, "dve", kev)
                    yield
                kup_free[h % 2] = kev
                ctx.wait("tensor", wqu_tgt[h], 1)
                qt_ = qTS[h % 2]
                for tsub in range(2):
                    bk = rot.acquire()
                    for r1c in range(RQC):
                        mm(pb[bk][:, 0:512], wqu_buf[h % 2][:, r1c * 128:(r1c + 1) * 128],
                           qlow[:, r1c * 1024 + tsub * 512:r1c * 1024 + tsub * 512 + 512],
                           start=(r1c == 0), stop=(r1c == RQC - 1),
                           inc="pe" if r1c == RQC - 1 else None)
                    ctx.wait("vector", "pe", ctx.at("pe"))
                    qev = vop(lambda e, S, b2=pb[bk], tsub=tsub, qt_=qt_:
                              e.tensor_copy(qt_[:, tsub * 512:(tsub + 1) * 512], b2[:, 0:512]),
                              inc="dve")
                    ctx.bank_free(bk, "dve", qev)
                    yield
                wqu_free[h % 2] = qev
                chain_done[h] = qev

            prefetch_w(0)
            gen = chain_gen(0)
            for _ in gen:
                pass

            for h in range(NH):
                pr = h // 2
                nxt = None
                if h + 1 < NH:
                    prefetch_w(h + 1)
                    nxt = chain_gen(h + 1)
                vtok = vtokS[pr % 2]
                kT = kTS[h % 2]
                qT = qTS[h % 2]
                ctx.wait("tensor", "dve", chain_done[h])

                qw = QW[h]
                nsub = 512 // qw
                for qc in range(2):
                    q_off = q0 + qc * 512
                    kt_hi = q_off // 128 + 4
                    c_sc = [0] * kt_hi
                    c_ex = [0] * kt_hi
                    c_dn = [0] * kt_hi
                    c_mk = [0] * kt_hi
                    ctx.bank_wait(2)
                    ctx.bank_wait(3)
                    for kt in range(kt_hi + 1):
                        if kt < kt_hi:
                            ctx.bank_wait(kt % 2)
                            c_sc[kt] = mm(pb[kt % 2][:, 0:512], kT[:, kt * 128:(kt + 1) * 128],
                                          qT[:, qc * 512:qc * 512 + 512], True, True, inc="pe")
                            ctx.wait("scalar", "pe", c_sc[kt])
                            for sub in range(nsub):
                                qso = q_off + sub * qw
                                cidx = (kt * 128 - qso + 2048 - qw) // 128
                                bias_ap = s_bias[:, h * NCB + cidx:h * NCB + cidx + 1]
                                c_ex[kt] = aop(
                                    lambda e, S, kt=kt, sub=sub, qw=qw, ba=bias_ap:
                                        e.activation(out=attn[kt % 2][:, sub * qw:(sub + 1) * qw],
                                                     in_=pb[kt % 2][:, sub * qw:(sub + 1) * qw],
                                                     func=AF.Exp, bias=ba, scale=SCALE),
                                    inc="act")
                            ctx.bank_free(kt % 2, "act", c_ex[kt])
                            if kt * 128 + 128 > q_off:
                                ctx.wait("gpsimd", "act", c_ex[kt])
                                c_mk[kt] = ctx.op(
                                    "gpsimd",
                                    lambda e, S, kt=kt, q_off=q_off:
                                        e.affine_select(out=attn[kt % 2][:],
                                                        in_=attn[kt % 2][:],
                                                        pattern=[[1, 512]],
                                                        compare_op=ALU.is_ge, fill=0.0,
                                                        base=q_off - kt * 128,
                                                        channel_multiplier=-1),
                                    inc="gp")
                        if nxt is not None:
                            next(nxt, None)
                        if kt >= 1:
                            j = kt - 1
                            if c_mk[j]:
                                ctx.wait("tensor", "gp", c_mk[j])
                            else:
                                ctx.wait("tensor", "act", c_ex[j])
                            mm(pb[2][:, 0:512],
                               vtok[:, j * 256 + (h % 2) * 128:j * 256 + (h % 2) * 128 + 128],
                               attn[j % 2][:], start=(j == 0), stop=(j == kt_hi - 1))
                            c_dn[j] = mm(pb[3][:, 0:512], s_ones[:], attn[j % 2][:],
                                         start=(j == 0), stop=(j == kt_hi - 1), inc="pe")
                    ctx.wait("vector", "pe", c_dn[kt_hi - 1])
                    vop(lambda e, S: e.reciprocal(tmp_b[:, 0:512], pb[3][:, 0:512]))
                    ocol = h * 1024 + qc * 512
                    oev = vop(lambda e, S, ocol=ocol:
                              e.tensor_tensor(ar_o[:, ocol:ocol + 512], pb[2][:, 0:512],
                                              tmp_b[:, 0:512], op=ALU.mult),
                              inc="dve")
                    ctx.bank_free(2, "dve", oev)
                    ctx.bank_free(3, "dve", oev)
                if nxt is not None:
                    for _ in nxt:
                        pass
            all_o = oev
            if t.get("_dbg") and b == 0 and Hf == 0:
                ctx.wait("sync", "dve", all_o)
                dmp3 = ctx.dma("sync", t["dbg_o"], ar_o[:], "sst")
                dmp3 = ctx.dma("sync", t["dbg_ax"], ar_x[:], "sst")
                ctx.wait("tensor", "sst", dmp3)

            wo_buf = [ar_w[:, 2048:4096], ar_w[:, 4096:6144]]
            wo_tgt = [0] * DCH
            wo_free = [0, max(kup_free)]
            ao_ev = [0] * DCH
            ctx.wait("gpsimd", "dve", max(kup_free))
            wo_tgt[0] = cast_cols(t["woT"], D, 0, 128, wo_buf[0], slot=0)
            for fc in range(DCH):
                if fc + 1 < DCH:
                    ctx.wait("gpsimd", "dve", wo_free[(fc + 1) % 2])
                    wo_tgt[fc + 1] = cast_cols(t["woT"], D, (fc + 1) * 128, 128,
                                               wo_buf[(fc + 1) % 2], slot=fc + 1)
                ctx.wait("tensor", wo_tgt[fc], 1)
                if fc == 0:
                    ctx.wait("tensor", "dve", all_o)
                for tsub in range(2):
                    bk = rot.acquire()
                    for ec in range(DCH):
                        mm(pb[bk][:, 0:512], wo_buf[fc % 2][:, ec * 128:(ec + 1) * 128],
                           ar_o[:, ec * 1024 + tsub * 512:ec * 1024 + tsub * 512 + 512],
                           start=(ec == 0), stop=(ec == DCH - 1),
                           inc="pe" if ec == DCH - 1 else None)
                    ctx.wait("vector", "pe", ctx.at("pe"))
                    aev = vop(lambda e, S, b2=pb[bk], fc=fc, tsub=tsub:
                              e.tensor_copy(
                                  ar_ao[:, fc * 1024 + tsub * 512:fc * 1024 + tsub * 512 + 512],
                                  b2[:, 0:512]),
                              inc="dve")
                    ctx.bank_free(bk, "dve", aev)
                ao_ev[fc] = aev
                wo_free[fc % 2] = aev

            ln_stats(lambda c: ar_ao[:, c * 1024:(c + 1) * 1024], DCH, HALF,
                     [6, 7], [2, 3], EPS_LN, 0.0, False,
                     ready_fn=lambda c: ("dve", ao_ev[c]))
            if t.get("_dbg") and b == 0 and Hf == 0:
                ctx.wait("sync", "dve", ao_ev[DCH - 1])
                dmp4 = ctx.dma("sync", t["dbg_ao2"], ar_ao[:], "sst")
                dmp4 = ctx.dma("sync", t["dbg_ao3"], ar_ao[:], "sst")
                ctx.wait("vector", "sst", dmp4)
            for dc in range(DCH):
                vop(lambda e, S, dc=dc:
                    e.tensor_tensor(tmp_b[:, 0:1024], ar_ao[:, dc * 1024:(dc + 1) * 1024],
                                    m_b[:, 0:1024], op=ALU.subtract))
                vop(lambda e, S:
                    e.tensor_tensor(tmp_b[:, 0:1024], tmp_b[:, 0:1024], r_b[:, 0:1024],
                                    op=ALU.mult))
                xu = vop(lambda e, S, dc=dc:
                         e.tensor_tensor(ar_x[:, dc * 1024:(dc + 1) * 1024],
                                         ar_x[:, dc * 1024:(dc + 1) * 1024],
                                         tmp_b[:, 0:1024], op=ALU.add),
                         inc="dve")
                ctx.wait("gpsimd", "dve", xu)
                x2_wr_last = ctx.dma(
                    "gpsimd", t["x2c"][b, dc * 128:(dc + 1) * 128, q0:q0 + HALF],
                    ar_x[:, dc * 1024:(dc + 1) * 1024], "sst")

        if b == 0:
            ffn0_mark.update({qn: len(ctx.q[qn]) for qn in ("vector", "sync")})
        for tc in (1, 0):
            tr0 = tc * TW
            x2r = [ar_ct[:, 0:1024], ar_ct[:, 1024:2048]]
            silu_sb = [ar_ct[:, 2048:2560], ar_ct[:, 2560:3072]]

            pe_epoch = ctx.at("pe")
            ctx.wait("sync", "pe", pe_epoch)
            ctx.wait("sync", "sst", 1)

            x2_tgt = [0] * DCH

            def x2_load(dc):
                ctx.dma(
                    "sync", x2r[dc % 2][:],
                    t["x2c"][b, dc * 128:(dc + 1) * 128, tr0:tr0 + TW],
                    ("sa", "sb")[dc % 2])
                x2_tgt[dc] = ("sa", "sb")[dc % 2]

            if tc == 1:
                dve_mark = ctx.at("dve")
                ctx.wait("tensor", "gp", ctx.at("gp"))
                ctx.wait("scalar", "gp", ctx.at("gp"))
                ln_stats(lambda c: ar_x[:, c * 1024:(c + 1) * 1024], DCH, TW,
                         [4, 5], [6, 7], EPS_LN, 0.0, False,
                         ready_fn=lambda c: ("dve", dve_mark))
            else:
                x2_load(0)
                x2_load(1)

                def ln2_loadfn(c, gate_pe, gate_sq):
                    ctx.wait("sync", "pe", gate_pe)
                    ctx.wait("sync", "act", gate_sq)
                    x2_load(c)

                ln_stats(lambda c: x2r[c % 2], DCH, TW, [4, 5], [6, 7],
                         EPS_LN, 0.0, False,
                         ready_fn=lambda c: (x2_tgt[c], 1),
                         load_fn=ln2_loadfn)

            h2 = ar_x
            h2_ev = [0] * DCH
            tmp_bf = tmp_b.bitcast(BF16)[:, 0:1024]
            if tc == 1:
                vop(lambda e, S: e.tensor_copy(m_bf_f[:], m_b[:, 0:1024]))
                vop(lambda e, S: e.tensor_copy(r_bf_f[:], r_b[:, 0:1024]))
                ctx.wait("vector", "sst", 1)
                for dc in range(DCH):
                    vop(lambda e, S, dc=dc:
                        e.tensor_tensor(tmp_bf,
                                        ar_x[:, dc * 1024:(dc + 1) * 1024],
                                        m_bf_f[:], op=ALU.subtract))
                    h2_ev[dc] = vop(lambda e, S, dc=dc:
                                    e.tensor_tensor(h2[:, dc * 1024:(dc + 1) * 1024],
                                                    tmp_bf, r_bf_f[:],
                                                    op=ALU.mult),
                                    inc="dve")
            else:
                ctx.wait("sync", "pe", ctx.at("pe"))
                ctx.wait("sync", "act", ctx.at("act"))
                x2_load(0)
                x2_load(1)
                vop(lambda e, S: e.tensor_copy(m_bf_f[:], m_b[:, 0:1024]))
                vop(lambda e, S: e.tensor_copy(r_bf_f[:], r_b[:, 0:1024]))
                for dc in range(DCH):
                    ctx.wait("vector", x2_tgt[dc], 1)
                    vop(lambda e, S, dc=dc:
                        e.tensor_tensor(tmp_bf, x2r[dc % 2][:], m_bf_f[:],
                                        op=ALU.subtract))
                    h2_ev[dc] = vop(lambda e, S, dc=dc:
                                    e.tensor_tensor(h2[:, dc * 1024:(dc + 1) * 1024],
                                                    tmp_bf, r_bf_f[:],
                                                    op=ALU.mult),
                                    inc="dve")
                    if dc + 2 < DCH:
                        ctx.wait("sync", "dve", h2_ev[dc])
                        x2_load(dc + 2)

            if t.get("_dbg") and b == 0 and tc == 0:
                ctx.wait("sync", "dve", h2_ev[DCH - 1])
                dmph = ctx.dma("sync", t["dbg_h2"], h2[:], "sst")
                ctx.wait("tensor", "sst", dmph)
            wgu_buf = [ar_h[:, 0:4096], ar_h[:, 4096:8192]]
            wgu_tgt = [0] * GP_N
            wgu_free = [0, 0]

            def wgu_load(gp, bi):
                tgt = 0
                for j in range(4):
                    src = t["cache_gu"][j * 512:(j + 1) * 512, gp * 256:(gp + 1) * 256]
                    src = src.rearrange("(c p) o -> p c o", p=128)
                    ctx.dma("sync", wgu_buf[bi][:, j * 1024:(j + 1) * 1024], src,
                            ("sa", "sb")[bi])
                wgu_tgt[gp] = ("sa", "sb")[bi]

            if b == 0:
                ctx.wait("sync", "so0", 1)
                ctx.wait("sync", "so1", 1)
            wgu_load(0, 0)
            wgu_load(1, 1)
            hint_ev = [0] * GP_N
            ctx.wait("tensor", "dve", h2_ev[DCH - 1])
            for gp in range(GP_N):
                gbank = [0 + 4 * (gp % 2), 1 + 4 * (gp % 2)]
                ubank = [2 + 4 * (gp % 2), 3 + 4 * (gp % 2)]
                wb = wgu_buf[gp % 2]
                ctx.wait("tensor", wgu_tgt[gp], 1)
                c_g = [0, 0]
                for tsub in range(2):
                    ctx.bank_wait(gbank[tsub])
                    for dc in range(DCH):
                        mm(pb[gbank[tsub]][:, 0:512], wb[:, dc * 256:dc * 256 + 128],
                           h2[:, dc * 1024 + tsub * 512:dc * 1024 + tsub * 512 + 512],
                           start=(dc == 0), stop=(dc == DCH - 1),
                           inc="pe" if dc == DCH - 1 else None)
                    c_g[tsub] = ctx.at("pe")
                    ctx.bank_wait(ubank[tsub])
                    for dc in range(DCH):
                        mm(pb[ubank[tsub]][:, 0:512], wb[:, dc * 256 + 128:dc * 256 + 256],
                           h2[:, dc * 1024 + tsub * 512:dc * 1024 + tsub * 512 + 512],
                           start=(dc == 0), stop=(dc == DCH - 1),
                           inc="pe" if dc == DCH - 1 else None)
                cu = ctx.at("pe")
                wgu_free[gp % 2] = cu
                if gp + 2 < GP_N:
                    ctx.wait("sync", "pe", cu)
                    wgu_load(gp + 2, gp % 2)
                if gp >= 1:
                    ctx.wait("scalar", "dve", hint_ev[gp - 1])
                for tsub in range(2):
                    ctx.wait("scalar", "pe", c_g[tsub])
                    sev = aop(lambda e, S, gb=pb[gbank[tsub]], o=silu_sb[tsub]:
                              e.activation(out=o[:], in_=gb[:, 0:512], func=AF.Silu,
                                           scale=S14),
                              inc="act")
                    ctx.bank_free(gbank[tsub], "act", sev)
                    ctx.wait("vector", "act", sev)
                    ctx.wait("vector", "pe", cu)
                    col = gp * 1024 + tsub * 512
                    if col < 12288:
                        hdst = ar_q[:, col:col + 512]
                    elif col < 12288 + 16384:
                        hdst = ar_o[:, col - 12288:col - 12288 + 512]
                    else:
                        hdst = ar_ao[:, col - 28672:col - 28672 + 512]
                    hint_ev[gp] = vop(
                        lambda e, S, ub=pb[ubank[tsub]], o=hdst, s2=silu_sb[tsub]:
                            e.scalar_tensor_tensor(o, ub[:, 0:512], S14, s2[:],
                                                   op0=ALU.mult, op1=ALU.mult),
                        inc="dve")
                    ctx.bank_free(ubank[tsub], "dve", hint_ev[gp])

            def hint_slice(ic, tsub):
                col = ic * 1024 + tsub * 512
                if col < 12288:
                    return ar_q[:, col:col + 512]
                elif col < 12288 + 16384:
                    return ar_o[:, col - 12288:col - 12288 + 512]
                return ar_ao[:, col - 28672:col - 28672 + 512]

            if t.get("_dbg") and b == 0 and tc == 0:
                ctx.wait("sync", "dve", hint_ev[GP_N - 1])
                dmpi = ctx.dma("sync", t["dbg_hint"], ar_q[:], "sst")
                ctx.wait("tensor", "sst", dmpi)
            wdn_buf = [ar_h[:, 0:5632], ar_h[:, 5632:11264]]
            wdn_tgt = [0] * DCH

            def wdn_load(hc, bi):
                tgt = 0
                for j in range(11):
                    src = t["cache_dn"][j * 512:(j + 1) * 512, hc * 128:(hc + 1) * 128]
                    src = src.rearrange("(c p) o -> p c o", p=128)
                    ctx.dma("sync", wdn_buf[bi][:, j * 512:(j + 1) * 512], src,
                            ("sa", "sb")[bi])
                wdn_tgt[hc] = ("sa", "sb")[bi]

            ctx.wait("sync", "pe", max(wgu_free))
            wdn_load(0, 0)
            wdn_load(1, 1)
            y_sb = ar_x
            y_ev = [0] * DCH
            wdn_free = [0, 0]
            ctx.wait("tensor", "dve", hint_ev[GP_N - 1])
            for hc in range(DCH):
                ctx.wait("tensor", wdn_tgt[hc], 1)
                for tsub in range(2):
                    bk = rot.acquire()
                    for ic in range(GP_N):
                        mm(pb[bk][:, 0:512], wdn_buf[hc % 2][:, ic * 128:(ic + 1) * 128],
                           hint_slice(ic, tsub),
                           start=(ic == 0), stop=(ic == GP_N - 1),
                           inc="pe" if ic == GP_N - 1 else None)
                    ctx.wait("vector", "pe", ctx.at("pe"))
                    yev = vop(lambda e, S, b2=pb[bk], hc=hc, tsub=tsub:
                              e.tensor_copy(
                                  y_sb[:, hc * 1024 + tsub * 512:hc * 1024 + tsub * 512 + 512],
                                  b2[:, 0:512]),
                              inc="dve")
                    ctx.bank_free(bk, "dve", yev)
                y_ev[hc] = yev
                wdn_free[hc % 2] = ctx.at("pe")
                if hc + 2 < DCH:
                    ctx.wait("sync", "pe", wdn_free[hc % 2])
                    wdn_load(hc + 2, hc % 2)

            if t.get("_dbg") and b == 0 and tc == 0:
                ctx.wait("sync", "dve", y_ev[DCH - 1])
                dmpy = ctx.dma("sync", t["dbg_y"], y_sb[:], "sst")
                ctx.wait("tensor", "sst", dmpy)
            ln_stats(lambda c: y_sb[:, c * 1024:(c + 1) * 1024], DCH, TW,
                     [4, 5], [6, 7],
                     EPS_LN * EPS_FF * (2.0 ** -28), EPS_LN, False,
                     ready_fn=lambda c: ("dve", y_ev[c]))

            ctx.wait("sync", "pe", ctx.at("pe"))
            ctx.wait("sync", "act", ctx.at("act"))
            x2_load(0)
            x2_load(1)
            stg = [ar_st[:, 0:512], ar_st[:, 512:1024]]
            st_free = [0, 0]
            vop(lambda e, S: e.tensor_copy(m_bf_f[:], m_b[:, 0:1024]))
            vop(lambda e, S: e.tensor_copy(r_bf_f[:], r_b[:, 0:1024]))
            for dc in range(DCH):
                ctx.wait("vector", x2_tgt[dc], 1)
                vop(lambda e, S, dc=dc:
                    e.tensor_tensor(tmp_bf, y_sb[:, dc * 1024:(dc + 1) * 1024],
                                    m_bf_f[:], op=ALU.subtract))
                vop(lambda e, S:
                    e.tensor_tensor(tmp_bf, tmp_bf, r_bf_f[:], op=ALU.mult))
                lastoev = 0
                for tsub in range(2):
                    bi = tsub
                    ctx.wait("vector", ("so0", "so1")[bi], 1)
                    oev2 = vop(lambda e, S, dc=dc, bi=bi, tsub=tsub:
                               e.tensor_tensor(stg[bi][:],
                                               x2r[dc % 2][:, tsub * 512:tsub * 512 + 512],
                                               tmp_bf[:, tsub * 512:tsub * 512 + 512],
                                               op=ALU.add),
                               inc="dve")
                    ctx.wait("sync", "dve", oev2)
                    st_free[bi] = ctx.dma(
                        "sync",
                        t["out"][b, dc * 128:(dc + 1) * 128,
                                 tr0 + tsub * 512:tr0 + tsub * 512 + 512],
                        stg[bi][:], ("so0", "so1")[bi])
                    lastoev = oev2
                ffn_dve_last = lastoev
                if dc + 2 < DCH:
                    ctx.wait("sync", "dve", lastoev)
                    x2_load(dc + 2)

    ctx.wait("sync", "sst", ctx.at("sst"))
    ctx.wait("sync", "so0", 1)
    ctx.wait("sync", "so1", 1)

    for qn in ("vector",):
        s, e = dq_range[qn]
        for attr in ("q", "meta"):
            q = getattr(ctx, attr)[qn]
            dq = q[s:e]
            rest = q[:s] + q[e:]
            a = b0_mark[qn] - (e - s)
            bnd = ffn0_mark[qn] - (e - s)
            mid = rest[a:bnd]
            n, m = len(mid), len(dq)
            out = []
            j = 0
            for i, op in enumerate(mid):
                while j < m and j * n <= i * m:
                    out.append(dq[j])
                    j += 1
                out.append(op)
            out.extend(dq[j:])
            getattr(ctx, attr)[qn] = rest[:a] + out + rest[bnd:]



def _pair_perm():
    perm = np.empty(2 * I_FF, dtype=np.int64)
    for g in range(GP_N):
        perm[g * 256:g * 256 + 128] = np.arange(g * 128, (g + 1) * 128)
        perm[g * 256 + 128:(g + 1) * 256] = np.arange(I_FF + g * 128, I_FF + (g + 1) * 128)
    return perm


def _host_prep(inputs):
    f32 = np.float32
    x = np.ascontiguousarray(
        np.asarray(inputs["L0_attn_hidden_states"], dtype=f32).transpose(0, 2, 1))
    wqdT = np.ascontiguousarray(np.asarray(inputs["L0_attn_Wq_down"], f32).T)
    wquT = np.ascontiguousarray(np.asarray(inputs["L0_attn_Wq_up"], f32).T)
    wkvT = np.ascontiguousarray(np.asarray(inputs["L0_attn_Wkv_down"], f32).T)
    kup = np.ascontiguousarray(
        np.asarray(inputs["L0_attn_k_up"], f32).transpose(1, 0, 2).reshape(RKV, NH * HD))
    vup = np.ascontiguousarray(
        np.asarray(inputs["L0_attn_v_up"], f32).transpose(1, 0, 2).reshape(RKV, NH * HD))
    woT = np.ascontiguousarray(np.asarray(inputs["L0_attn_Wo"], f32).T)

    perm = _pair_perm()
    gpk = np.asarray(inputs["L0_ffn_gate_up_packed"], np.int32).view(np.uint16)
    gpk = np.ascontiguousarray(np.repeat(gpk[perm].T, 4, axis=0))
    gsc = np.asarray(inputs["L0_ffn_gate_up_scales"], np.float16)
    gsc = np.ascontiguousarray(np.repeat(gsc[:, perm], 128, axis=0))
    dpk = np.asarray(inputs["L0_ffn_down_packed"], np.int32).view(np.uint16)
    dpk = np.ascontiguousarray(np.repeat(dpk.T, 4, axis=0))
    dsc = np.asarray(inputs["L0_ffn_down_scales"], np.float16)
    dsc = np.ascontiguousarray(np.repeat(dsc, 128, axis=0))

    ones = np.ones((128, 128), dtype=BF)
    shifts = np.broadcast_to(
        (4 * (np.arange(128) % 4)).astype(np.uint16).reshape(128, 1), (128, 512)).copy()

    bias = np.zeros((128, NH * NCB), dtype=f32)
    p = np.arange(128, dtype=np.float64)
    for h in range(NH):
        qw = QW[h]
        slope = SLOPES[h]
        center = (qw - 1) / 2.0
        for c in range(NCB):
            delta = c * 128 - (2048 - qw)
            if delta > qw - 128:
                bias[:, h * NCB + c] = -30000.0
            else:
                bias[:, h * NCB + c] = (slope * (delta + p - center)).astype(f32)

    return x, dict(wqdT=wqdT, wquT=wquT, wkvT=wkvT, kup=kup, vup=vup, woT=woT,
                   gu_pk=gpk, gu_sc=gsc, dn_pk=dpk, dn_sc=dsc,
                   c_ones=ones, c_shifts=shifts, c_bias=bias)


def kernel(**inputs):
    x, shared = _host_prep(inputs)
    if "nc" not in _CACHE:
        _CACHE["nc"] = build_nc()
    nc = _CACHE["nc"]
    in_maps = []
    for c in range(NCORES):
        m = dict(shared)
        m["x"] = np.ascontiguousarray(x[c * BC:(c + 1) * BC])
        in_maps.append(m)
    res = run_bass_kernel_spmd(nc, in_maps, core_ids=list(range(NCORES)))
    out = np.concatenate([res.results[c]["out"] for c in range(NCORES)], axis=0)
    return np.ascontiguousarray(out.transpose(0, 2, 1))


def simulate_deadlock(ctx):
    pos = {e: 0 for e in ENGINES}
    cnt = {}
    qs = ctx.meta
    progress = True
    while progress:
        progress = False
        for e in ENGINES:
            while pos[e] < len(qs[e]):
                m = qs[e][pos[e]]
                if m[0] == "w":
                    if cnt.get(m[1], 0) >= m[2]:
                        pos[e] += 1
                        progress = True
                    else:
                        break
                else:
                    if m[1] is not None:
                        cnt[m[1]] = cnt.get(m[1], 0) + m[2]
                    pos[e] += 1
                    progress = True
    stuck = {e: pos[e] for e in ENGINES if pos[e] < len(qs[e])}
    if not stuck:
        print("NO DEADLOCK: all queues drain")
        return
    print("DEADLOCK! queue positions:")
    for e, p in stuck.items():
        m = qs[e][p]
        print(f"  {e}: pos {p}/{len(qs[e])} waiting {m} (cnt={cnt.get(m[1],0)})")


# version 28
# speedup vs baseline: 1.1481x; 1.0150x over previous
import contextlib
import math

import numpy as np
import ml_dtypes

import concourse.bass as bass
import concourse.mybir as mybir
from concourse.bass_utils import run_bass_kernel_spmd

BF = ml_dtypes.bfloat16
F32 = mybir.dt.float32
BF16 = mybir.dt.bfloat16
FP16 = mybir.dt.float16
U16 = mybir.dt.uint16
AF = mybir.ActivationFunctionType
ALU = mybir.AluOpType

NCORES = 8
BC = 2
L = 2048
D = 2048
DCH = 16
RQ = 1536
RQC = 12
RKV = 512
RKC = 4
NH = 16
HD = 128
I_FF = 5632
GP_N = 44
TW = 1024
HALF = 1024
SCALE = 1.0 / math.sqrt(HD)
BASE = 2.0 ** (-0.5)
SLOPES = [BASE ** (i + 1) for i in range(NH)]
QW = [128, 128, 256, 256] + [512] * 12
NCB = 19
CUT = 90.0
EPS_LN = 1e-5


def kt_low(h, q_off):
    import math as _m
    lo = int(_m.floor((q_off - 127 - CUT / SLOPES[h]) / 128))
    return max(0, lo)
EPS_FF = 1e-6
S14 = 16384.0

ENGINES = ("tensor", "vector", "scalar", "gpsimd", "sync")

_CACHE = {}


class Ctx:
    def __init__(self):
        self.q = {e: [] for e in ENGINES}
        self.meta = {e: [] for e in ENGINES}
        self.cnt = {}
        self.pg = {i: ("dve", 0) for i in range(8)}

    def newsem(self, name):
        self.cnt[name] = 0

    def at(self, name):
        return self.cnt[name]

    def wait(self, eng, sname, val):
        if val is None or val <= 0:
            return
        if sname in ("gld", "sld", "sst", "ga", "gb", "sa", "sb", "so0", "so1"):
            val = self.cnt[sname]
        if val <= 0:
            return
        def f(e, S, sname=sname, val=val):
            e.wait_ge(S[sname], val)
        self.q[eng].append(f)
        self.meta[eng].append(("w", sname, val))

    def op(self, eng, fn, inc=None):
        tgt = None
        if inc is not None:
            self.cnt[inc] += 1
            tgt = self.cnt[inc]
        def f(e, S, fn=fn, inc=inc):
            ins = fn(e, S)
            if inc is not None:
                ins.then_inc(S[inc], 1)
        self.q[eng].append(f)
        self.meta[eng].append(("i", inc, 1))
        return tgt

    def dma(self, queue, out, in_, sname):
        self.cnt[sname] += 16
        tgt = self.cnt[sname]
        def f(e, S, out=out, in_=in_, sname=sname):
            e.dma_start(out=out, in_=in_).then_inc(S[sname], 16)
        self.q[queue].append(f)
        self.meta[queue].append(("i", sname, 16))
        return tgt

    def bank_wait(self, i, eng="tensor"):
        sem, cnt = self.pg[i]
        self.wait(eng, sem, cnt)

    def bank_free(self, i, sem, cnt):
        self.pg[i] = (sem, cnt)


def _split_rows(n_rows, max_rows=512):
    out = []
    r = 0
    while r < n_rows:
        c = min(max_rows, n_rows - r)
        out.append((r, c))
        r += c
    return out


def build_nc(dbg_x2=False):
    nc = bass.Bass("TRN2", target_bir_lowering=False)

    t = {}
    t["x"] = nc.dram_tensor("x", [BC, D, L], F32, kind="ExternalInput").ap()
    t["wqdT"] = nc.dram_tensor("wqdT", [D, RQ], F32, kind="ExternalInput").ap()
    t["wquT"] = nc.dram_tensor("wquT", [RQ, D], F32, kind="ExternalInput").ap()
    t["wkvT"] = nc.dram_tensor("wkvT", [D, RKV], F32, kind="ExternalInput").ap()
    t["kup"] = nc.dram_tensor("kup", [RKV, D], F32, kind="ExternalInput").ap()
    t["vup"] = nc.dram_tensor("vup", [RKV, D], F32, kind="ExternalInput").ap()
    t["woT"] = nc.dram_tensor("woT", [D, D], F32, kind="ExternalInput").ap()
    t["gu_pk"] = nc.dram_tensor("gu_pk", [D, 2 * I_FF], U16, kind="ExternalInput").ap()
    t["gu_sc"] = nc.dram_tensor("gu_sc", [D, 2 * I_FF], FP16, kind="ExternalInput").ap()
    t["dn_pk"] = nc.dram_tensor("dn_pk", [I_FF, D], U16, kind="ExternalInput").ap()
    t["dn_sc"] = nc.dram_tensor("dn_sc", [I_FF, D], FP16, kind="ExternalInput").ap()
    t["c_ones"] = nc.dram_tensor("c_ones", [128, 128], BF16, kind="ExternalInput").ap()
    t["c_shifts"] = nc.dram_tensor("c_shifts", [128, 512], U16, kind="ExternalInput").ap()
    t["c_bias"] = nc.dram_tensor("c_bias", [128, NH * NCB], F32, kind="ExternalInput").ap()
    t["out"] = nc.dram_tensor("out", [BC, D, L], F32, kind="ExternalOutput").ap()
    t["cache_gu"] = nc.dram_tensor("cache_gu", [D, 2 * I_FF], BF16,
                                   kind="ExternalOutput" if dbg_x2 else "Internal").ap()
    t["cache_dn"] = nc.dram_tensor("cache_dn", [I_FF, D], BF16,
                                   kind="ExternalOutput" if dbg_x2 else "Internal").ap()
    t["x2c"] = nc.dram_tensor("x2c", [BC, D, L], BF16,
                          kind="ExternalOutput" if dbg_x2 else "Internal").ap()
    if dbg_x2:
        t["dbg_o"] = nc.dram_tensor("dbg_o", [128, 16384], BF16, kind="ExternalOutput").ap()
        t["dbg_attn"] = nc.dram_tensor("dbg_attn", [128, 512], BF16, kind="ExternalOutput").ap()
        t["dbg_den"] = nc.dram_tensor("dbg_den", [128, 512], F32, kind="ExternalOutput").ap()
        t["dbg_ax"] = nc.dram_tensor("dbg_ax", [128, 16384], BF16, kind="ExternalOutput").ap()
        t["dbg_ao3"] = nc.dram_tensor("dbg_ao3", [128, 16384], BF16, kind="ExternalOutput").ap()
        t["dbg_h2"] = nc.dram_tensor("dbg_h2", [128, 16384], BF16, kind="ExternalOutput").ap()
        t["dbg_hint"] = nc.dram_tensor("dbg_hint", [128, 12288], BF16, kind="ExternalOutput").ap()
        t["dbg_y"] = nc.dram_tensor("dbg_y", [128, 16384], BF16, kind="ExternalOutput").ap()
        t["dbg_ao2"] = nc.dram_tensor("dbg_ao2", [128, 16384], BF16, kind="ExternalOutput").ap()
    t["_dbg"] = dbg_x2

    ctx = Ctx()
    sem_names = ["pe", "act", "dve", "gp", "gld", "sld", "sst",
                 "ga", "gb", "sa", "sb", "so0", "so1", "dqv"]
    for s in sem_names:
        ctx.newsem(s)

    es = contextlib.ExitStack()
    with es:
        sb = {}
        sb["ar_x"] = es.enter_context(nc.sbuf_tensor("ar_x", [128, 16384], BF16))
        sb["ar_ct"] = es.enter_context(nc.sbuf_tensor("ar_ct", [128, 8192], BF16))
        sb["ar_q"] = es.enter_context(nc.sbuf_tensor("ar_q", [128, 12288], BF16))
        sb["ar_o"] = es.enter_context(nc.sbuf_tensor("ar_o", [128, 16384], BF16))
        sb["ar_ao"] = es.enter_context(nc.sbuf_tensor("ar_ao", [128, 16384], BF16))
        sb["ar_h"] = es.enter_context(nc.sbuf_tensor("ar_h", [128, 12288], BF16))
        sb["ar_w"] = es.enter_context(nc.sbuf_tensor("ar_w", [128, 6144], BF16))
        sb["ar_dq"] = es.enter_context(nc.sbuf_tensor("ar_dq", [128, 4096], BF16))
        sb["ar_s"] = es.enter_context(nc.sbuf_tensor("ar_s", [128, 4096], F32))
        sb["ar_st"] = es.enter_context(nc.sbuf_tensor("ar_st", [128, 1024], F32))
        sb["s_wsum"] = es.enter_context(nc.sbuf_tensor("s_wsum", [128, 16], F32))
        sb["s_ones"] = es.enter_context(nc.sbuf_tensor("s_ones", [128, 128], BF16))
        sb["s_shift"] = es.enter_context(nc.sbuf_tensor("s_shift", [128, 512], U16))
        sb["s_bias"] = es.enter_context(nc.sbuf_tensor("s_bias", [128, NH * NCB], F32))
        pb = [es.enter_context(nc.psum_tensor(f"pb{i}", [128, 512], F32))
              for i in range(8)]
        S = {s: es.enter_context(nc.semaphore(s)) for s in sem_names}

        emit_program(ctx, t, sb, pb)
        _CACHE["ctx"] = ctx

        with nc.Block() as block:
            @block.tensor
            def _(e):
                for f in ctx.q["tensor"]:
                    f(e, S)

            @block.vector
            def _(e):
                for f in ctx.q["vector"]:
                    f(e, S)

            @block.scalar
            def _(e):
                for f in ctx.q["scalar"]:
                    f(e, S)

            @block.gpsimd
            def _(e):
                for f in ctx.q["gpsimd"]:
                    f(e, S)

            @block.sync
            def _(e):
                for f in ctx.q["sync"]:
                    f(e, S)

    return nc


def emit_program(ctx, t, sb, pb):
    ar_x, ar_ct, ar_q, ar_o, ar_ao, ar_h, ar_w = (
        sb["ar_x"], sb["ar_ct"], sb["ar_q"], sb["ar_o"], sb["ar_ao"],
        sb["ar_h"], sb["ar_w"])
    ar_s, ar_st, ar_dq = sb["ar_s"], sb["ar_st"], sb["ar_dq"]
    s_wsum, s_ones, s_shift, s_bias = (
        sb["s_wsum"], sb["s_ones"], sb["s_shift"], sb["s_bias"])

    m_b = ar_s[:, 0:1024]
    r_b = ar_s[:, 1024:2048]
    m_bf = sb["ar_h"][:, 5632:6656]
    r_bf = sb["ar_h"][:, 6656:7680]
    m_bf_f = sb["ar_ct"][:, 3072:4096]
    r_bf_f = sb["ar_ct"][:, 4096:5120]
    tmp_b = ar_s[:, 2048:3072]
    aux_b = ar_s[:, 3072:4096]
    aux_bf = ar_s[:, 2048:4096].bitcast(BF16)

    dq_pk = [ar_dq[:, 0:512].bitcast(U16), ar_dq[:, 512:1024].bitcast(U16)]
    dq_sc = [ar_dq[:, 1024:1536].bitcast(FP16), ar_dq[:, 1536:2048].bitcast(FP16)]
    dq_t1 = ar_dq[:, 2048:2560].bitcast(U16)
    dq_lo = ar_dq[:, 2560:3072].bitcast(U16)
    dq_w = [ar_dq[:, 3072:3584], ar_dq[:, 3584:4096]]

    def mm(out_ap, lhsT, rhs, start, stop, inc=None):
        return ctx.op(
            "tensor",
            lambda e, S, o=out_ap, a=lhsT, b=rhs, st=start, sp=stop:
                e.matmul(o, a, b, start=st, stop=sp),
            inc=inc)

    def vop(fn, inc=None):
        return ctx.op("vector", fn, inc=inc)

    def aop(fn, inc=None):
        return ctx.op("scalar", fn, inc=inc)

    ctx.dma("sync", s_ones[:], t["c_ones"], "sa")
    ctx.dma("sync", s_shift[:], t["c_shifts"], "sa")
    ctx.dma("sync", s_bias[:], t["c_bias"], "sa")
    ctx.wait("vector", "sa", 1)
    ctx.wait("scalar", "sa", 1)
    ctx.wait("tensor", "sa", 1)
    dq_range = {"vector": None, "sync": None}
    dq_s = {qn: len(ctx.q[qn]) for qn in ("vector", "sync")}

    dq_specs = []
    for (pk_name, sc_name, c_name, n_dc, width) in [
        ("gu_pk", "gu_sc", "cache_gu", DCH, 2 * I_FF),
        ("dn_pk", "dn_sc", "cache_dn", GP_N, D),
    ]:
        for dc in range(n_dc):
            for ob in range(width // 512):
                dq_specs.append((pk_name, sc_name, c_name, dc, ob))
    n_dq = len(dq_specs)
    dq_ld = [0] * n_dq
    dq_cons = [0] * n_dq
    dq_st = [0] * n_dq

    def dq_load(i):
        pk_name, sc_name, c_name, dc, ob = dq_specs[i]
        bi = i % 2
        sem = ("sa", "sb")[bi]
        if i >= 2:
            ctx.wait("sync", "dqv", dq_cons[i - 2])
        ctx.dma("sync", dq_pk[bi][:],
                t[pk_name][dc * 128:(dc + 1) * 128, ob * 512:(ob + 1) * 512], sem)
        ctx.dma("sync", dq_sc[bi][:],
                t[sc_name][dc * 128:(dc + 1) * 128, ob * 512:(ob + 1) * 512], sem)
        dq_ld[i] = sem

    dq_load(0)
    dq_load(1)
    for i in range(n_dq):
        pk_name, sc_name, c_name, dc, ob = dq_specs[i]
        bi = i % 2
        ctx.wait("vector", dq_ld[i], 1)
        if i >= 2:
            ctx.wait("vector", dq_st[i - 2], 1)
        vop(lambda e, S, a=dq_pk[bi]:
            e.tensor_tensor(dq_t1[:], a[:], s_shift[:], op=ALU.logical_shift_right))
        vop(lambda e, S: e.tensor_scalar(dq_lo[:], dq_t1[:], 7, 9,
                                         op0=ALU.bitwise_and, op1=ALU.logical_shift_left))
        vop(lambda e, S, o=dq_pk[bi]: e.tensor_scalar(o[:], dq_t1[:], 8, 12,
                                                      op0=ALU.bitwise_and, op1=ALU.logical_shift_left))
        vop(lambda e, S, b2=dq_pk[bi]:
            e.tensor_tensor(dq_t1[:], dq_lo[:], b2[:], op=ALU.bitwise_or))
        dq_cons[i] = vop(
            lambda e, S, o=dq_w[bi], b2=dq_sc[bi]:
                e.tensor_tensor(o[:], dq_t1[:].bitcast(FP16), b2[:], op=ALU.mult),
            inc="dqv")
        ctx.wait("sync", "dqv", dq_cons[i])
        ctx.dma(
            "sync", t[c_name][dc * 128:(dc + 1) * 128, ob * 512:(ob + 1) * 512],
            dq_w[bi][:], ("so0", "so1")[bi])
        dq_st[i] = ("so0", "so1")[bi]
        if i + 2 < n_dq:
            dq_load(i + 2)
    dq_range["vector"] = (dq_s["vector"], len(ctx.q["vector"]))
    dq_range["sync"] = (dq_s["sync"], len(ctx.q["sync"]))
    b0_mark = {}
    ffn0_mark = {}

    class Rot:
        def __init__(self, banks):
            self.banks = banks
            self.i = 0

        def acquire(self):
            j = self.banks[self.i % len(self.banks)]
            self.i += 1
            ctx.bank_wait(j)
            return j

    rot = Rot([4, 5, 6, 7])

    def ln_stats(src_fn, n_ch, width, s1b, s2b, eps_const, extra_coef, negate,
                 ready_fn=None, load_fn=None):
        nts = width // 512
        for bk in s1b + s2b:
            ctx.bank_wait(bk)
        sq = [aux_bf[:, 0:width], aux_bf[:, width:2 * width]]
        c_sq = [0] * n_ch
        c_done = [0] * n_ch
        for c in range(n_ch):
            if ready_fn is not None:
                sem, cnt = ready_fn(c)
                ctx.wait("scalar", sem, cnt)
                ctx.wait("tensor", sem, cnt)
            if c >= 2:
                ctx.wait("scalar", "pe", c_done[c - 2])
            c_sq[c] = aop(lambda e, S, o=sq[c % 2], a=src_fn(c):
                          e.activation(out=o, in_=a, func=AF.Square), inc="act")
            for ts_ in range(nts):
                mm(pb[s1b[ts_]][:, 0:512], s_ones[:],
                   src_fn(c)[:, ts_ * 512:(ts_ + 1) * 512],
                   start=(c == 0), stop=(c == n_ch - 1))
            ctx.wait("tensor", "act", c_sq[c])
            for ts_ in range(nts):
                mm(pb[s2b[ts_]][:, 0:512], s_ones[:],
                   sq[c % 2][:, ts_ * 512:(ts_ + 1) * 512],
                   start=(c == 0), stop=(c == n_ch - 1),
                   inc="pe" if ts_ == nts - 1 else None)
            c_done[c] = ctx.at("pe")
            if load_fn is not None and c + 2 < n_ch:
                load_fn(c + 2, c_done[c], c_sq[c])
        done = ctx.at("pe")
        ctx.wait("vector", "pe", done)
        inv = 1.0 / D
        for ts_ in range(nts):
            sl = slice(ts_ * 512, (ts_ + 1) * 512)
            vop(lambda e, S, b2=pb[s1b[ts_]], sl=sl:
                e.tensor_scalar(m_b[:, sl], b2[:, 0:512], inv, None, op0=ALU.mult))
            vop(lambda e, S, b2=pb[s2b[ts_]], sl=sl:
                e.tensor_scalar(aux_b[:, sl], b2[:, 0:512], inv, None, op0=ALU.mult))
        w = width
        vop(lambda e, S: e.tensor_tensor(tmp_b[:, 0:w], m_b[:, 0:w], m_b[:, 0:w], op=ALU.mult))
        vop(lambda e, S: e.tensor_tensor(tmp_b[:, 0:w], aux_b[:, 0:w], tmp_b[:, 0:w], op=ALU.subtract))
        if extra_coef:
            vop(lambda e, S, k=extra_coef:
                e.scalar_tensor_tensor(tmp_b[:, 0:w], aux_b[:, 0:w], k, tmp_b[:, 0:w],
                                       op0=ALU.mult, op1=ALU.add))
        vop(lambda e, S, ec=eps_const:
            e.tensor_scalar(tmp_b[:, 0:w], tmp_b[:, 0:w], ec, None, op0=ALU.add))
        rc = vop(lambda e, S: e.reciprocal(tmp_b[:, 0:w], tmp_b[:, 0:w]), inc="dve")
        for bk in s1b + s2b:
            ctx.bank_free(bk, "dve", rc)
        ctx.wait("scalar", "dve", rc)
        sd = aop(lambda e, S: e.activation(out=r_b[:, 0:w], in_=tmp_b[:, 0:w], func=AF.Sqrt),
                 inc="act")
        ctx.wait("vector", "act", sd)
        if negate:
            vop(lambda e, S: e.tensor_scalar(r_b[:, 0:w], r_b[:, 0:w], -1.0, None, op0=ALU.mult))
        return rc

    def cast_cols(src2d, nrows, col0, ncols, dst, slot=0, split=512):
        sem = ("ga", "gb")[slot % 2]
        for (r, c) in _split_rows(nrows, split):
            src = src2d[r:r + c, col0:col0 + ncols]
            src = src.rearrange("(c p) j -> p c j", p=128)
            d0 = (r // 128) * ncols
            ctx.dma("gpsimd", dst[:, d0:d0 + (c // 128) * ncols], src, sem)
        return sem

    x2_wr_last = 0
    ffn_dve_last = 0

    prev_xst = [0] * DCH
    for b in range(BC):
        if b == 0:
            b0_mark.update({qn: len(ctx.q[qn]) for qn in ("vector", "sync")})
        for Hf in range(2):
            q0 = Hf * HALF
            KR = (Hf + 1) * HALF
            KT = KR // 128

            ctx.wait("gpsimd", "dve", ffn_dve_last)
            x_tgt = []
            for dc in range(DCH):
                if Hf == 1:
                    ctx.wait("gpsimd", "dve", prev_xst[dc])
                else:
                    ctx.wait("gpsimd", "sst", 1)
                src = t["x"][b, dc * 128:(dc + 1) * 128, q0:q0 + HALF]
                ctx.dma("gpsimd", ar_x[:, dc * 1024:(dc + 1) * 1024], src,
                        ("ga", "gb")[dc % 2])
                x_tgt.append(("ga", "gb")[dc % 2])

            ln_stats(lambda c: ar_x[:, c * 1024:(c + 1) * 1024], DCH, HALF,
                     [4, 5], [6, 7], EPS_LN, 0.0, True,
                     ready_fn=lambda c: (x_tgt[c], 1))

            wqd_buf = [ar_w[:, 0:2048], ar_w[:, 2048:4096]]
            wqd_free = [0, 0]

            def proj(n_out_ch, wsrc, wsum_off, out_fn):
                wtgt = [0] * n_out_ch
                ctx.wait("gpsimd", "dve", wqd_free[0])
                ctx.wait("gpsimd", "sst", 1)
                wtgt[0] = cast_cols(t[wsrc], D, 0, 128, wqd_buf[0], slot=0)
                last_ev = 0
                for oc in range(n_out_ch):
                    if oc + 1 < n_out_ch:
                        ctx.wait("gpsimd", "dve", wqd_free[(oc + 1) % 2])
                        wtgt[oc + 1] = cast_cols(t[wsrc], D, (oc + 1) * 128, 128,
                                                 wqd_buf[(oc + 1) % 2], slot=oc + 1)
                    wb = wqd_buf[oc % 2]
                    ctx.wait("tensor", wtgt[oc], 1)
                    bk = rot.acquire()
                    for dc in range(DCH):
                        if oc == 0:
                            ctx.wait("tensor", x_tgt[dc], 1)
                        mm(pb[bk][:, 0:1], wb[:, dc * 128:(dc + 1) * 128], s_ones[:, 0:1],
                           start=(dc == 0), stop=(dc == DCH - 1),
                           inc="pe" if dc == DCH - 1 else None)
                    ctx.wait("vector", "pe", ctx.at("pe"))
                    wev = vop(lambda e, S, b2=pb[bk], oc=oc, off=wsum_off:
                              e.tensor_copy(s_wsum[:, off + oc:off + oc + 1], b2[:, 0:1]),
                              inc="dve")
                    ctx.bank_free(bk, "dve", wev)
                    for tsub in range(2):
                        bk = rot.acquire()
                        for dc in range(DCH):
                            mm(pb[bk][:, 0:512], wb[:, dc * 128:(dc + 1) * 128],
                               ar_x[:, dc * 1024 + tsub * 512:dc * 1024 + tsub * 512 + 512],
                               start=(dc == 0), stop=(dc == DCH - 1),
                               inc="pe" if dc == DCH - 1 else None)
                        ctx.wait("vector", "pe", ctx.at("pe"))
                        msl = slice(tsub * 512, tsub * 512 + 512)
                        vop(lambda e, S, b2=pb[bk], msl=msl, oc=oc, off=wsum_off:
                            e.scalar_tensor_tensor(tmp_b[:, 0:512], m_b[:, msl],
                                                   s_wsum[:, off + oc:off + oc + 1],
                                                   b2[:, 0:512],
                                                   op0=ALU.mult, op1=ALU.subtract))
                        ev = vop(lambda e, S, o=out_fn(oc, tsub), msl=msl:
                                 e.tensor_tensor(o, tmp_b[:, 0:512], r_b[:, msl],
                                                 op=ALU.mult),
                                 inc="dve")
                        ctx.bank_free(bk, "dve", ev)
                        last_ev = ev
                    wqd_free[oc % 2] = last_ev
                return last_ev

            qlow = ar_q
            proj(RQC, "wqdT", 0,
                 lambda oc, tsub: qlow[:, oc * 1024 + tsub * 512:oc * 1024 + tsub * 512 + 512])
            lastc = proj(RKC, "wkvT", 12,
                         lambda oc, tsub: ar_ct[:, oc * 2048 + q0 + tsub * 512:
                                                oc * 2048 + q0 + tsub * 512 + 512])

            vtokS = [ar_h[:, 0:4096], ar_ao[:, 0:4096]]
            kTS = [ar_h[:, 4096:6144], ar_ao[:, 4096:6144]]
            qTS = [ar_h[:, 6144:7168], ar_ao[:, 6144:7168]]
            attn = [ar_h[:, 7168:7680], ar_h[:, 7680:8192]]
            wqu_buf = [ar_h[:, 8192:9728], ar_h[:, 9728:11264]]
            kup_buf = [ar_w[:, 4096:4608], ar_w[:, 4608:5120]]
            vup_buf = [ar_w[:, 0:1024], ar_w[:, 1024:2048]]
            kup_tgt = [0] * NH
            wqu_tgt = [0] * NH
            vup_tgt = [0] * (NH // 2)
            kup_free = [0, 0]
            wqu_free = [0, 0]
            vup_free = [0, 0]
            chain_done = [0] * NH

            ctx.wait("gpsimd", "dve", lastc)
            ctx.wait("gpsimd", "pe", ctx.at("pe"))
            ctx.wait("gpsimd", "sst", 1)
            ctx.wait("tensor", "dve", lastc)

            def prefetch_w(h):
                pr = h // 2
                ctx.wait("gpsimd", "dve", kup_free[h % 2])
                kup_tgt[h] = cast_cols(t["kup"], RKV, h * 128, 128,
                                       kup_buf[h % 2], slot=h)
                ctx.wait("gpsimd", "dve", wqu_free[h % 2])
                wqu_tgt[h] = cast_cols(t["wquT"], RQ, h * 128, 128,
                                       wqu_buf[h % 2], slot=h)
                if h % 2 == 0:
                    ctx.wait("gpsimd", "dve", vup_free[pr % 2])
                    vup_tgt[pr] = cast_cols(t["vup"], RKV, pr * 256, 256,
                                            vup_buf[pr % 2], slot=pr)

            def chain_gen(h):
                pr = h // 2
                klo_h = kt_low(h, q0)
                if h % 2 == 0:
                    klo_v = min(klo_h, kt_low(h + 1, q0)) if h + 1 < NH else klo_h
                    ctx.wait("tensor", vup_tgt[pr], 1)
                    vt = vtokS[pr % 2]
                    for tt in range(klo_v, KT):
                        bk = rot.acquire()
                        for rc in range(RKC):
                            mm(pb[bk][:, 0:256],
                               ar_ct[:, rc * 2048 + tt * 128:rc * 2048 + tt * 128 + 128],
                               vup_buf[pr % 2][:, rc * 256:(rc + 1) * 256],
                               start=(rc == 0), stop=(rc == RKC - 1),
                               inc="pe" if rc == RKC - 1 else None)
                        ctx.wait("vector", "pe", ctx.at("pe"))
                        vev = vop(lambda e, S, b2=pb[bk], tt=tt, vt=vt:
                                  e.tensor_copy(vt[:, tt * 256:(tt + 1) * 256], b2[:, 0:256]),
                                  inc="dve")
                        ctx.bank_free(bk, "dve", vev)
                        yield
                    vup_free[pr % 2] = vev
                ctx.wait("tensor", kup_tgt[h], 1)
                kt_ = kTS[h % 2]
                for ks in range(klo_h // 4, KR // 512):
                    bk = rot.acquire()
                    for rc in range(RKC):
                        mm(pb[bk][:, 0:512], kup_buf[h % 2][:, rc * 128:(rc + 1) * 128],
                           ar_ct[:, rc * 2048 + ks * 512:rc * 2048 + ks * 512 + 512],
                           start=(rc == 0), stop=(rc == RKC - 1),
                           inc="pe" if rc == RKC - 1 else None)
                    ctx.wait("vector", "pe", ctx.at("pe"))
                    kev = vop(lambda e, S, b2=pb[bk], ks=ks, kt_=kt_:
                              e.tensor_copy(kt_[:, ks * 512:(ks + 1) * 512], b2[:, 0:512]),
                              inc="dve")
                    ctx.bank_free(bk, "dve", kev)
                    yield
                kup_free[h % 2] = kev
                ctx.wait("tensor", wqu_tgt[h], 1)
                qt_ = qTS[h % 2]
                for tsub in range(2):
                    bk = rot.acquire()
                    for r1c in range(RQC):
                        mm(pb[bk][:, 0:512], wqu_buf[h % 2][:, r1c * 128:(r1c + 1) * 128],
                           qlow[:, r1c * 1024 + tsub * 512:r1c * 1024 + tsub * 512 + 512],
                           start=(r1c == 0), stop=(r1c == RQC - 1),
                           inc="pe" if r1c == RQC - 1 else None)
                    ctx.wait("vector", "pe", ctx.at("pe"))
                    qev = vop(lambda e, S, b2=pb[bk], tsub=tsub, qt_=qt_:
                              e.tensor_copy(qt_[:, tsub * 512:(tsub + 1) * 512], b2[:, 0:512]),
                              inc="dve")
                    ctx.bank_free(bk, "dve", qev)
                    yield
                wqu_free[h % 2] = qev
                chain_done[h] = qev

            prefetch_w(0)
            gen = chain_gen(0)
            for _ in gen:
                pass

            for h in range(NH):
                pr = h // 2
                nxt = None
                if h + 1 < NH:
                    prefetch_w(h + 1)
                    nxt = chain_gen(h + 1)
                vtok = vtokS[pr % 2]
                kT = kTS[h % 2]
                qT = qTS[h % 2]
                ctx.wait("tensor", "dve", chain_done[h])

                qw = QW[h]
                nsub = 512 // qw
                for qc in range(2):
                    q_off = q0 + qc * 512
                    kt_hi = q_off // 128 + 4
                    kt_lo = kt_low(h, q_off)
                    c_sc = [0] * kt_hi
                    c_ex = [0] * kt_hi
                    c_dn = [0] * kt_hi
                    c_mk = [0] * kt_hi
                    ctx.bank_wait(2)
                    ctx.bank_wait(3)
                    for kt in range(kt_lo, kt_hi + 1):
                        if kt < kt_hi:
                            ctx.bank_wait(kt % 2)
                            c_sc[kt] = mm(pb[kt % 2][:, 0:512], kT[:, kt * 128:(kt + 1) * 128],
                                          qT[:, qc * 512:qc * 512 + 512], True, True, inc="pe")
                            ctx.wait("scalar", "pe", c_sc[kt])
                            for sub in range(nsub):
                                qso = q_off + sub * qw
                                cidx = (kt * 128 - qso + 2048 - qw) // 128
                                bias_ap = s_bias[:, h * NCB + cidx:h * NCB + cidx + 1]
                                c_ex[kt] = aop(
                                    lambda e, S, kt=kt, sub=sub, qw=qw, ba=bias_ap:
                                        e.activation(out=attn[kt % 2][:, sub * qw:(sub + 1) * qw],
                                                     in_=pb[kt % 2][:, sub * qw:(sub + 1) * qw],
                                                     func=AF.Exp, bias=ba, scale=SCALE),
                                    inc="act")
                            ctx.bank_free(kt % 2, "act", c_ex[kt])
                            if kt * 128 + 128 > q_off:
                                ctx.wait("gpsimd", "act", c_ex[kt])
                                c_mk[kt] = ctx.op(
                                    "gpsimd",
                                    lambda e, S, kt=kt, q_off=q_off:
                                        e.affine_select(out=attn[kt % 2][:],
                                                        in_=attn[kt % 2][:],
                                                        pattern=[[1, 512]],
                                                        compare_op=ALU.is_ge, fill=0.0,
                                                        base=q_off - kt * 128,
                                                        channel_multiplier=-1),
                                    inc="gp")
                        if nxt is not None:
                            next(nxt, None)
                        if kt >= kt_lo + 1:
                            j = kt - 1
                            if c_mk[j]:
                                ctx.wait("tensor", "gp", c_mk[j])
                            else:
                                ctx.wait("tensor", "act", c_ex[j])
                            mm(pb[2][:, 0:512],
                               vtok[:, j * 256 + (h % 2) * 128:j * 256 + (h % 2) * 128 + 128],
                               attn[j % 2][:], start=(j == kt_lo), stop=(j == kt_hi - 1))
                            c_dn[j] = mm(pb[3][:, 0:512], s_ones[:], attn[j % 2][:],
                                         start=(j == kt_lo), stop=(j == kt_hi - 1), inc="pe")
                    ctx.wait("vector", "pe", c_dn[kt_hi - 1])
                    vop(lambda e, S: e.reciprocal(tmp_b[:, 0:512], pb[3][:, 0:512]))
                    ocol = h * 1024 + qc * 512
                    oev = vop(lambda e, S, ocol=ocol:
                              e.tensor_tensor(ar_o[:, ocol:ocol + 512], pb[2][:, 0:512],
                                              tmp_b[:, 0:512], op=ALU.mult),
                              inc="dve")
                    ctx.bank_free(2, "dve", oev)
                    ctx.bank_free(3, "dve", oev)
                if nxt is not None:
                    for _ in nxt:
                        pass
            all_o = oev
            if t.get("_dbg") and b == 0 and Hf == 0:
                ctx.wait("sync", "dve", all_o)
                dmp3 = ctx.dma("sync", t["dbg_o"], ar_o[:], "sst")
                dmp3 = ctx.dma("sync", t["dbg_ax"], ar_x[:], "sst")
                ctx.wait("tensor", "sst", dmp3)

            wo_buf = [ar_w[:, 2048:4096], ar_w[:, 4096:6144]]
            wo_tgt = [0] * DCH
            wo_free = [0, max(kup_free)]
            ao_ev = [0] * DCH
            ctx.wait("gpsimd", "dve", max(kup_free))
            wo_tgt[0] = cast_cols(t["woT"], D, 0, 128, wo_buf[0], slot=0)
            for fc in range(DCH):
                if fc + 1 < DCH:
                    ctx.wait("gpsimd", "dve", wo_free[(fc + 1) % 2])
                    wo_tgt[fc + 1] = cast_cols(t["woT"], D, (fc + 1) * 128, 128,
                                               wo_buf[(fc + 1) % 2], slot=fc + 1)
                ctx.wait("tensor", wo_tgt[fc], 1)
                if fc == 0:
                    ctx.wait("tensor", "dve", all_o)
                for tsub in range(2):
                    bk = rot.acquire()
                    for ec in range(DCH):
                        mm(pb[bk][:, 0:512], wo_buf[fc % 2][:, ec * 128:(ec + 1) * 128],
                           ar_o[:, ec * 1024 + tsub * 512:ec * 1024 + tsub * 512 + 512],
                           start=(ec == 0), stop=(ec == DCH - 1),
                           inc="pe" if ec == DCH - 1 else None)
                    ctx.wait("vector", "pe", ctx.at("pe"))
                    aev = vop(lambda e, S, b2=pb[bk], fc=fc, tsub=tsub:
                              e.tensor_copy(
                                  ar_ao[:, fc * 1024 + tsub * 512:fc * 1024 + tsub * 512 + 512],
                                  b2[:, 0:512]),
                              inc="dve")
                    ctx.bank_free(bk, "dve", aev)
                ao_ev[fc] = aev
                wo_free[fc % 2] = aev

            ln_stats(lambda c: ar_ao[:, c * 1024:(c + 1) * 1024], DCH, HALF,
                     [6, 7], [2, 3], EPS_LN, 0.0, False,
                     ready_fn=lambda c: ("dve", ao_ev[c]))
            if t.get("_dbg") and b == 0 and Hf == 0:
                ctx.wait("sync", "dve", ao_ev[DCH - 1])
                dmp4 = ctx.dma("sync", t["dbg_ao2"], ar_ao[:], "sst")
                dmp4 = ctx.dma("sync", t["dbg_ao3"], ar_ao[:], "sst")
                ctx.wait("vector", "sst", dmp4)
            for dc in range(DCH):
                vop(lambda e, S, dc=dc:
                    e.tensor_tensor(tmp_b[:, 0:1024], ar_ao[:, dc * 1024:(dc + 1) * 1024],
                                    m_b[:, 0:1024], op=ALU.subtract))
                vop(lambda e, S:
                    e.tensor_tensor(tmp_b[:, 0:1024], tmp_b[:, 0:1024], r_b[:, 0:1024],
                                    op=ALU.mult))
                xu = vop(lambda e, S, dc=dc:
                         e.tensor_tensor(ar_x[:, dc * 1024:(dc + 1) * 1024],
                                         ar_x[:, dc * 1024:(dc + 1) * 1024],
                                         tmp_b[:, 0:1024], op=ALU.add),
                         inc="dve")
                ctx.wait("gpsimd", "dve", xu)
                x2_wr_last = ctx.dma(
                    "gpsimd", t["x2c"][b, dc * 128:(dc + 1) * 128, q0:q0 + HALF],
                    ar_x[:, dc * 1024:(dc + 1) * 1024], "sst")

        if b == 0:
            ffn0_mark.update({qn: len(ctx.q[qn]) for qn in ("vector", "sync")})
        for tc in (1, 0):
            tr0 = tc * TW
            x2r = [ar_ct[:, 0:1024], ar_ct[:, 1024:2048]]
            silu_sb = [ar_ct[:, 2048:2560], ar_ct[:, 2560:3072]]

            pe_epoch = ctx.at("pe")
            ctx.wait("sync", "pe", pe_epoch)
            ctx.wait("sync", "sst", 1)

            x2_tgt = [0] * DCH

            def x2_load(dc):
                ctx.dma(
                    "sync", x2r[dc % 2][:],
                    t["x2c"][b, dc * 128:(dc + 1) * 128, tr0:tr0 + TW],
                    ("sa", "sb")[dc % 2])
                x2_tgt[dc] = ("sa", "sb")[dc % 2]

            if tc == 1:
                dve_mark = ctx.at("dve")
                ln_stats(lambda c: ar_x[:, c * 1024:(c + 1) * 1024], DCH, TW,
                         [4, 5], [6, 7], EPS_LN, 0.0, False,
                         ready_fn=lambda c: ("dve", dve_mark))
            else:
                x2_load(0)
                x2_load(1)

                def ln2_loadfn(c, gate_pe, gate_sq):
                    ctx.wait("sync", "pe", gate_pe)
                    ctx.wait("sync", "act", gate_sq)
                    x2_load(c)

                ln_stats(lambda c: x2r[c % 2], DCH, TW, [4, 5], [6, 7],
                         EPS_LN, 0.0, False,
                         ready_fn=lambda c: (x2_tgt[c], 1),
                         load_fn=ln2_loadfn)

            h2 = ar_x
            h2_ev = [0] * DCH
            tmp_bf = tmp_b.bitcast(BF16)[:, 0:1024]
            if tc == 1:
                vop(lambda e, S: e.tensor_copy(m_bf_f[:], m_b[:, 0:1024]))
                vop(lambda e, S: e.tensor_copy(r_bf_f[:], r_b[:, 0:1024]))
                ctx.wait("vector", "sst", 1)
                for dc in range(DCH):
                    vop(lambda e, S, dc=dc:
                        e.tensor_tensor(tmp_bf,
                                        ar_x[:, dc * 1024:(dc + 1) * 1024],
                                        m_bf_f[:], op=ALU.subtract))
                    h2_ev[dc] = vop(lambda e, S, dc=dc:
                                    e.tensor_tensor(h2[:, dc * 1024:(dc + 1) * 1024],
                                                    tmp_bf, r_bf_f[:],
                                                    op=ALU.mult),
                                    inc="dve")
            else:
                ctx.wait("sync", "pe", ctx.at("pe"))
                ctx.wait("sync", "act", ctx.at("act"))
                x2_load(0)
                x2_load(1)
                vop(lambda e, S: e.tensor_copy(m_bf_f[:], m_b[:, 0:1024]))
                vop(lambda e, S: e.tensor_copy(r_bf_f[:], r_b[:, 0:1024]))
                for dc in range(DCH):
                    ctx.wait("vector", x2_tgt[dc], 1)
                    vop(lambda e, S, dc=dc:
                        e.tensor_tensor(tmp_bf, x2r[dc % 2][:], m_bf_f[:],
                                        op=ALU.subtract))
                    h2_ev[dc] = vop(lambda e, S, dc=dc:
                                    e.tensor_tensor(h2[:, dc * 1024:(dc + 1) * 1024],
                                                    tmp_bf, r_bf_f[:],
                                                    op=ALU.mult),
                                    inc="dve")
                    if dc + 2 < DCH:
                        ctx.wait("sync", "dve", h2_ev[dc])
                        x2_load(dc + 2)

            if t.get("_dbg") and b == 0 and tc == 0:
                ctx.wait("sync", "dve", h2_ev[DCH - 1])
                dmph = ctx.dma("sync", t["dbg_h2"], h2[:], "sst")
                ctx.wait("tensor", "sst", dmph)
            wgu_buf = [ar_h[:, 0:4096], ar_h[:, 4096:8192]]
            wgu_tgt = [0] * GP_N
            wgu_free = [0, 0]

            def wgu_load(gp, bi):
                tgt = 0
                for j in range(4):
                    src = t["cache_gu"][j * 512:(j + 1) * 512, gp * 256:(gp + 1) * 256]
                    src = src.rearrange("(c p) o -> p c o", p=128)
                    ctx.dma("sync", wgu_buf[bi][:, j * 1024:(j + 1) * 1024], src,
                            ("sa", "sb")[bi])
                wgu_tgt[gp] = ("sa", "sb")[bi]

            if b == 0:
                ctx.wait("sync", "so0", 1)
                ctx.wait("sync", "so1", 1)
            wgu_load(0, 0)
            wgu_load(1, 1)
            hint_ev = [0] * GP_N
            ctx.wait("tensor", "dve", h2_ev[DCH - 1])
            for gp in range(GP_N):
                gbank = [0 + 4 * (gp % 2), 1 + 4 * (gp % 2)]
                ubank = [2 + 4 * (gp % 2), 3 + 4 * (gp % 2)]
                wb = wgu_buf[gp % 2]
                ctx.wait("tensor", wgu_tgt[gp], 1)
                c_g = [0, 0]
                for tsub in range(2):
                    ctx.bank_wait(gbank[tsub])
                    for dc in range(DCH):
                        mm(pb[gbank[tsub]][:, 0:512], wb[:, dc * 256:dc * 256 + 128],
                           h2[:, dc * 1024 + tsub * 512:dc * 1024 + tsub * 512 + 512],
                           start=(dc == 0), stop=(dc == DCH - 1),
                           inc="pe" if dc == DCH - 1 else None)
                    c_g[tsub] = ctx.at("pe")
                    ctx.bank_wait(ubank[tsub])
                    for dc in range(DCH):
                        mm(pb[ubank[tsub]][:, 0:512], wb[:, dc * 256 + 128:dc * 256 + 256],
                           h2[:, dc * 1024 + tsub * 512:dc * 1024 + tsub * 512 + 512],
                           start=(dc == 0), stop=(dc == DCH - 1),
                           inc="pe" if dc == DCH - 1 else None)
                cu = ctx.at("pe")
                wgu_free[gp % 2] = cu
                if gp + 2 < GP_N:
                    ctx.wait("sync", "pe", cu)
                    wgu_load(gp + 2, gp % 2)
                if gp >= 1:
                    ctx.wait("scalar", "dve", hint_ev[gp - 1])
                for tsub in range(2):
                    ctx.wait("scalar", "pe", c_g[tsub])
                    sev = aop(lambda e, S, gb=pb[gbank[tsub]], o=silu_sb[tsub]:
                              e.activation(out=o[:], in_=gb[:, 0:512], func=AF.Silu,
                                           scale=S14),
                              inc="act")
                    ctx.bank_free(gbank[tsub], "act", sev)
                    ctx.wait("vector", "act", sev)
                    ctx.wait("vector", "pe", cu)
                    col = gp * 1024 + tsub * 512
                    if col < 12288:
                        hdst = ar_q[:, col:col + 512]
                    elif col < 12288 + 16384:
                        hdst = ar_o[:, col - 12288:col - 12288 + 512]
                    else:
                        hdst = ar_ao[:, col - 28672:col - 28672 + 512]
                    hint_ev[gp] = vop(
                        lambda e, S, ub=pb[ubank[tsub]], o=hdst, s2=silu_sb[tsub]:
                            e.scalar_tensor_tensor(o, ub[:, 0:512], S14, s2[:],
                                                   op0=ALU.mult, op1=ALU.mult),
                        inc="dve")
                    ctx.bank_free(ubank[tsub], "dve", hint_ev[gp])

            def hint_slice(ic, tsub):
                col = ic * 1024 + tsub * 512
                if col < 12288:
                    return ar_q[:, col:col + 512]
                elif col < 12288 + 16384:
                    return ar_o[:, col - 12288:col - 12288 + 512]
                return ar_ao[:, col - 28672:col - 28672 + 512]

            if t.get("_dbg") and b == 0 and tc == 0:
                ctx.wait("sync", "dve", hint_ev[GP_N - 1])
                dmpi = ctx.dma("sync", t["dbg_hint"], ar_q[:], "sst")
                ctx.wait("tensor", "sst", dmpi)
            wdn_buf = [ar_h[:, 0:5632], ar_h[:, 5632:11264]]
            wdn_tgt = [0] * DCH

            def wdn_load(hc, bi):
                tgt = 0
                for j in range(11):
                    src = t["cache_dn"][j * 512:(j + 1) * 512, hc * 128:(hc + 1) * 128]
                    src = src.rearrange("(c p) o -> p c o", p=128)
                    ctx.dma("sync", wdn_buf[bi][:, j * 512:(j + 1) * 512], src,
                            ("sa", "sb")[bi])
                wdn_tgt[hc] = ("sa", "sb")[bi]

            ctx.wait("sync", "pe", max(wgu_free))
            wdn_load(0, 0)
            wdn_load(1, 1)
            y_sb = ar_x
            y_ev = [0] * DCH
            wdn_free = [0, 0]
            ctx.wait("tensor", "dve", hint_ev[GP_N - 1])
            for hc in range(DCH):
                ctx.wait("tensor", wdn_tgt[hc], 1)
                for tsub in range(2):
                    bk = rot.acquire()
                    for ic in range(GP_N):
                        mm(pb[bk][:, 0:512], wdn_buf[hc % 2][:, ic * 128:(ic + 1) * 128],
                           hint_slice(ic, tsub),
                           start=(ic == 0), stop=(ic == GP_N - 1),
                           inc="pe" if ic == GP_N - 1 else None)
                    ctx.wait("vector", "pe", ctx.at("pe"))
                    yev = vop(lambda e, S, b2=pb[bk], hc=hc, tsub=tsub:
                              e.tensor_copy(
                                  y_sb[:, hc * 1024 + tsub * 512:hc * 1024 + tsub * 512 + 512],
                                  b2[:, 0:512]),
                              inc="dve")
                    ctx.bank_free(bk, "dve", yev)
                y_ev[hc] = yev
                wdn_free[hc % 2] = ctx.at("pe")
                if hc + 2 < DCH:
                    ctx.wait("sync", "pe", wdn_free[hc % 2])
                    wdn_load(hc + 2, hc % 2)

            if t.get("_dbg") and b == 0 and tc == 0:
                ctx.wait("sync", "dve", y_ev[DCH - 1])
                dmpy = ctx.dma("sync", t["dbg_y"], y_sb[:], "sst")
                ctx.wait("tensor", "sst", dmpy)
            ln_stats(lambda c: y_sb[:, c * 1024:(c + 1) * 1024], DCH, TW,
                     [4, 5], [6, 7],
                     EPS_LN * EPS_FF * (2.0 ** -28), EPS_LN, False,
                     ready_fn=lambda c: ("dve", y_ev[c]))

            ctx.wait("sync", "pe", ctx.at("pe"))
            ctx.wait("sync", "act", ctx.at("act"))
            x2_load(0)
            x2_load(1)
            stg = [ar_st[:, 0:512], ar_st[:, 512:1024]]
            st_free = [0, 0]
            vop(lambda e, S: e.tensor_copy(m_bf_f[:], m_b[:, 0:1024]))
            vop(lambda e, S: e.tensor_copy(r_bf_f[:], r_b[:, 0:1024]))
            for dc in range(DCH):
                ctx.wait("vector", x2_tgt[dc], 1)
                vop(lambda e, S, dc=dc:
                    e.tensor_tensor(tmp_bf, y_sb[:, dc * 1024:(dc + 1) * 1024],
                                    m_bf_f[:], op=ALU.subtract))
                vop(lambda e, S:
                    e.tensor_tensor(tmp_bf, tmp_bf, r_bf_f[:], op=ALU.mult))
                lastoev = 0
                for tsub in range(2):
                    bi = tsub
                    ctx.wait("vector", ("so0", "so1")[bi], 1)
                    oev2 = vop(lambda e, S, dc=dc, bi=bi, tsub=tsub:
                               e.tensor_tensor(stg[bi][:],
                                               x2r[dc % 2][:, tsub * 512:tsub * 512 + 512],
                                               tmp_bf[:, tsub * 512:tsub * 512 + 512],
                                               op=ALU.add),
                               inc="dve")
                    ctx.wait("sync", "dve", oev2)
                    st_free[bi] = ctx.dma(
                        "sync",
                        t["out"][b, dc * 128:(dc + 1) * 128,
                                 tr0 + tsub * 512:tr0 + tsub * 512 + 512],
                        stg[bi][:], ("so0", "so1")[bi])
                    lastoev = oev2
                ffn_dve_last = lastoev
                if dc + 2 < DCH:
                    ctx.wait("sync", "dve", lastoev)
                    x2_load(dc + 2)

    ctx.wait("sync", "sst", ctx.at("sst"))
    ctx.wait("sync", "so0", 1)
    ctx.wait("sync", "so1", 1)

    for qn in ("vector",):
        s, e = dq_range[qn]
        for attr in ("q", "meta"):
            q = getattr(ctx, attr)[qn]
            dq = q[s:e]
            rest = q[:s] + q[e:]
            a = b0_mark[qn] - (e - s)
            bnd = ffn0_mark[qn] - (e - s)
            mid = rest[a:bnd]
            n, m = len(mid), len(dq)
            out = []
            j = 0
            for i, op in enumerate(mid):
                while j < m and j * n <= i * m:
                    out.append(dq[j])
                    j += 1
                out.append(op)
            out.extend(dq[j:])
            getattr(ctx, attr)[qn] = rest[:a] + out + rest[bnd:]



def _pair_perm():
    perm = np.empty(2 * I_FF, dtype=np.int64)
    for g in range(GP_N):
        perm[g * 256:g * 256 + 128] = np.arange(g * 128, (g + 1) * 128)
        perm[g * 256 + 128:(g + 1) * 256] = np.arange(I_FF + g * 128, I_FF + (g + 1) * 128)
    return perm


def _host_prep(inputs):
    f32 = np.float32
    x = np.ascontiguousarray(
        np.asarray(inputs["L0_attn_hidden_states"], dtype=f32).transpose(0, 2, 1))
    wqdT = np.ascontiguousarray(np.asarray(inputs["L0_attn_Wq_down"], f32).T)
    wquT = np.ascontiguousarray(np.asarray(inputs["L0_attn_Wq_up"], f32).T)
    wkvT = np.ascontiguousarray(np.asarray(inputs["L0_attn_Wkv_down"], f32).T)
    kup = np.ascontiguousarray(
        np.asarray(inputs["L0_attn_k_up"], f32).transpose(1, 0, 2).reshape(RKV, NH * HD))
    vup = np.ascontiguousarray(
        np.asarray(inputs["L0_attn_v_up"], f32).transpose(1, 0, 2).reshape(RKV, NH * HD))
    woT = np.ascontiguousarray(np.asarray(inputs["L0_attn_Wo"], f32).T)

    perm = _pair_perm()
    gpk = np.asarray(inputs["L0_ffn_gate_up_packed"], np.int32).view(np.uint16)
    gpk = np.ascontiguousarray(np.repeat(gpk[perm].T, 4, axis=0))
    gsc = np.asarray(inputs["L0_ffn_gate_up_scales"], np.float16)
    gsc = np.ascontiguousarray(np.repeat(gsc[:, perm], 128, axis=0))
    dpk = np.asarray(inputs["L0_ffn_down_packed"], np.int32).view(np.uint16)
    dpk = np.ascontiguousarray(np.repeat(dpk.T, 4, axis=0))
    dsc = np.asarray(inputs["L0_ffn_down_scales"], np.float16)
    dsc = np.ascontiguousarray(np.repeat(dsc, 128, axis=0))

    ones = np.ones((128, 128), dtype=BF)
    shifts = np.broadcast_to(
        (4 * (np.arange(128) % 4)).astype(np.uint16).reshape(128, 1), (128, 512)).copy()

    bias = np.zeros((128, NH * NCB), dtype=f32)
    p = np.arange(128, dtype=np.float64)
    for h in range(NH):
        qw = QW[h]
        slope = SLOPES[h]
        center = (qw - 1) / 2.0
        for c in range(NCB):
            delta = c * 128 - (2048 - qw)
            if delta > qw - 128:
                bias[:, h * NCB + c] = -30000.0
            else:
                bias[:, h * NCB + c] = (slope * (delta + p - center)).astype(f32)

    return x, dict(wqdT=wqdT, wquT=wquT, wkvT=wkvT, kup=kup, vup=vup, woT=woT,
                   gu_pk=gpk, gu_sc=gsc, dn_pk=dpk, dn_sc=dsc,
                   c_ones=ones, c_shifts=shifts, c_bias=bias)


def kernel(**inputs):
    x, shared = _host_prep(inputs)
    if "nc" not in _CACHE:
        _CACHE["nc"] = build_nc()
    nc = _CACHE["nc"]
    in_maps = []
    for c in range(NCORES):
        m = dict(shared)
        m["x"] = np.ascontiguousarray(x[c * BC:(c + 1) * BC])
        in_maps.append(m)
    res = run_bass_kernel_spmd(nc, in_maps, core_ids=list(range(NCORES)))
    out = np.concatenate([res.results[c]["out"] for c in range(NCORES)], axis=0)
    return np.ascontiguousarray(out.transpose(0, 2, 1))


def simulate_deadlock(ctx):
    pos = {e: 0 for e in ENGINES}
    cnt = {}
    qs = ctx.meta
    progress = True
    while progress:
        progress = False
        for e in ENGINES:
            while pos[e] < len(qs[e]):
                m = qs[e][pos[e]]
                if m[0] == "w":
                    if cnt.get(m[1], 0) >= m[2]:
                        pos[e] += 1
                        progress = True
                    else:
                        break
                else:
                    if m[1] is not None:
                        cnt[m[1]] = cnt.get(m[1], 0) + m[2]
                    pos[e] += 1
                    progress = True
    stuck = {e: pos[e] for e in ENGINES if pos[e] < len(qs[e])}
    if not stuck:
        print("NO DEADLOCK: all queues drain")
        return
    print("DEADLOCK! queue positions:")
    for e, p in stuck.items():
        m = qs[e][p]
        print(f"  {e}: pos {p}/{len(qs[e])} waiting {m} (cnt={cnt.get(m[1],0)})")
